# version 10
# baseline (speedup 1.0000x reference)
import math
import numpy as np
from contextlib import ExitStack

import concourse.bass as bass
import concourse.mybir as mybir
from concourse.bass_utils import run_bass_kernel_spmd

F32 = mybir.dt.float32
BF16 = mybir.dt.bfloat16
AF = mybir.ActivationFunctionType
ALU = mybir.AluOpType
AX = mybir.AxisListType

B, S, D = 4, 4096, 1024
H, DH, DR = 16, 64, 32
DC = DH + DR
SCALE = 1.0 / math.sqrt(DC)
E = 8
SQ = S // 2
NKV = S // 512
NQ = SQ // 512
NKC = S // 128
NTC = SQ // 128

ENGS = ("tensor", "scalar", "vector", "sync", "gpsimd")


class Sched:

    def __init__(self):
        self.ops = []
        self.cnt = {}

    def cur(self, sem):
        return self.cnt.get(id(sem), 0)

    def add(self, eng, fn, waits=(), inc=None, inc_k=1):
        ws = [(s, v) for (s, v) in waits if v > 0]
        self.ops.append((eng, ws, (inc, inc_k) if inc is not None else None, fn))
        if inc is not None:
            self.cnt[id(inc)] = self.cur(inc) + inc_k

    def dma(self, eng, fn, waits=(), inc=None):
        self.add(eng, fn, waits, inc, 16)

    def emit(self, block):
        def mk(eng_name):
            ops = [(w, i, f) for (e, w, i, f) in self.ops if e == eng_name]

            def run(eng):
                for (waits, inc, fn) in ops:
                    for (sem, val) in waits:
                        eng.wait_ge(sem, val)
                    instr = fn(eng)
                    if inc is not None:
                        assert instr is not None, "op with inc returned None"
                        instr.then_inc(inc[0], inc[1])
            return run

        for name in ENGS:
            if any(e == name for (e, _, _, _) in self.ops):
                getattr(block, name)(mk(name))


def build(dev=0, trunc=None):
    nc = bass.Bass("TRN2", target_bir_lowering=False)

    x_kv = nc.dram_tensor("x_kv", [S, D], F32, kind="ExternalInput")
    x_q = nc.dram_tensor("x_q", [SQ, D], F32, kind="ExternalInput")
    wq_d = nc.dram_tensor("wq", [D, D], F32, kind="ExternalInput")
    wk_d = nc.dram_tensor("wk", [D, D], F32, kind="ExternalInput")
    wv_d = nc.dram_tensor("wv", [D, D], F32, kind="ExternalInput")
    wr_d = nc.dram_tensor("wr", [D, H * DR], F32, kind="ExternalInput")
    wo_d = nc.dram_tensor("wo", [D, D], F32, kind="ExternalInput")
    gw_d = nc.dram_tensor("gw", [E, D], F32, kind="ExternalInput")
    gb_d = nc.dram_tensor("gb", [E], F32, kind="ExternalInput")
    ew_d = nc.dram_tensor("ew", [E, D, D], F32, kind="ExternalInput")
    eb_d = nc.dram_tensor("eb", [E, D], F32, kind="ExternalInput")
    id_d = nc.dram_tensor("ident", [128, 128], F32, kind="ExternalInput")
    out_d = nc.dram_tensor("out", [SQ, D], F32, kind="ExternalOutput")

    skw = dict(kind="ExternalOutput") if dev == 1 else {}
    kscr = nc.dram_tensor("kscr", [H, 2, 97, S], BF16, **skw)
    qscr = nc.dram_tensor("qscr", [H, 2, 97, SQ], BF16, **skw)
    vscr = nc.dram_tensor("vscr", [2, 128, H, NKC * 65], BF16, **skw)
    h1kw = dict(kind="ExternalOutput") if dev == 3 else {}
    h1scr = nc.dram_tensor("h1scr", [SQ, D], F32, **h1kw)
    if dev == 2:
        attn_dbg = nc.dram_tensor("attn_dbg", [2, 128, 8 * SQ], BF16,
                                  kind="ExternalOutput")

    with ExitStack() as g:
        def gsb(name, shape, dt=F32, st=None):
            return (st or g).enter_context(nc.sbuf_tensor(name, shape, dt))

        def gps(name, shape, st=None):
            return (st or g).enter_context(nc.psum_tensor(name, shape, F32))

        def gsem(name, st=None):
            return (st or g).enter_context(nc.semaphore(name))

        id_sb = gsb("id_sb", [128, 128])
        onesb = gsb("onesb", [128, 128], BF16)
        onesf = gsb("onesf", [128, 64])
        c1024 = gsb("c1024", [128, 1])
        ceps = gsb("ceps", [128, 1])
        czero = gsb("czero", [128, 1])
        cn30 = gsb("cn30", [128, 1])

        sd = gsem("sd")
        sv0 = gsem("sv0")
        actw = gsb("actw", [128, 4])
        with nc.Block() as blk:
            @blk.sync
            def _(sy):
                sy.dma_start(id_sb[:, :], id_d[:, :]).then_inc(sd, 16)
                sy.wait_ge(sd, 16)

            @blk.vector
            def _(v):
                v.memset(onesb[:, :], 1.0)
                v.memset(onesf[:, :], 1.0)
                v.memset(c1024[:, :], 1.0 / 1024.0)
                v.memset(ceps[:, :], 1e-5)
                v.memset(czero[:, :], 0.0)
                v.memset(cn30[:, :], -1e30).then_inc(sv0, 1)

            @blk.scalar
            def _(sc):
                sc.wait_ge(sv0, 1)
                sc.activation(actw[:, 0:1], c1024[:, :], AF.Square)
                sc.activation(actw[:, 1:2], c1024[:, :], AF.Exp)
                sc.activation(actw[:, 2:3], c1024[:, :], AF.Sqrt)

        with ExitStack() as p1s:
            wa_h = gsb("wa_h", [128, 8 * D], BF16, p1s)
            wa_l = gsb("wa_l", [128, 8 * D], BF16, p1s)
            wv_h = gsb("wv_h", [128, 8 * D], BF16, p1s)
            wv_l = gsb("wv_l", [128, 8 * D], BF16, p1s)
            wr_h = gsb("wr_h", [128, 8 * 512], BF16, p1s)
            wr_l = gsb("wr_l", [128, 8 * 512], BF16, p1s)
            wst = [gsb(f"wst{i}", [128, D], F32, p1s) for i in range(2)]
            x_tok = [gsb(f"xt{i}", [128, D], F32, p1s) for i in range(4)]
            xfm_h = [gsb(f"xfm_h{i}", [128, 8 * 512], BF16, p1s)
                     for i in range(2)]
            xfm_l = [gsb(f"xfm_l{i}", [128, 8 * 512], BF16, p1s)
                     for i in range(2)]
            ev = [gsb(f"ev{i}", [128, 1024], BF16, p1s) for i in range(4)]
            sqb = [gsb(f"sqb{i}", [128, 512], BF16, p1s) for i in range(4)]
            vst_h = gsb("vst_h", [128, H * 260], BF16, p1s)
            vst_l = gsb("vst_l", [128, H * 260], BF16, p1s)
            emat = gsb("emat", [128, 12 * 16], BF16, p1s)
            nrm2k = gsb("nrm2k", [16, S], F32, p1s)
            nrm2q = gsb("nrm2q", [16, SQ], F32, p1s)
            rowb = gsb("rowb", [16, S], BF16, p1s)
            chib = gsb("chib", [16, SQ], BF16, p1s)
            mkt = gsb("mkt", [16, 1], F32, p1s)

            tp_ps = [gps(f"tpp{i}", [128, 128], p1s) for i in range(4)]
            pj_ps = [gps(f"pjp{i}", [128, 512], p1s) for i in range(2)]
            nr_ps = gps("nrp", [16, 512], p1s)

            s_wld = [gsem("s_wld0", p1s), gsem("s_wld1", p1s)]
            s_wh = gsem("s_wh", p1s)
            s_wl = gsem("s_wl", p1s)
            s_x = [gsem(f"s_x{i}", p1s) for i in range(4)]
            s_tp = gsem("s_tp", p1s)
            s_xh = gsem("s_xh", p1s)
            s_xl = gsem("s_xl", p1s)
            s_pj = gsem("s_pj", p1s)
            s_eh = gsem("s_eh", p1s)
            s_el = gsem("s_el", p1s)
            s_sq = gsem("s_sq", p1s)
            s_nm = gsem("s_nm", p1s)
            s_ne = gsem("s_ne", p1s)
            s_scw = [gsem(f"s_scw{i}", p1s) for i in range(4)]
            s_vscw = gsem("s_vscw", p1s)
            s_vh = gsem("s_vh", p1s)
            s_vl = gsem("s_vl", p1s)
            s_em = gsem("s_em", p1s)

            st = {"el_hist": [], "sq_hist": [], "nm_hist": [],
                  "scw_hist": [], "vst_dma": 0, "pj_tile": {}}
            pend_nrm = []

            def p1_phase(p, q_pass):
                ntile = NQ if q_pass else NKV
                xsrc = x_q if q_pass else x_kv
                scr_main = qscr if q_pass else kscr
                nrm_dst = nrm2q if q_pass else nrm2k

                wl = ([(wq_d, D, wa_h, wa_l)] if q_pass else
                      [(wk_d, D, wa_h, wa_l), (wv_d, D, wv_h, wv_l),
                       (wr_d, 512, wr_h, wr_l)])
                kv_chunks_total = p.cur(s_pj)
                for (wd, ncol, whi, wlo) in wl:
                    for ci in range(8):
                        n = st.setdefault("n_w", 0)
                        st["n_w"] = n + 1
                        wt = [(s_wl, p.cur(s_wl) - 1)] if n >= 2 else []
                        p.dma("gpsimd",
                              (lambda wd=wd, ncol=ncol, ci=ci, n=n:
                               lambda e: e.dma_start(
                                   wst[n % 2][:, 0:ncol],
                                   wd[ci * 128:(ci + 1) * 128, 0:ncol]))(),
                              waits=wt, inc=s_wld[n % 2])
                        wt_h = [(s_wld[n % 2], p.cur(s_wld[n % 2]))]
                        if q_pass and ci == 0:
                            wt_h.append((s_pj, kv_chunks_total))
                        p.add("scalar",
                              (lambda whi=whi, ncol=ncol, ci=ci, n=n:
                               lambda e: e.copy(
                                   whi[:, ci * ncol:(ci + 1) * ncol],
                                   wst[n % 2][:, 0:ncol]))(),
                              waits=wt_h, inc=s_wh)
                        p.add("vector",
                              (lambda whi=whi, wlo=wlo, ncol=ncol, ci=ci, n=n:
                               lambda e: e.tensor_sub(
                                   wlo[:, ci * ncol:(ci + 1) * ncol],
                                   wst[n % 2][:, 0:ncol],
                                   whi[:, ci * ncol:(ci + 1) * ncol]))(),
                              waits=[(s_wh, p.cur(s_wh))], inc=s_wl)
                wl_total = p.cur(s_wl)

                if not q_pass:
                    p.add("vector", lambda e: e.memset(emat[:, :], 0.0),
                          inc=s_em)

                    def setup(e):
                        for j in range(8):
                            e.memset(emat[0:64, j * 16 + 2 * j:
                                          j * 16 + 2 * j + 1], 1.0)
                            e.memset(emat[64:128, j * 16 + 2 * j + 1:
                                          j * 16 + 2 * j + 2], 1.0)
                        for j in range(4):
                            for rb in range(4):
                                hh = 4 * j + rb
                                e.memset(emat[32 * rb:32 * rb + 32,
                                              (8 + j) * 16 + hh:
                                              (8 + j) * 16 + hh + 1], 1.0)
                        vh3 = vst_h[:, :].rearrange("p (h c) -> p h c",
                                                    h=H * 4, c=65)
                        vl3 = vst_l[:, :].rearrange("p (h c) -> p h c",
                                                    h=H * 4, c=65)
                        e.memset(vh3[:, :, 64:65], 1.0)
                        return e.memset(vl3[:, :, 64:65], 0.0)
                    p.add("vector", setup, waits=[(s_em, 1)], inc=s_vl)

                if q_pass:
                    chunks = [("q", co) for co in range(8)] + \
                             [("r", co) for co in range(4)]
                else:
                    chunks = [("k", co) for co in range(8)] + \
                             [("r", co) for co in range(4)] + \
                             [("v", tch, oc) for tch in range(4)
                              for oc in range(2)]

                for tt in range(ntile):
                    gt = st.setdefault("gtile", 0)
                    st["gtile"] = gt + 1
                    xp = gt % 2
                    for tch in range(4):
                        xc = st.setdefault("xc", 0)
                        st["xc"] = xc + 1
                        wt = ([(s_tp, p.cur(s_tp) - 24)]
                              if p.cur(s_tp) >= 32 else [])
                        p.dma("sync",
                              (lambda xc=xc, tt=tt, tch=tch, xsrc=xsrc:
                               lambda e: e.dma_start(
                                   x_tok[xc % 4][:, :],
                                   xsrc[(tt * 4 + tch) * 128:
                                        (tt * 4 + tch + 1) * 128, :]))(),
                              waits=wt, inc=s_x[xc % 4])
                        for dc in range(8):
                            gi = p.cur(s_tp)
                            wt_t = [(s_x[xc % 4], p.cur(s_x[xc % 4]))]
                            if gi >= 4:
                                wt_t.append((s_xl, gi - 3))
                            p.add("tensor",
                                  (lambda gi=gi, xc=xc, dc=dc:
                                   lambda e: e.transpose(
                                       tp_ps[gi % 4][:, :],
                                       x_tok[xc % 4][:, dc * 128:dc * 128 + 128],
                                       id_sb[:, :]))(),
                                  waits=wt_t, inc=s_tp)
                            wt_s = [(s_tp, gi + 1)]
                            if tch == 0 and dc == 0:
                                if gt >= 2:
                                    wt_s.append(
                                        (s_pj, st["pj_tile"][gt - 2]))
                                wt_s.append((s_wl, wl_total))
                            col = dc * 512 + tch * 128
                            p.add("scalar",
                                  (lambda gi=gi, col=col, xp=xp:
                                   lambda e: e.copy(
                                       xfm_h[xp][:, col:col + 128],
                                       tp_ps[gi % 4][:, :]))(),
                                  waits=wt_s, inc=s_xh)
                            p.add("vector",
                                  (lambda gi=gi, col=col, xp=xp:
                                   lambda e: e.tensor_sub(
                                       xfm_l[xp][:, col:col + 128],
                                       tp_ps[gi % 4][:, :],
                                       xfm_h[xp][:, col:col + 128]))(),
                                  waits=[(s_xh, gi + 1)], inc=s_xl)

                    for (ic, c) in enumerate(chunks):
                        kind = c[0]
                        chc = p.cur(s_pj)
                        wt_m = [(s_xl, p.cur(s_xl)), (s_wl, wl_total)]
                        if chc >= 2:
                            wt_m += st["el_hist"][chc - 2]

                        def mk_proj(kind=kind, c=c, chc=chc, xp=xp):
                            def f(e):
                                last = None
                                for ci in range(8):
                                    xh_ = xfm_h[xp][:, ci * 512:(ci + 1) * 512]
                                    xl_ = xfm_l[xp][:, ci * 512:(ci + 1) * 512]
                                    if kind in ("k", "q"):
                                        co = c[1]
                                        lh = wa_h[:, ci * D + co * 128:
                                                  ci * D + co * 128 + 128]
                                        ll = wa_l[:, ci * D + co * 128:
                                                  ci * D + co * 128 + 128]
                                        rh, rl = xh_, xl_
                                    elif kind == "r":
                                        co = c[1]
                                        lh = wr_h[:, ci * 512 + co * 128:
                                                  ci * 512 + co * 128 + 128]
                                        ll = wr_l[:, ci * 512 + co * 128:
                                                  ci * 512 + co * 128 + 128]
                                        rh, rl = xh_, xl_
                                    else:
                                        _, tch, oc = c
                                        lh = xfm_h[xp][
                                            :, ci * 512 + tch * 128:
                                            ci * 512 + tch * 128 + 128]
                                        ll = xfm_l[xp][
                                            :, ci * 512 + tch * 128:
                                            ci * 512 + tch * 128 + 128]
                                        rh = wv_h[:, ci * D + oc * 512:
                                                  ci * D + oc * 512 + 512]
                                        rl = wv_l[:, ci * D + oc * 512:
                                                  ci * D + oc * 512 + 512]
                                    for (pi, (a, b)) in enumerate(
                                            [(lh, rh), (lh, rl), (ll, rh)]):
                                        last = e.matmul(
                                            pj_ps[chc % 2][:, :], a, b,
                                            start=(ci == 0 and pi == 0),
                                            stop=(ci == 7 and pi == 2))
                                return last
                            return f
                        p.add("tensor", mk_proj(), waits=wt_m, inc=s_pj)
                        while len(pend_nrm) > 1:
                            (fnrm, wnrm) = pend_nrm.pop(0)
                            p.add("tensor", fnrm, waits=wnrm, inc=s_nm)

                        if kind != "v":
                            evc = p.cur(s_eh)
                            wt_h = [(s_pj, chc + 1)]
                            if evc >= 4:
                                wt_h.append(st["scw_hist"][evc - 4])
                            p.add("scalar",
                                  (lambda evc=evc, chc=chc:
                                   lambda e: e.copy(ev[evc % 4][:, 0:512],
                                                    pj_ps[chc % 2][:, :]))(),
                                  waits=wt_h, inc=s_eh)
                            p.add("vector",
                                  (lambda evc=evc, chc=chc:
                                   lambda e: e.tensor_sub(
                                       ev[evc % 4][:, 512:1024],
                                       pj_ps[chc % 2][:, :],
                                       ev[evc % 4][:, 0:512]))(),
                                  waits=[(s_eh, p.cur(s_eh))], inc=s_el)
                            sqc = p.cur(s_sq)
                            wt_q = [(s_pj, chc + 1), (s_el, p.cur(s_el))]
                            if sqc >= 4:
                                wt_q.append(st["nm_hist"][sqc - 4])
                            p.add("scalar",
                                  (lambda sqc=sqc, chc=chc:
                                   lambda e: e.activation(
                                       sqb[sqc % 4][:, :],
                                       pj_ps[chc % 2][:, :], AF.Square))(),
                                  waits=wt_q, inc=s_sq)
                            co = c[1]
                            ei = co if kind in ("k", "q") else 8 + co
                            wt_n = [(s_sq, sqc + 1), (s_vl, 1)]
                            if ic == 0:
                                wt_n.append((s_ne, p.cur(s_ne)))
                            pend_nrm.append(
                                ((lambda sqc=sqc, ei=ei, nf=(ic == 0),
                                          nl=(ic == 11):
                                  lambda e: e.matmul(
                                      nr_ps[:, :],
                                      emat[:, ei * 16:ei * 16 + 16],
                                      sqb[sqc % 4][:, :],
                                      start=nf, stop=nl))(), wt_n))
                            st["nm_hist"].append((s_nm, p.cur(s_nm) + 1))
                            SS = SQ if q_pass else S
                            dmas = []
                            if kind in ("k", "q"):
                                for hh in (2 * co, 2 * co + 1):
                                    r0 = 0 if hh % 2 == 0 else 64
                                    dst = bass.AP(
                                        scr_main, hh * 2 * 97 * SS + tt * 512,
                                        [[SS, 64], [97 * SS, 2], [1, 512]])
                                    dmas.append(
                                        (dst, ev[evc % 4][r0:r0 + 64, :]))
                            else:
                                for rb in range(4):
                                    hh = 4 * co + rb
                                    dst = bass.AP(
                                        scr_main,
                                        hh * 2 * 97 * SS + 64 * SS + tt * 512,
                                        [[SS, 32], [97 * SS, 2], [1, 512]])
                                    dmas.append(
                                        (dst,
                                         ev[evc % 4][32 * rb:32 * rb + 32, :]))
                            for (j, (dst, srcap)) in enumerate(dmas):
                                p.dma("gpsimd",
                                      (lambda dst=dst, srcap=srcap:
                                       lambda e: e.dma_start(dst, srcap))(),
                                      waits=([(s_el, evc + 1)]
                                             if j == 0 else []),
                                      inc=s_scw[evc % 4])
                            st["scw_hist"].append(
                                (s_scw[evc % 4], p.cur(s_scw[evc % 4])))
                            st["el_hist"].append([(s_el, p.cur(s_el)),
                                                  (s_sq, p.cur(s_sq))])
                        else:
                            _, tch, oc = c
                            wt_vh = [(s_pj, chc + 1)]
                            if tch == 0 and oc == 0 and st["vst_dma"] > 0:
                                wt_vh.append((s_vscw, st["vst_dma"]))

                            def mk_vh(tch=tch, oc=oc, chc=chc):
                                def f(e):
                                    last = None
                                    for i in range(8):
                                        cl = (oc * 8 + i) * 260 + tch * 65
                                        last = e.copy(
                                            vst_h[:, cl:cl + 64],
                                            pj_ps[chc % 2][:, i * 64:i * 64 + 64])
                                    return last
                                return f
                            p.add("scalar", mk_vh(), waits=wt_vh, inc=s_vh)

                            def mk_vl(tch=tch, oc=oc, chc=chc):
                                def f(e):
                                    last = None
                                    for i in range(8):
                                        cl = (oc * 8 + i) * 260 + tch * 65
                                        last = e.tensor_sub(
                                            vst_l[:, cl:cl + 64],
                                            pj_ps[chc % 2][:, i * 64:i * 64 + 64],
                                            vst_h[:, cl:cl + 64])
                                    return last
                                return f
                            p.add("vector", mk_vl(),
                                  waits=[(s_vh, p.cur(s_vh))], inc=s_vl)
                            st["el_hist"].append([(s_vl, p.cur(s_vl))])

                    st.setdefault("pj_tile", {})[gt] = p.cur(s_pj)
                    while pend_nrm:
                        (fnrm, wnrm) = pend_nrm.pop(0)
                        p.add("tensor", fnrm, waits=wnrm, inc=s_nm)
                    p.add("vector",
                          (lambda tt=tt, nrm_dst=nrm_dst:
                           lambda e: e.tensor_copy(
                               nrm_dst[:, tt * 512:(tt + 1) * 512],
                               nr_ps[:, :]))(),
                          waits=[(s_nm, p.cur(s_nm))], inc=s_ne)
                    if not q_pass:
                        for (hl, srct) in ((0, vst_h), (1, vst_l)):
                            p.dma("gpsimd",
                                  (lambda hl=hl, srct=srct, tt=tt:
                                   lambda e: e.dma_start(
                                       vscr[hl, :, :,
                                            tt * 260:(tt + 1) * 260],
                                       srct[:, :].rearrange(
                                           "p (h c) -> p h c", h=H, c=260)))(),
                                  waits=([(s_vl, p.cur(s_vl))]
                                         if hl == 0 else []),
                                  inc=s_vscw)
                        st["vst_dma"] = p.cur(s_vscw)

            p = Sched()
            p1_phase(p, False)
            p1_phase(p, True)
            fin = ([(s_scw[i], p.cur(s_scw[i])) for i in range(4)]
                   + [(s_vscw, p.cur(s_vscw))])
            p.add("sync", lambda e: None, waits=fin)
            if trunc is not None:
                p.ops = p.ops[:trunc]
            with nc.Block() as blk:
                p.emit(blk)
            if trunc is not None:
                return nc

            sp1 = gsem("sp1", p1s)
            sv1 = gsem("sv1", p1s)
            with nc.Block() as blk:
                @blk.scalar
                def _(e):
                    e.activation(nrm2k[:, :], nrm2k[:, :],
                                 AF.Sqrt).then_inc(sp1, 1)
                    e.activation(nrm2q[:, :], nrm2q[:, :],
                                 AF.Sqrt).then_inc(sp1, 1)

                @blk.vector
                def _(e):
                    e.wait_ge(sp1, 1)
                    e.reduce_max(mkt[:, :], nrm2k[:, :],
                                 axis=AX.X).then_inc(sv1, 1)
                    e.wait_ge(sp1, 2)
                    e.wait_ge(sv1, 1)
                    e.tensor_scalar(nrm2q[:, :], nrm2q[:, :], mkt[:, :], None,
                                    op0=ALU.mult).then_inc(sv1, 1)
                    e.wait_ge(sv1, 2)
                    e.tensor_copy(chib[:, :], nrm2q[:, :]).then_inc(sv1, 1)
                    e.memset(rowb[:, :], -1.0).then_inc(sv1, 1)

                @blk.sync
                def _(e):
                    e.wait_ge(sv1, 4)
                    e.dma_start(qscr[:, 0, 96, :], chib[:, :]).then_inc(sd, 16)
                    e.dma_start(kscr[:, 0, 96, :], rowb[:, :]).then_inc(sd, 16)
                    e.wait_ge(sd, 48)
            with nc.Block() as blk:
                @blk.vector
                def _(e):
                    e.memset(rowb[:, :], 0.0).then_inc(sv1, 1)

                @blk.sync
                def _(e):
                    e.wait_ge(sv1, 5)
                    e.dma_start(kscr[:, 1, 96, :], rowb[:, :]).then_inc(sd, 16)
                    e.dma_start(qscr[:, 1, 96, :],
                                rowb[:, 0:SQ]).then_inc(sd, 16)
                    e.wait_ge(sd, 80)

        if dev == 1:
            return nc

        with ExitStack() as p23:
            at_h = gsb("at_h", [128, 8 * SQ], BF16, p23)
            at_l = gsb("at_l", [128, 8 * SQ], BF16, p23)
            with ExitStack() as p2s:
                k2 = [gsb(f"k2_{i}", [128, 2 * S], BF16, p2s)
                      for i in range(2)]
                q2 = [gsb(f"q2_{i}", [128, 2 * SQ], BF16, p2s)
                      for i in range(2)]
                v2 = [gsb(f"v2_{i}", [128, 2 * NKC * 65], BF16, p2s)
                      for i in range(2)]
                p32 = [gsb(f"p32_{i}", [128, 2048], F32, p2s)
                       for i in range(2)]
                p_h = [gsb(f"p2h{i}", [128, 2048], BF16, p2s)
                       for i in range(2)]
                p_l = [gsb(f"p2l{i}", [128, 2048], BF16, p2s)
                       for i in range(2)]
                linv = gsb("linv", [128, 2048], F32, p2s)
                bc_sb = gsb("bc_sb", [128, 512], F32, p2s)
                t32 = gsb("t32", [128, 512], F32, p2s)
                hi0 = gsb("hi0", [128, 512], BF16, p2s)
                lo0 = gsb("lo0", [128, 512], BF16, p2s)

                sc_ps = [gps(f"scp{i}", [128, 1024], p2s) for i in range(2)]
                av_ps = [gps(f"avp{i}", [128, 512], p2s) for i in range(4)]

                s2d = [gsem("s2d0", p2s), gsem("s2d1", p2s)]
                s2sa = gsem("s2sa", p2s)
                s2sb = gsem("s2sb", p2s)
                s2ea = gsem("s2ea", p2s)
                s2eb = gsem("s2eb", p2s)
                s2pA = gsem("s2pA", p2s)
                s2pB = gsem("s2pB", p2s)
                s2av = gsem("s2av", p2s)
                s2li = gsem("s2li", p2s)
                s2bc = gsem("s2bc", p2s)
                s2ev = gsem("s2ev", p2s)
                s2ch = gsem("s2ch", p2s)
                s2bcc = gsem("s2bcc", p2s)
                s2hc = gsem("s2hc", p2s)

                p = Sched()

                def head_load(hh):
                    pb = hh % 2
                    wt = [(s2av, NKC * (hh - 1))] if hh >= 2 else []
                    ksrc = bass.AP(kscr, hh * 2 * 97 * S,
                                   [[S, 97], [97 * S, 2], [1, S]])
                    qsrc = bass.AP(qscr, hh * 2 * 97 * SQ,
                                   [[SQ, 97], [97 * SQ, 2], [1, SQ]])
                    vsrc = bass.AP(vscr, hh * NKC * 65,
                                   [[H * NKC * 65, 128],
                                    [128 * H * NKC * 65, 2], [1, NKC * 65]])
                    loads = [(k2[pb][0:97, :], ksrc),
                             (q2[pb][0:97, :], qsrc),
                             (v2[pb][:, :], vsrc)]
                    for (j, (dst, srcap)) in enumerate(loads):
                        p.dma("sync",
                              (lambda dst=dst, srcap=srcap:
                               lambda e: e.dma_start(dst, srcap))(),
                              waits=(wt if j == 0 else []), inc=s2d[pb])

                def sched_evac(hh):
                    rows0 = (hh % 2) * 64
                    chj = hh // 2
                    for qt in range(4):
                        p.add("vector",
                              (lambda qt=qt: lambda e: e.reciprocal(
                                  linv[64:65, qt * 512:qt * 512 + 512],
                                  av_ps[qt][64:65, :]))(),
                              waits=[(s2av, p.cur(s2av))], inc=s2li)
                    li_base = p.cur(s2li)
                    for qt in range(4):
                        wt_b = [(s2li, li_base - 3 + qt)]
                        p.add("tensor",
                              (lambda qt=qt:
                               lambda e: e.matmul(
                                   av_ps[qt][64:128, :],
                                   onesf[64:65, 0:64],
                                   linv[64:65, qt * 512:qt * 512 + 512],
                                   start=True, stop=True))(),
                              waits=wt_b, inc=s2bc)
                        rows0 = (hh % 2) * 64
                        chj = hh // 2
                        cc = chj * SQ + qt * 512
                        p.add("vector",
                              (lambda qt=qt:
                               lambda e: e.tensor_copy(
                                   bc_sb[0:64, :],
                                   av_ps[qt][64:128, :]))(),
                              waits=[(s2bc, p.cur(s2bc)),
                                     (s2ch, p.cur(s2ch))], inc=s2bcc)
                        p.add("vector",
                              (lambda qt=qt: lambda e: e.tensor_mul(
                                  t32[0:64, :], av_ps[qt][0:64, :],
                                  bc_sb[0:64, :]))(),
                              waits=[(s2bcc, p.cur(s2bcc))], inc=s2ch)
                        p.add("vector",
                              (lambda: lambda e: e.tensor_copy(
                                  hi0[0:64, :], t32[0:64, :]))(),
                              waits=[(s2ch, p.cur(s2ch))], inc=s2ch)
                        p.add("vector",
                              (lambda: lambda e: e.tensor_sub(
                                  lo0[0:64, :], t32[0:64, :],
                                  hi0[0:64, :]))(),
                              waits=[(s2ch, p.cur(s2ch)),
                                     (s2ev, p.cur(s2ev))], inc=s2ch)
                        p.add("vector",
                              (lambda cc=cc, rows0=rows0:
                               lambda e: e.tensor_copy(
                                   at_h[rows0:rows0 + 64, cc:cc + 512],
                                   hi0[0:64, :]))(),
                              waits=[(s2ch, p.cur(s2ch))], inc=s2ch)
                        p.add("vector",
                              (lambda cc=cc, rows0=rows0:
                               lambda e: e.tensor_copy(
                                   at_l[rows0:rows0 + 64, cc:cc + 512],
                                   lo0[0:64, :]))(),
                              waits=[(s2ch, p.cur(s2ch))], inc=s2ev)


                head_load(0)
                for hh in range(H):
                    pb = hh % 2
                    d2val = p.cur(s2d[pb])
                    if hh + 1 < H:
                        head_load(hh + 1)

                    def mk_av(kc, pp, qts, pb=pb):
                        def f(e):
                            last = None
                            VL = NKC * 65
                            for qt in qts:
                                ops = [(0, p_h[pp]), (0, p_l[pp]),
                                       (1, p_h[pp])]
                                for (pi, (va, b)) in enumerate(ops):
                                    last = e.matmul(
                                        av_ps[qt][0:65, :],
                                        v2[pb][:, va * VL + kc * 65:
                                               va * VL + kc * 65 + 65],
                                        b[:, qt * 512:qt * 512 + 512],
                                        start=(kc == 0 and pi == 0),
                                        stop=(kc == NKC - 1 and pi == 2))
                            return last
                        return f

                    p_marks = {}
                    for kc in range(NKC):
                        pp = kc % 2
                        for (half, sps, ssem, esem) in (
                                (0, sc_ps[0], s2sa, s2ea),
                                (1, sc_ps[1], s2sb, s2eb)):
                            wt = [(s2d[pb], d2val), (esem, p.cur(esem))]

                            def mk_sc(kc=kc, half=half, sps=sps, pb=pb):
                                def f(e):
                                    last = None
                                    for qi in range(2):
                                        qt = half * 2 + qi
                                        ops = [(0, 0), (0, 1), (1, 0)]
                                        for (pi, (ka, qa)) in enumerate(ops):
                                            last = e.matmul(
                                                sps[:, qi * 512:qi * 512 + 512],
                                                k2[pb][0:97,
                                                       ka * S + kc * 128:
                                                       ka * S + kc * 128 + 128],
                                                q2[pb][0:97,
                                                       qa * SQ + qt * 512:
                                                       qa * SQ + qt * 512 + 512],
                                                start=(pi == 0), stop=(pi == 2))
                                    return last
                                return f
                            p.add("tensor", mk_sc(), waits=wt, inc=ssem)
                            wt_e = [(ssem, p.cur(ssem))]
                            phs = s2pA if half == 0 else s2pB
                            if p.cur(phs) >= 2:
                                wt_e.append((phs, p.cur(phs) - 1))
                            p.add("scalar",
                                  (lambda pp=pp, half=half, sps=sps:
                                   lambda e: e.activation(
                                       p32[pp][:, half * 1024:
                                               half * 1024 + 1024],
                                       sps[:, :], AF.Exp, scale=SCALE))(),
                                  waits=wt_e, inc=esem)
                        wt_v = [(s2ea, p.cur(s2ea))]
                        if p.cur(s2av) >= 1:
                            wt_v.append((s2av, p.cur(s2av)))
                        p.add("vector",
                              (lambda pp=pp: lambda e: e.tensor_copy(
                                  p_h[pp][:, 0:1024], p32[pp][:, 0:1024]))(),
                              waits=wt_v, inc=s2ch)
                        p.add("vector",
                              (lambda pp=pp: lambda e: e.tensor_sub(
                                  p_l[pp][:, 0:1024], p32[pp][:, 0:1024],
                                  p_h[pp][:, 0:1024]))(),
                              waits=[(s2ch, p.cur(s2ch))], inc=s2pA)
                        p.add("vector",
                              (lambda pp=pp: lambda e: e.tensor_copy(
                                  p_h[pp][:, 1024:2048],
                                  p32[pp][:, 1024:2048]))(),
                              waits=[(s2eb, p.cur(s2eb))], inc=s2ch)
                        p.add("vector",
                              (lambda pp=pp: lambda e: e.tensor_sub(
                                  p_l[pp][:, 1024:2048],
                                  p32[pp][:, 1024:2048],
                                  p_h[pp][:, 1024:2048]))(),
                              waits=[(s2ch, p.cur(s2ch))], inc=s2pB)
                        p_marks[kc] = (p.cur(s2pA), p.cur(s2pB))
                        if kc >= 1:
                            if kc == 1 and hh > 0:
                                sched_evac(hh - 1)
                            wt_av = [(s2pA, p_marks[kc - 1][0])]
                            if kc == 1 and hh > 0:
                                wt_av.append((s2ev, 4 * hh))
                            p.add("tensor",
                                  mk_av(kc - 1, (kc - 1) % 2, (0, 1)),
                                  waits=wt_av)
                            p.add("tensor",
                                  mk_av(kc - 1, (kc - 1) % 2, (2, 3)),
                                  waits=[(s2pB, p_marks[kc - 1][1])],
                                  inc=s2av)
                    p.add("tensor", mk_av(NKC - 1, (NKC - 1) % 2, (0, 1)),
                          waits=[(s2pA, p_marks[NKC - 1][0])])
                    p.add("tensor", mk_av(NKC - 1, (NKC - 1) % 2, (2, 3)),
                          waits=[(s2pB, p_marks[NKC - 1][1])], inc=s2av)
                    if hh == H - 1:
                        sched_evac(hh)

                with nc.Block() as blk:
                    p.emit(blk)

                if dev == 2:
                    with nc.Block() as blk:
                        @blk.sync
                        def _(e):
                            e.dma_start(attn_dbg[0, :, :],
                                        at_h[:, :]).then_inc(sd, 16)
                            e.dma_start(attn_dbg[1, :, :],
                                        at_l[:, :]).then_inc(sd, 16)
                            e.wait_ge(sd, 80 + 32)
                    return nc

            h1f_h, h1f_l = at_h, at_l
            if True:
                with ExitStack() as p3s:
                    wo_h = gsb("wo_h", [128, 8 * D], BF16, p3s)
                    wo_l = gsb("wo_l", [128, 8 * D], BF16, p3s)
                    wst3 = [gsb(f"wst3{i}", [128, D], F32, p3s)
                            for i in range(2)]
                    xq_sb = [gsb(f"xq{i}", [128, D], F32, p3s)
                             for i in range(2)]
                    s1b = [gsb(f"s1b{i}", [128, D], F32, p3s)
                           for i in range(2)]
                    sq3 = [gsb(f"sq3{i}", [128, D], F32, p3s)
                           for i in range(2)]
                    h1b = [gsb(f"h1b{i}", [128, D], F32, p3s)
                           for i in range(2)]
                    st3 = gsb("st3", [128, 8], F32, p3s)

                    y_ps = [gps(f"yp{i}", [128, 1024], p3s) for i in range(2)]
                    t3_ps = [gps(f"t3p{i}", [128, 128], p3s) for i in range(4)]

                    s3w = [gsem("s3w0", p3s), gsem("s3w1", p3s)]
                    s3wh = gsem("s3wh", p3s)
                    s3wl = gsem("s3wl", p3s)
                    s3x = [gsem("s3x0", p3s), gsem("s3x1", p3s)]
                    s3t = gsem("s3t", p3s)
                    s3s1 = gsem("s3s1", p3s)
                    s3sq = gsem("s3sq", p3s)
                    s3st = gsem("s3st", p3s)
                    s3ch = gsem("s3ch", p3s)
                    s3ri = gsem("s3ri", p3s)
                    s3h1 = gsem("s3h1", p3s)
                    s3d = [gsem("s3d0", p3s), gsem("s3d1", p3s)]
                    s3tp = gsem("s3tp", p3s)
                    s3fh = gsem("s3fh", p3s)
                    s3fl = gsem("s3fl", p3s)

                    p = Sched()
                    pend_tp3 = []

                    def flush_tp3():
                        while pend_tp3:
                            (f3, w3, kind3, inc3) = pend_tp3.pop(0)
                            p.add(kind3, f3, waits=w3, inc=inc3)
                    for n in range(8):
                        wt = [(s3wl, n - 1)] if n >= 2 else []
                        p.dma("gpsimd",
                              (lambda n=n: lambda e: e.dma_start(
                                  wst3[n % 2][:, :],
                                  wo_d[n * 128:(n + 1) * 128, :]))(),
                              waits=wt, inc=s3w[n % 2])
                        p.add("scalar",
                              (lambda n=n: lambda e: e.copy(
                                  wo_h[:, n * D:(n + 1) * D],
                                  wst3[n % 2][:, :]))(),
                              waits=[(s3w[n % 2], p.cur(s3w[n % 2]))],
                              inc=s3wh)
                        p.add("vector",
                              (lambda n=n: lambda e: e.tensor_sub(
                                  wo_l[:, n * D:(n + 1) * D],
                                  wst3[n % 2][:, :],
                                  wo_h[:, n * D:(n + 1) * D]))(),
                              waits=[(s3wh, p.cur(s3wh))], inc=s3wl)
                    wosplit = p.cur(s3wl)

                    for tc in range(NTC):
                        pb = tc % 2
                        wt_x = [(s3s1, p.cur(s3s1) - 1)] if tc >= 2 else []
                        p.dma("sync",
                              (lambda tc=tc, pb=pb: lambda e: e.dma_start(
                                  xq_sb[pb][:, :],
                                  x_q[tc * 128:(tc + 1) * 128, :]))(),
                              waits=wt_x, inc=s3x[pb])

                        wt_t = [(s3wl, wosplit)]
                        if tc >= 2:
                            wt_t.append((s3s1, p.cur(s3s1) - 1))

                        def mk_y(tc=tc, pb=pb):
                            def f(e):
                                last = None
                                for ot in range(2):
                                    n = 0
                                    for j in range(8):
                                        ah = at_h[:, j * SQ + tc * 128:
                                                  j * SQ + tc * 128 + 128]
                                        al = at_l[:, j * SQ + tc * 128:
                                                  j * SQ + tc * 128 + 128]
                                        bh = wo_h[:, j * D + ot * 512:
                                                  j * D + ot * 512 + 512]
                                        bl = wo_l[:, j * D + ot * 512:
                                                  j * D + ot * 512 + 512]
                                        for (a, b) in ((ah, bh), (ah, bl),
                                                       (al, bh)):
                                            last = e.matmul(
                                                y_ps[pb][:, ot * 512:
                                                         ot * 512 + 512],
                                                a, b, start=(n == 0),
                                                stop=(n == 23))
                                            n += 1
                                return last
                            return f
                        p.add("tensor", mk_y(), waits=wt_t, inc=s3t)
                        flush_tp3()

                        p.add("vector",
                              (lambda pb=pb: lambda e: e.tensor_add(
                                  s1b[pb][:, :], y_ps[pb][:, :],
                                  xq_sb[pb][:, :]))(),
                              waits=[(s3t, p.cur(s3t)),
                                     (s3x[pb], p.cur(s3x[pb]))],
                              inc=s3s1)
                        p.add("scalar",
                              (lambda pb=pb: lambda e: e.activation(
                                  sq3[pb][:, :], s1b[pb][:, :], AF.Square))(),
                              waits=[(s3s1, p.cur(s3s1))], inc=s3sq)

                        def chain3(fn, waits=()):
                            p.add("vector", fn,
                                  waits=list(waits) + [(s3ch, p.cur(s3ch))],
                                  inc=s3ch)
                        chain3((lambda pb=pb: lambda e: e.reduce_sum(
                            st3[:, 0:1], s1b[pb][:, :], axis=AX.X))(),
                            waits=[(s3ri, p.cur(s3ri)),
                                   (s3s1, p.cur(s3s1))])
                        chain3((lambda pb=pb: lambda e: e.reduce_sum(
                            st3[:, 1:2], sq3[pb][:, :], axis=AX.X))(),
                            waits=[(s3sq, p.cur(s3sq))])
                        chain3(lambda e: e.tensor_scalar(
                            st3[:, 2:3], st3[:, 0:1], c1024[:, :], None,
                            op0=ALU.mult))
                        chain3(lambda e: e.tensor_scalar(
                            st3[:, 3:4], st3[:, 1:2], c1024[:, :], None,
                            op0=ALU.mult))
                        chain3(lambda e: e.tensor_mul(
                            st3[:, 4:5], st3[:, 2:3], st3[:, 2:3]))
                        chain3(lambda e: e.tensor_sub(
                            st3[:, 5:6], st3[:, 3:4], st3[:, 4:5]))
                        chain3(lambda e: e.tensor_add(
                            st3[:, 6:7], st3[:, 5:6], ceps[:, :]))
                        p.add("vector",
                              lambda e: e.reciprocal(st3[:, 7:8],
                                                     st3[:, 6:7]),
                              waits=[(s3ch, p.cur(s3ch))], inc=s3st)
                        p.add("scalar",
                              (lambda: lambda e: e.activation(
                                  st3[:, 6:7], st3[:, 7:8], AF.Sqrt))(),
                              waits=[(s3st, p.cur(s3st))], inc=s3ri)
                        wt_h1 = [(s3ri, p.cur(s3ri))]
                        if tc >= 2:
                            wt_h1 += [(s3d[pb], p.cur(s3d[pb])),
                                      (s3tp, 8 * (tc - 1))]
                        p.add("vector",
                              (lambda pb=pb: lambda e: e.tensor_scalar(
                                  h1b[pb][:, :], s1b[pb][:, :], st3[:, 2:3],
                                  st3[:, 6:7], op0=ALU.subtract,
                                  op1=ALU.mult))(),
                              waits=wt_h1, inc=s3h1)
                        p.dma("sync",
                              (lambda tc=tc, pb=pb: lambda e: e.dma_start(
                                  h1scr[tc * 128:(tc + 1) * 128, :],
                                  h1b[pb][:, :]))(),
                              waits=[(s3h1, p.cur(s3h1))], inc=s3d[pb])
                        h1_mark = p.cur(s3h1)
                        for dc in range(8):
                            gi = st.setdefault("gi3", 0)
                            st["gi3"] = gi + 1
                            wt_tp = [(s3h1, h1_mark)]
                            if gi >= 4:
                                wt_tp.append((s3fl, gi - 3))
                            pend_tp3.append((
                                (lambda gi=gi, pb=pb, dc=dc:
                                 lambda e: e.transpose(
                                     t3_ps[gi % 4][:, :],
                                     h1b[pb][:, dc * 128:dc * 128 + 128],
                                     id_sb[:, :]))(),
                                wt_tp, "tensor", s3tp))
                            col = dc * SQ + tc * 128
                            pend_tp3.append((
                                (lambda gi=gi, col=col: lambda e: e.copy(
                                    h1f_h[:, col:col + 128],
                                    t3_ps[gi % 4][:, :]))(),
                                [(s3tp, gi + 1)], "scalar", s3fh))
                            pend_tp3.append((
                                (lambda gi=gi, col=col:
                                 lambda e: e.tensor_sub(
                                     h1f_l[:, col:col + 128],
                                     t3_ps[gi % 4][:, :],
                                     h1f_h[:, col:col + 128]))(),
                                [(s3fh, gi + 1)], "vector", s3fl))

                    flush_tp3()
                    h1d_tot = (p.cur(s3d[0]), p.cur(s3d[1]))
                    p.add("sync", lambda e: None,
                          waits=[(s3d[0], h1d_tot[0]), (s3d[1], h1d_tot[1])])
                    with nc.Block() as blk:
                        p.emit(blk)

                if dev == 3:
                    with nc.Block() as blk:
                        @blk.sync
                        def _(e):
                            e.wait_ge(s3d[0], h1d_tot[0])
                            e.wait_ge(s3d[1], h1d_tot[1])
                    return nc

                with ExitStack() as p4s:
                    ew_h = [gsb(f"ewh{i}", [128, 8 * D], BF16, p4s)
                            for i in range(2)]
                    ew_l = [gsb(f"ewl{i}", [128, 8 * D], BF16, p4s)
                            for i in range(2)]
                    wst4 = [gsb(f"wst4{i}", [128, D], F32, p4s)
                            for i in range(2)]
                    acc = gsb("acc", [128, 8 * D], F32, p4s)
                    gw_sb = wst4[0][0:8, :]
                    eb_sb = wst4[1][0:8, :]
                    gbr_t = gsb("gbr_t", [1, 8], F32, p4s)
                    gbr = gbr_t[0:1, 0:8]
                    gwt_h = gsb("gwt_h", [128, 64], BF16, p4s)
                    gwt_l = gsb("gwt_l", [128, 64], BF16, p4s)
                    gbr_h = gsb("gbr_h", [1, 8], BF16, p4s)
                    gbr_l = gsb("gbr_l", [1, 8], BF16, p4s)
                    eb_h = gsb("eb_h", [8, D], BF16, p4s)
                    eb_l = gsb("eb_l", [8, D], BF16, p4s)
                    lg_sb = gsb("lg_sb", [128, 8], F32, p4s)
                    gt1 = gsb("gt1", [128, 8], F32, p4s)
                    gt2 = gsb("gt2", [128, 8], F32, p4s)
                    g_all = gsb("g_all", [128, NTC * 8], F32, p4s)
                    gT_h = gsb("gT_h", [8, SQ], BF16, p4s)
                    gT_l = gsb("gT_l", [8, SQ], BF16, p4s)
                    h1r = [gsb(f"h1r{i}", [128, D], F32, p4s)
                           for i in range(2)]
                    s2b = [gsb(f"s2b{i}", [128, D], F32, p4s)
                           for i in range(2)]
                    ob = [gsb(f"ob{i}", [128, D], F32, p4s) for i in range(2)]
                    st4 = gsb("st4", [128, 8], F32, p4s)

                    lg_ps = gps("lgp", [128, 512], p4s)
                    ep_ps = [gps(f"epp{i}", [128, 1024], p4s)
                             for i in range(3)]
                    t4_ps = gps("t4p", [128, 128], p4s)

                    s4gw = gsem("s4gw", p4s)
                    s4w = [gsem("s4w0", p4s), gsem("s4w1", p4s)]
                    s4wh = gsem("s4wh", p4s)
                    s4wl = gsem("s4wl", p4s)
                    s4g = gsem("s4g", p4s)
                    s4gt = gsem("s4gt", p4s)
                    s4ge = gsem("s4ge", p4s)
                    s4t = gsem("s4t", p4s)
                    s4c = gsem("s4c", p4s)
                    s4x = [gsem("s4x0", p4s), gsem("s4x1", p4s)]
                    s4s2 = gsem("s4s2", p4s)
                    s4sq = gsem("s4sq", p4s)
                    s4st = gsem("s4st", p4s)
                    s4ch = gsem("s4ch", p4s)
                    s4ri = gsem("s4ri", p4s)
                    s4o = gsem("s4o", p4s)
                    s4od = [gsem("s4od0", p4s), gsem("s4od1", p4s)]

                    p = Sched()
                    p.dma("sync", lambda e: e.dma_start(gw_sb, gw_d[:, :]),
                          inc=s4gw)
                    p.dma("sync", lambda e: e.dma_start(gbr, gb_d[None, :]),
                          inc=s4gw)
                    p.dma("sync", lambda e: e.dma_start(eb_sb, eb_d[:, :]),
                          inc=s4gw)
                    p.add("scalar", lambda e: e.copy(gbr_h[:, :], gbr),
                          waits=[(s4gw, 48)], inc=s4wh)
                    p.add("vector", lambda e: e.tensor_sub(
                        gbr_l[:, :], gbr, gbr_h[:, :]),
                        waits=[(s4wh, p.cur(s4wh))], inc=s4wl)
                    p.add("scalar", lambda e: e.copy(eb_h[:, :], eb_sb),
                          waits=[(s4gw, 48)], inc=s4wh)
                    p.add("vector", lambda e: e.tensor_sub(
                        eb_l[:, :], eb_sb, eb_h[:, :]),
                        waits=[(s4wh, p.cur(s4wh))], inc=s4wl)
                    for ci in range(8):
                        p.add("tensor",
                              (lambda ci=ci: lambda e: e.transpose(
                                  t4_ps[:, 0:8],
                                  wst4[0][0:8, ci * 128:(ci + 1) * 128],
                                  id_sb[0:8, 0:8]))(),
                              waits=[(s4gw, 48), (s4wl, p.cur(s4wl))],
                              inc=s4gt)
                        p.add("scalar",
                              (lambda ci=ci: lambda e: e.copy(
                                  gwt_h[:, ci * 8:(ci + 1) * 8],
                                  t4_ps[:, 0:8]))(),
                              waits=[(s4gt, p.cur(s4gt))], inc=s4wh)
                        p.add("vector",
                              (lambda ci=ci: lambda e: e.tensor_sub(
                                  gwt_l[:, ci * 8:(ci + 1) * 8],
                                  t4_ps[:, 0:8],
                                  gwt_h[:, ci * 8:(ci + 1) * 8]))(),
                              waits=[(s4wh, p.cur(s4wh))], inc=s4wl)
                    gwt_done = p.cur(s4wl)

                    gate_mark = {}

                    def sched_gate(tc):
                        def mk_lg(tc=tc):
                            def f(e):
                                n = 0
                                for j in range(8):
                                    ah = h1f_h[:, j * SQ + tc * 128:
                                               j * SQ + tc * 128 + 128]
                                    al = h1f_l[:, j * SQ + tc * 128:
                                               j * SQ + tc * 128 + 128]
                                    bh = gwt_h[:, j * 8:(j + 1) * 8]
                                    bl = gwt_l[:, j * 8:(j + 1) * 8]
                                    for (a, b) in ((ah, bh), (ah, bl),
                                                   (al, bh)):
                                        e.matmul(lg_ps[:, 0:8], a, b,
                                                 start=(n == 0), stop=False)
                                        n += 1
                                e.matmul(lg_ps[:, 0:8], onesb[0:1, 0:128],
                                         gbr_h[0:1, :], start=False,
                                         stop=False)
                                return e.matmul(lg_ps[:, 0:8],
                                                onesb[0:1, 0:128],
                                                gbr_l[0:1, :], start=False,
                                                stop=True)
                            return f
                        wt = [(s4wl, gwt_done)]
                        if st.get("copy_mark", 0) > 0:
                            wt.append((s4ch, st["copy_mark"]))
                        p.add("tensor", mk_lg(), waits=wt, inc=s4gt)

                        def chaing(fn, waits=()):
                            p.add("vector", fn,
                                  waits=list(waits)
                                  + [(s4ch, p.cur(s4ch))], inc=s4ch)
                        chaing(lambda e: e.tensor_copy(lg_sb[:, :],
                                                       lg_ps[:, 0:8]),
                               waits=[(s4gt, p.cur(s4gt)),
                                      (s4ge, p.cur(s4ge))])
                        copy_mark_next[0] = p.cur(s4ch)
                        chaing(lambda e: e.reduce_max(
                            st4[:, 0:1], lg_sb[:, :], axis=AX.X, negate=True))
                        chaing(lambda e: e.tensor_scalar(
                            gt1[:, :], lg_sb[:, :], st4[:, 0:1], czero[:, :],
                            op0=ALU.add, op1=ALU.is_ge))
                        chaing(lambda e: e.scalar_tensor_tensor(
                            gt2[:, :], gt1[:, :], cn30[:, :], lg_sb[:, :],
                            op0=ALU.mult, op1=ALU.add))
                        chaing(lambda e: e.reduce_max(
                            st4[:, 1:2], gt2[:, :], axis=AX.X, negate=True))
                        p.add("vector",
                              lambda e: e.tensor_scalar(
                                  gt1[:, :], lg_sb[:, :], st4[:, 1:2],
                                  czero[:, :], op0=ALU.add, op1=ALU.is_ge),
                              waits=[(s4ch, p.cur(s4ch))], inc=s4g)
                        p.add("scalar",
                              (lambda tc=tc: lambda e: e.activation(
                                  gt2[:, :], lg_sb[:, :], AF.Exp,
                                  bias=st4[:, 0:1]))(),
                              waits=[(s4g, p.cur(s4g))], inc=s4ge)
                        chaing(lambda e: e.tensor_mul(
                            gt2[:, :], gt2[:, :], gt1[:, :]),
                            waits=[(s4ge, p.cur(s4ge))])
                        chaing(lambda e: e.reduce_sum(
                            st4[:, 2:3], gt2[:, :], axis=AX.X))
                        chaing(lambda e: e.reciprocal(st4[:, 3:4],
                                                      st4[:, 2:3]))
                        p.add("vector",
                              (lambda tc=tc: lambda e: e.tensor_scalar(
                                  g_all[:, tc * 8:(tc + 1) * 8], gt2[:, :],
                                  st4[:, 3:4], None, op0=ALU.mult))(),
                              waits=[(s4ch, p.cur(s4ch))], inc=s4g)
                        gate_mark[tc] = p.cur(s4g)
                        st["copy_mark"] = copy_mark_next[0]
                        pend_gT.append((tc, p.cur(s4g), p.cur(s4wl)))

                    order = [(g, e) for g in range(2) for e in range(E)]
                    marks = {}
                    smarks = {}
                    slot_free = {0: 0, 1: 0, 2: 0}
                    gates_todo = list(range(NTC))
                    pend_gT = []
                    copy_mark_next = [0]

                    def flush_gT():
                        (tc_, gmark, wlmark) = pend_gT.pop(0)
                        p.add("tensor",
                              (lambda tc=tc_: lambda e: e.transpose(
                                  t4_ps[0:8, 0:128],
                                  g_all[:, tc * 8:(tc + 1) * 8],
                                  id_sb[:, :]))(),
                              waits=[(s4g, gmark),
                                     (s4wl, p.cur(s4wl))], inc=s4gt)
                        p.add("scalar",
                              (lambda tc=tc_: lambda e: e.copy(
                                  gT_h[0:8, tc * 128:(tc + 1) * 128],
                                  t4_ps[0:8, 0:128]))(),
                              waits=[(s4gt, p.cur(s4gt))], inc=s4wh)
                        p.add("vector",
                              (lambda tc=tc_: lambda e: e.tensor_sub(
                                  gT_l[0:8, tc * 128:(tc + 1) * 128],
                                  t4_ps[0:8, 0:128],
                                  gT_h[0:8, tc * 128:(tc + 1) * 128]))(),
                              waits=[(s4wh, p.cur(s4wh))], inc=s4wl)

                    def sched_loads(eli):
                        e_i = order[eli][1]
                        for ci in range(8):
                            n = st.setdefault("n4", 0)
                            st["n4"] = n + 1
                            if eli > 0 or ci >= 2:
                                wt = [(s4wl, p.cur(s4wl) - 1)]
                            else:
                                wt = [(s4wl, gwt_done), (s4gt, 8)]
                            p.dma("gpsimd",
                                  (lambda e_i=e_i, ci=ci, n=n:
                                   lambda e: e.dma_start(
                                       wst4[n % 2][:, :],
                                       ew_d[e_i, ci * 128:(ci + 1) * 128,
                                            :]))(),
                                  waits=wt, inc=s4w[n % 2])
                            wt_h = [(s4w[n % 2], p.cur(s4w[n % 2]))]
                            if ci == 0 and eli >= 2:
                                wt_h.append((s4t, marks[eli - 2]))
                            p.add("scalar",
                                  (lambda eli=eli, ci=ci, n=n:
                                   lambda e: e.copy(
                                       ew_h[eli % 2][:, ci * D:(ci + 1) * D],
                                       wst4[n % 2][:, :]))(),
                                  waits=wt_h, inc=s4wh)
                            p.add("vector",
                                  (lambda eli=eli, ci=ci, n=n:
                                   lambda e: e.tensor_sub(
                                       ew_l[eli % 2][:, ci * D:(ci + 1) * D],
                                       wst4[n % 2][:, :],
                                       ew_h[eli % 2][:, ci * D:(ci + 1) * D]
                                   ))(),
                                  waits=[(s4wh, p.cur(s4wh))], inc=s4wl)
                        smarks[eli] = p.cur(s4wl)

                    for grp in range(2):
                        tcs = list(range(grp * 8, grp * 8 + 8))
                        for e_i in range(E):
                            eli = grp * E + e_i
                            if eli == 0:
                                sched_loads(0)
                            split_mark = smarks[eli]

                            for (ti, tc) in enumerate(tcs):
                                if gates_todo and eli < 4:
                                    sched_gate(gates_todo.pop(0))
                                elif pend_gT:
                                    flush_gT()
                                pb = tc % 3
                                sl = tc % 8

                                def mk_exp(eli=eli, tc=tc, pb=pb):
                                    def f(e):
                                        last = None
                                        for ot in range(2):
                                            n = 0
                                            for j in range(8):
                                                ah = h1f_h[
                                                    :, j * SQ + tc * 128:
                                                    j * SQ + tc * 128 + 128]
                                                al = h1f_l[
                                                    :, j * SQ + tc * 128:
                                                    j * SQ + tc * 128 + 128]
                                                bh = ew_h[eli % 2][
                                                    :, j * D + ot * 512:
                                                    j * D + ot * 512 + 512]
                                                bl = ew_l[eli % 2][
                                                    :, j * D + ot * 512:
                                                    j * D + ot * 512 + 512]
                                                for (a, b) in ((ah, bh),
                                                               (ah, bl),
                                                               (al, bh)):
                                                    last = e.matmul(
                                                        ep_ps[pb][
                                                            :, ot * 512:
                                                            ot * 512 + 512],
                                                        a, b,
                                                        start=(n == 0),
                                                        stop=(n == 23))
                                                    n += 1
                                        return last
                                    return f
                                p.add("tensor", mk_exp(),
                                      waits=[(s4wl, split_mark),
                                             (s4c, slot_free[pb])],
                                      inc=s4t)
                                wt_c = [(s4t, p.cur(s4t))]
                                if e_i == 0:
                                    wt_c.append((s4g, gate_mark[tc]))
                                    if grp > 0:
                                        wt_c.append(
                                            (s4s2, 8 * (grp - 1) + sl + 1))
                                    p.add("vector",
                                          (lambda tc=tc, pb=pb, sl=sl,
                                                  e_i=e_i:
                                           lambda e: e.tensor_scalar(
                                               acc[:, sl * D:(sl + 1) * D],
                                               ep_ps[pb][:, :],
                                               g_all[:, tc * 8 + e_i:
                                                     tc * 8 + e_i + 1],
                                               None, op0=ALU.mult))(),
                                          waits=wt_c, inc=s4c)
                                else:
                                    p.add("vector",
                                          (lambda tc=tc, pb=pb, sl=sl,
                                                  e_i=e_i:
                                           lambda e: e.scalar_tensor_tensor(
                                               acc[:, sl * D:(sl + 1) * D],
                                               ep_ps[pb][:, :],
                                               g_all[:, tc * 8 + e_i:
                                                     tc * 8 + e_i + 1],
                                               acc[:, sl * D:(sl + 1) * D],
                                               op0=ALU.mult,
                                               op1=ALU.add))(),
                                          waits=wt_c, inc=s4c)
                                slot_free[pb] = p.cur(s4c)
                                if ti == 0:
                                    marks[eli] = None
                                    if eli + 1 < len(order):
                                        sched_loads(eli + 1)
                            marks[eli] = p.cur(s4t)

                        while pend_gT:
                            flush_gT()
                        for tc in tcs:
                            sl = tc % 8
                            pb = tc % 3

                            def mk_bias(tc=tc, pb=pb):
                                def f(e):
                                    last = None
                                    for ot in range(2):
                                        ops = [(gT_h, eb_h), (gT_h, eb_l),
                                               (gT_l, eb_h)]
                                        for (pi, (a, b)) in enumerate(ops):
                                            last = e.matmul(
                                                ep_ps[pb][:, ot * 512:
                                                          ot * 512 + 512],
                                                a[0:8, tc * 128:
                                                  (tc + 1) * 128],
                                                b[0:8, ot * 512:
                                                  ot * 512 + 512],
                                                start=(pi == 0),
                                                stop=(pi == 2))
                                    return last
                                return f
                            p.add("tensor", mk_bias(),
                                  waits=[(s4wl, p.cur(s4wl)),
                                         (s4c, slot_free[pb])], inc=s4t)
                            p.add("vector",
                                  (lambda tc=tc, pb=pb, sl=sl:
                                   lambda e: e.tensor_add(
                                       acc[:, sl * D:(sl + 1) * D],
                                       ep_ps[pb][:, :],
                                       acc[:, sl * D:(sl + 1) * D]))(),
                                  waits=[(s4t, p.cur(s4t))], inc=s4c)
                            slot_free[pb] = p.cur(s4c)
                        experts_done = p.cur(s4c)

                        for tc in tcs:
                            pb = tc % 2
                            sl = tc % 8
                            wt_x = [(s4s2, p.cur(s4s2) - 1)] if tc >= 2 else []
                            p.dma("sync",
                                  (lambda tc=tc, pb=pb:
                                   lambda e: e.dma_start(
                                       h1r[pb][:, :],
                                       h1scr[tc * 128:(tc + 1) * 128, :]))(),
                                  waits=wt_x, inc=s4x[pb])
                            p.add("gpsimd",
                                  (lambda tc=tc, pb=pb, sl=sl:
                                   lambda e: e.tensor_add(
                                       s2b[pb][:, :],
                                       acc[:, sl * D:(sl + 1) * D],
                                       h1r[pb][:, :]))(),
                                  waits=[(s4x[pb], p.cur(s4x[pb])),
                                         (s4c, experts_done)], inc=s4s2)
                            wt_sq = [(s4s2, p.cur(s4s2))]
                            if p.cur(s4od[pb]) > 0:
                                wt_sq.append((s4od[pb], p.cur(s4od[pb])))
                            p.add("scalar",
                                  (lambda pb=pb: lambda e: e.activation(
                                      ob[pb][:, :], s2b[pb][:, :],
                                      AF.Square,
                                      accum_out=st4[:, 1:2]))(),
                                  waits=wt_sq, inc=s4sq)

                            def chain4(fn, waits=()):
                                p.add("vector", fn,
                                      waits=list(waits)
                                      + [(s4ch, p.cur(s4ch))], inc=s4ch)
                            chain4((lambda pb=pb: lambda e: e.reduce_sum(
                                st4[:, 0:1], s2b[pb][:, :], axis=AX.X))(),
                                waits=[(s4ri, p.cur(s4ri)),
                                       (s4sq, p.cur(s4sq)),
                                       (s4s2, p.cur(s4s2))])
                            chain4(lambda e: e.tensor_scalar(
                                st4[:, 2:3], st4[:, 0:1], c1024[:, :], None,
                                op0=ALU.mult))
                            chain4(lambda e: e.tensor_scalar(
                                st4[:, 3:4], st4[:, 1:2], c1024[:, :], None,
                                op0=ALU.mult))
                            chain4(lambda e: e.tensor_mul(
                                st4[:, 4:5], st4[:, 2:3], st4[:, 2:3]))
                            chain4(lambda e: e.tensor_sub(
                                st4[:, 5:6], st4[:, 3:4], st4[:, 4:5]))
                            chain4(lambda e: e.tensor_add(
                                st4[:, 6:7], st4[:, 5:6], ceps[:, :]))
                            p.add("vector",
                                  lambda e: e.reciprocal(st4[:, 7:8],
                                                         st4[:, 6:7]),
                                  waits=[(s4ch, p.cur(s4ch))], inc=s4st)
                            p.add("scalar",
                                  (lambda: lambda e: e.activation(
                                      st4[:, 6:7], st4[:, 7:8], AF.Sqrt))(),
                                  waits=[(s4st, p.cur(s4st))], inc=s4ri)
                            p.add("vector",
                                  (lambda pb=pb: lambda e: e.tensor_scalar(
                                      ob[pb][:, :], s2b[pb][:, :],
                                      st4[:, 2:3], st4[:, 6:7],
                                      op0=ALU.subtract, op1=ALU.mult))(),
                                  waits=[(s4ri, p.cur(s4ri))], inc=s4o)
                            p.dma("sync",
                                  (lambda tc=tc, pb=pb:
                                   lambda e: e.dma_start(
                                       out_d[tc * 128:(tc + 1) * 128, :],
                                       ob[pb][:, :]))(),
                                  waits=[(s4o, p.cur(s4o))], inc=s4od[pb])

                    final_od = (p.cur(s4od[0]), p.cur(s4od[1]))
                    with nc.Block() as blk:
                        p.emit(blk)
                    with nc.Block() as blk:
                        @blk.sync
                        def _(e):
                            e.wait_ge(s4od[0], final_od[0])
                            e.wait_ge(s4od[1], final_od[1])

    return nc


def kernel(**inputs):
    xs = np.ascontiguousarray(np.asarray(inputs["L0_attn_x"],
                                         dtype=np.float32))
    ws = {k: np.ascontiguousarray(np.asarray(v, dtype=np.float32))
          for (k, v) in inputs.items() if k != "L0_attn_x"}
    ident = np.eye(128, dtype=np.float32)
    in_maps = []
    for core in range(8):
        b, half = core // 2, core % 2
        in_maps.append(dict(
            x_kv=xs[b],
            x_q=np.ascontiguousarray(xs[b, half * SQ:(half + 1) * SQ]),
            wq=ws["L0_attn_Wq"], wk=ws["L0_attn_Wk"], wv=ws["L0_attn_Wv"],
            wr=ws["L0_attn_Wr"], wo=ws["L0_attn_Wo"],
            gw=ws["L0_ffn_gate_w"], gb=ws["L0_ffn_gate_b"],
            ew=ws["L0_ffn_expert_w"], eb=ws["L0_ffn_expert_b"],
            ident=ident,
        ))
    nc = build(dev=0)
    res = run_bass_kernel_spmd(nc, in_maps, core_ids=list(range(8)))
    out = np.empty((B, S, D), dtype=np.float32)
    for core in range(8):
        b, half = core // 2, core % 2
        out[b, half * SQ:(half + 1) * SQ] = res.results[core]["out"]
    return out


# version 11
# speedup vs baseline: 1.0635x; 1.0635x over previous
import math
import numpy as np
from contextlib import ExitStack

import concourse.bass as bass
import concourse.mybir as mybir
from concourse.bass_utils import run_bass_kernel_spmd

F32 = mybir.dt.float32
BF16 = mybir.dt.bfloat16
AF = mybir.ActivationFunctionType
ALU = mybir.AluOpType
AX = mybir.AxisListType

B, S, D = 4, 4096, 1024
H, DH, DR = 16, 64, 32
DC = DH + DR
SCALE = 1.0 / math.sqrt(DC)
E = 8
SQ = S // 2
NKV = S // 512
NQ = SQ // 512
NKC = S // 128
NTC = SQ // 128

ENGS = ("tensor", "scalar", "vector", "sync", "gpsimd")


class Sched:

    def __init__(self):
        self.ops = []
        self.cnt = {}

    def cur(self, sem):
        return self.cnt.get(id(sem), 0)

    def add(self, eng, fn, waits=(), inc=None, inc_k=1):
        ws = [(s, v) for (s, v) in waits if v > 0]
        self.ops.append((eng, ws, (inc, inc_k) if inc is not None else None, fn))
        if inc is not None:
            self.cnt[id(inc)] = self.cur(inc) + inc_k

    def dma(self, eng, fn, waits=(), inc=None):
        self.add(eng, fn, waits, inc, 16)

    def emit(self, block):
        def mk(eng_name):
            ops = [(w, i, f) for (e, w, i, f) in self.ops if e == eng_name]

            def run(eng):
                for (waits, inc, fn) in ops:
                    for (sem, val) in waits:
                        eng.wait_ge(sem, val)
                    instr = fn(eng)
                    if inc is not None:
                        assert instr is not None, "op with inc returned None"
                        instr.then_inc(inc[0], inc[1])
            return run

        for name in ENGS:
            if any(e == name for (e, _, _, _) in self.ops):
                getattr(block, name)(mk(name))


def build(dev=0, trunc=None):
    nc = bass.Bass("TRN2", target_bir_lowering=False)

    x_kv = nc.dram_tensor("x_kv", [S, D], F32, kind="ExternalInput")
    x_q = nc.dram_tensor("x_q", [SQ, D], F32, kind="ExternalInput")
    wq_d = nc.dram_tensor("wq", [D, D], F32, kind="ExternalInput")
    wk_d = nc.dram_tensor("wk", [D, D], F32, kind="ExternalInput")
    wv_d = nc.dram_tensor("wv", [D, D], F32, kind="ExternalInput")
    wr_d = nc.dram_tensor("wr", [D, H * DR], F32, kind="ExternalInput")
    wo_d = nc.dram_tensor("wo", [D, D], F32, kind="ExternalInput")
    gw_d = nc.dram_tensor("gw", [E, D], F32, kind="ExternalInput")
    gb_d = nc.dram_tensor("gb", [E], F32, kind="ExternalInput")
    ew_d = nc.dram_tensor("ew", [E, D, D], F32, kind="ExternalInput")
    eb_d = nc.dram_tensor("eb", [E, D], F32, kind="ExternalInput")
    id_d = nc.dram_tensor("ident", [128, 128], F32, kind="ExternalInput")
    out_d = nc.dram_tensor("out", [SQ, D], F32, kind="ExternalOutput")

    skw = dict(kind="ExternalOutput") if dev == 1 else {}
    kscr = nc.dram_tensor("kscr", [H, 2, 97, S], BF16, **skw)
    qscr = nc.dram_tensor("qscr", [H, 2, 97, SQ], BF16, **skw)
    vscr = nc.dram_tensor("vscr", [2, 128, H, NKC * 65], BF16, **skw)
    h1kw = dict(kind="ExternalOutput") if dev == 3 else {}
    h1scr = nc.dram_tensor("h1scr", [SQ, D], F32, **h1kw)
    if dev == 2:
        attn_dbg = nc.dram_tensor("attn_dbg", [2, 128, 8 * SQ], BF16,
                                  kind="ExternalOutput")

    with ExitStack() as g:
        def gsb(name, shape, dt=F32, st=None):
            return (st or g).enter_context(nc.sbuf_tensor(name, shape, dt))

        def gps(name, shape, st=None):
            return (st or g).enter_context(nc.psum_tensor(name, shape, F32))

        def gsem(name, st=None):
            return (st or g).enter_context(nc.semaphore(name))

        id_sb = gsb("id_sb", [128, 128])
        onesb = gsb("onesb", [128, 128], BF16)
        onesf = gsb("onesf", [128, 64])
        c1024 = gsb("c1024", [128, 1])
        ceps = gsb("ceps", [128, 1])
        czero = gsb("czero", [128, 1])
        cn30 = gsb("cn30", [128, 1])

        sd = gsem("sd")
        sv0 = gsem("sv0")
        actw = gsb("actw", [128, 4])
        with nc.Block() as blk:
            @blk.sync
            def _(sy):
                sy.dma_start(id_sb[:, :], id_d[:, :]).then_inc(sd, 16)
                sy.wait_ge(sd, 16)

            @blk.vector
            def _(v):
                v.memset(onesb[:, :], 1.0)
                v.memset(onesf[:, :], 1.0)
                v.memset(c1024[:, :], 1.0 / 1024.0)
                v.memset(ceps[:, :], 1e-5)
                v.memset(czero[:, :], 0.0)
                v.memset(cn30[:, :], -1e30).then_inc(sv0, 1)

            @blk.scalar
            def _(sc):
                sc.wait_ge(sv0, 1)
                sc.activation(actw[:, 0:1], c1024[:, :], AF.Square)
                sc.activation(actw[:, 1:2], c1024[:, :], AF.Exp)
                sc.activation(actw[:, 2:3], c1024[:, :], AF.Sqrt)

        with ExitStack() as p1s:
            wa_h = gsb("wa_h", [128, 8 * D], BF16, p1s)
            wa_l = gsb("wa_l", [128, 8 * D], BF16, p1s)
            wv_h = gsb("wv_h", [128, 8 * D], BF16, p1s)
            wv_l = gsb("wv_l", [128, 8 * D], BF16, p1s)
            wr_h = gsb("wr_h", [128, 8 * 512], BF16, p1s)
            wr_l = gsb("wr_l", [128, 8 * 512], BF16, p1s)
            wst = [gsb(f"wst{i}", [128, D], F32, p1s) for i in range(2)]
            x_tok = [gsb(f"xt{i}", [128, D], F32, p1s) for i in range(4)]
            xfm_h = [gsb(f"xfm_h{i}", [128, 8 * 512], BF16, p1s)
                     for i in range(2)]
            xfm_l = [gsb(f"xfm_l{i}", [128, 8 * 512], BF16, p1s)
                     for i in range(2)]
            ev = [gsb(f"ev{i}", [128, 1024], BF16, p1s) for i in range(4)]
            sqb = [gsb(f"sqb{i}", [128, 512], BF16, p1s) for i in range(4)]
            vst_h = gsb("vst_h", [128, H * 260], BF16, p1s)
            vst_l = gsb("vst_l", [128, H * 260], BF16, p1s)
            emat = gsb("emat", [128, 12 * 16], BF16, p1s)
            nrm2k = gsb("nrm2k", [16, S], F32, p1s)
            nrm2q = gsb("nrm2q", [16, SQ], F32, p1s)
            rowb = gsb("rowb", [16, S], BF16, p1s)
            chib = gsb("chib", [16, SQ], BF16, p1s)
            mkt = gsb("mkt", [16, 1], F32, p1s)

            tp_ps = [gps(f"tpp{i}", [128, 128], p1s) for i in range(4)]
            pj_ps = [gps(f"pjp{i}", [128, 512], p1s) for i in range(2)]
            nr_ps = gps("nrp", [16, 512], p1s)

            s_wld = [gsem("s_wld0", p1s), gsem("s_wld1", p1s)]
            s_wh = gsem("s_wh", p1s)
            s_wl = gsem("s_wl", p1s)
            s_x = [gsem(f"s_x{i}", p1s) for i in range(4)]
            s_tp = gsem("s_tp", p1s)
            s_xh = gsem("s_xh", p1s)
            s_xl = gsem("s_xl", p1s)
            s_pj = gsem("s_pj", p1s)
            s_eh = gsem("s_eh", p1s)
            s_el = gsem("s_el", p1s)
            s_sq = gsem("s_sq", p1s)
            s_nm = gsem("s_nm", p1s)
            s_ne = gsem("s_ne", p1s)
            s_scw = [gsem(f"s_scw{i}", p1s) for i in range(4)]
            s_vscw = gsem("s_vscw", p1s)
            s_vh = gsem("s_vh", p1s)
            s_vl = gsem("s_vl", p1s)
            s_em = gsem("s_em", p1s)

            st = {"el_hist": [], "sq_hist": [], "nm_hist": [],
                  "scw_hist": [], "vst_dma": 0, "pj_tile": {}}
            pend_nrm = []

            def p1_phase(p, q_pass):
                ntile = NQ if q_pass else NKV
                xsrc = x_q if q_pass else x_kv
                scr_main = qscr if q_pass else kscr
                nrm_dst = nrm2q if q_pass else nrm2k

                wl = ([(wq_d, D, wa_h, wa_l)] if q_pass else
                      [(wk_d, D, wa_h, wa_l), (wv_d, D, wv_h, wv_l),
                       (wr_d, 512, wr_h, wr_l)])
                kv_chunks_total = p.cur(s_pj)
                for (wd, ncol, whi, wlo) in wl:
                    for ci in range(8):
                        n = st.setdefault("n_w", 0)
                        st["n_w"] = n + 1
                        wt = [(s_wl, p.cur(s_wl) - 1)] if n >= 2 else []
                        p.dma("gpsimd",
                              (lambda wd=wd, ncol=ncol, ci=ci, n=n:
                               lambda e: e.dma_start(
                                   wst[n % 2][:, 0:ncol],
                                   wd[ci * 128:(ci + 1) * 128, 0:ncol]))(),
                              waits=wt, inc=s_wld[n % 2])
                        wt_h = [(s_wld[n % 2], p.cur(s_wld[n % 2]))]
                        if q_pass and ci == 0:
                            wt_h.append((s_pj, kv_chunks_total))
                        p.add("scalar",
                              (lambda whi=whi, ncol=ncol, ci=ci, n=n:
                               lambda e: e.copy(
                                   whi[:, ci * ncol:(ci + 1) * ncol],
                                   wst[n % 2][:, 0:ncol]))(),
                              waits=wt_h, inc=s_wh)
                        p.add("vector",
                              (lambda whi=whi, wlo=wlo, ncol=ncol, ci=ci, n=n:
                               lambda e: e.tensor_sub(
                                   wlo[:, ci * ncol:(ci + 1) * ncol],
                                   wst[n % 2][:, 0:ncol],
                                   whi[:, ci * ncol:(ci + 1) * ncol]))(),
                              waits=[(s_wh, p.cur(s_wh))], inc=s_wl)
                wl_total = p.cur(s_wl)

                if not q_pass:
                    p.add("vector", lambda e: e.memset(emat[:, :], 0.0),
                          inc=s_em)

                    def setup(e):
                        for j in range(8):
                            e.memset(emat[0:64, j * 16 + 2 * j:
                                          j * 16 + 2 * j + 1], 1.0)
                            e.memset(emat[64:128, j * 16 + 2 * j + 1:
                                          j * 16 + 2 * j + 2], 1.0)
                        for j in range(4):
                            for rb in range(4):
                                hh = 4 * j + rb
                                e.memset(emat[32 * rb:32 * rb + 32,
                                              (8 + j) * 16 + hh:
                                              (8 + j) * 16 + hh + 1], 1.0)
                        vh3 = vst_h[:, :].rearrange("p (h c) -> p h c",
                                                    h=H * 4, c=65)
                        vl3 = vst_l[:, :].rearrange("p (h c) -> p h c",
                                                    h=H * 4, c=65)
                        e.memset(vh3[:, :, 64:65], 1.0)
                        return e.memset(vl3[:, :, 64:65], 0.0)
                    p.add("vector", setup, waits=[(s_em, 1)], inc=s_vl)

                if q_pass:
                    chunks = [("q", co) for co in range(8)] + \
                             [("r", co) for co in range(4)]
                else:
                    chunks = [("k", co) for co in range(8)] + \
                             [("r", co) for co in range(4)] + \
                             [("v", tch, oc) for tch in range(4)
                              for oc in range(2)]

                for tt in range(ntile):
                    gt = st.setdefault("gtile", 0)
                    st["gtile"] = gt + 1
                    xp = gt % 2
                    for tch in range(4):
                        xc = st.setdefault("xc", 0)
                        st["xc"] = xc + 1
                        wt = ([(s_tp, p.cur(s_tp) - 24)]
                              if p.cur(s_tp) >= 32 else [])
                        p.dma("sync",
                              (lambda xc=xc, tt=tt, tch=tch, xsrc=xsrc:
                               lambda e: e.dma_start(
                                   x_tok[xc % 4][:, :],
                                   xsrc[(tt * 4 + tch) * 128:
                                        (tt * 4 + tch + 1) * 128, :]))(),
                              waits=wt, inc=s_x[xc % 4])
                        for dc in range(8):
                            gi = p.cur(s_tp)
                            wt_t = [(s_x[xc % 4], p.cur(s_x[xc % 4]))]
                            if gi >= 4:
                                wt_t.append((s_xl, gi - 3))
                            p.add("tensor",
                                  (lambda gi=gi, xc=xc, dc=dc:
                                   lambda e: e.transpose(
                                       tp_ps[gi % 4][:, :],
                                       x_tok[xc % 4][:, dc * 128:dc * 128 + 128],
                                       id_sb[:, :]))(),
                                  waits=wt_t, inc=s_tp)
                            wt_s = [(s_tp, gi + 1)]
                            if tch == 0 and dc == 0:
                                if gt >= 2:
                                    wt_s.append(
                                        (s_pj, st["pj_tile"][gt - 2]))
                                wt_s.append((s_wl, wl_total))
                            col = dc * 512 + tch * 128
                            p.add("scalar",
                                  (lambda gi=gi, col=col, xp=xp:
                                   lambda e: e.copy(
                                       xfm_h[xp][:, col:col + 128],
                                       tp_ps[gi % 4][:, :]))(),
                                  waits=wt_s, inc=s_xh)
                            p.add("vector",
                                  (lambda gi=gi, col=col, xp=xp:
                                   lambda e: e.tensor_sub(
                                       xfm_l[xp][:, col:col + 128],
                                       tp_ps[gi % 4][:, :],
                                       xfm_h[xp][:, col:col + 128]))(),
                                  waits=[(s_xh, gi + 1)], inc=s_xl)

                    for (ic, c) in enumerate(chunks):
                        kind = c[0]
                        chc = p.cur(s_pj)
                        wt_m = [(s_xl, p.cur(s_xl)), (s_wl, wl_total)]
                        if chc >= 2:
                            wt_m += st["el_hist"][chc - 2]

                        def mk_proj(kind=kind, c=c, chc=chc, xp=xp):
                            def f(e):
                                last = None
                                for ci in range(8):
                                    xh_ = xfm_h[xp][:, ci * 512:(ci + 1) * 512]
                                    xl_ = xfm_l[xp][:, ci * 512:(ci + 1) * 512]
                                    if kind in ("k", "q"):
                                        co = c[1]
                                        lh = wa_h[:, ci * D + co * 128:
                                                  ci * D + co * 128 + 128]
                                        ll = wa_l[:, ci * D + co * 128:
                                                  ci * D + co * 128 + 128]
                                        rh, rl = xh_, xl_
                                    elif kind == "r":
                                        co = c[1]
                                        lh = wr_h[:, ci * 512 + co * 128:
                                                  ci * 512 + co * 128 + 128]
                                        ll = wr_l[:, ci * 512 + co * 128:
                                                  ci * 512 + co * 128 + 128]
                                        rh, rl = xh_, xl_
                                    else:
                                        _, tch, oc = c
                                        lh = xfm_h[xp][
                                            :, ci * 512 + tch * 128:
                                            ci * 512 + tch * 128 + 128]
                                        ll = xfm_l[xp][
                                            :, ci * 512 + tch * 128:
                                            ci * 512 + tch * 128 + 128]
                                        rh = wv_h[:, ci * D + oc * 512:
                                                  ci * D + oc * 512 + 512]
                                        rl = wv_l[:, ci * D + oc * 512:
                                                  ci * D + oc * 512 + 512]
                                    for (pi, (a, b)) in enumerate(
                                            [(lh, rh), (lh, rl), (ll, rh)]):
                                        last = e.matmul(
                                            pj_ps[chc % 2][:, :], a, b,
                                            start=(ci == 0 and pi == 0),
                                            stop=(ci == 7 and pi == 2))
                                return last
                            return f
                        p.add("tensor", mk_proj(), waits=wt_m, inc=s_pj)
                        while len(pend_nrm) > 1:
                            (fnrm, wnrm) = pend_nrm.pop(0)
                            p.add("tensor", fnrm, waits=wnrm, inc=s_nm)

                        if kind != "v":
                            evc = p.cur(s_eh)
                            wt_h = [(s_pj, chc + 1)]
                            if evc >= 4:
                                wt_h.append(st["scw_hist"][evc - 4])
                            p.add("scalar",
                                  (lambda evc=evc, chc=chc:
                                   lambda e: e.copy(ev[evc % 4][:, 0:512],
                                                    pj_ps[chc % 2][:, :]))(),
                                  waits=wt_h, inc=s_eh)
                            p.add("vector",
                                  (lambda evc=evc, chc=chc:
                                   lambda e: e.tensor_sub(
                                       ev[evc % 4][:, 512:1024],
                                       pj_ps[chc % 2][:, :],
                                       ev[evc % 4][:, 0:512]))(),
                                  waits=[(s_eh, p.cur(s_eh))], inc=s_el)
                            sqc = p.cur(s_sq)
                            wt_q = [(s_pj, chc + 1), (s_el, p.cur(s_el))]
                            if sqc >= 4:
                                wt_q.append(st["nm_hist"][sqc - 4])
                            p.add("scalar",
                                  (lambda sqc=sqc, chc=chc:
                                   lambda e: e.activation(
                                       sqb[sqc % 4][:, :],
                                       pj_ps[chc % 2][:, :], AF.Square))(),
                                  waits=wt_q, inc=s_sq)
                            co = c[1]
                            ei = co if kind in ("k", "q") else 8 + co
                            wt_n = [(s_sq, sqc + 1), (s_vl, 1)]
                            if ic == 0:
                                wt_n.append((s_ne, p.cur(s_ne)))
                            pend_nrm.append(
                                ((lambda sqc=sqc, ei=ei, nf=(ic == 0),
                                          nl=(ic == 11):
                                  lambda e: e.matmul(
                                      nr_ps[:, :],
                                      emat[:, ei * 16:ei * 16 + 16],
                                      sqb[sqc % 4][:, :],
                                      start=nf, stop=nl))(), wt_n))
                            st["nm_hist"].append((s_nm, p.cur(s_nm) + 1))
                            SS = SQ if q_pass else S
                            dmas = []
                            if kind in ("k", "q"):
                                for hh in (2 * co, 2 * co + 1):
                                    r0 = 0 if hh % 2 == 0 else 64
                                    dst = bass.AP(
                                        scr_main, hh * 2 * 97 * SS + tt * 512,
                                        [[SS, 64], [97 * SS, 2], [1, 512]])
                                    dmas.append(
                                        (dst, ev[evc % 4][r0:r0 + 64, :]))
                            else:
                                for rb in range(4):
                                    hh = 4 * co + rb
                                    dst = bass.AP(
                                        scr_main,
                                        hh * 2 * 97 * SS + 64 * SS + tt * 512,
                                        [[SS, 32], [97 * SS, 2], [1, 512]])
                                    dmas.append(
                                        (dst,
                                         ev[evc % 4][32 * rb:32 * rb + 32, :]))
                            for (j, (dst, srcap)) in enumerate(dmas):
                                p.dma("gpsimd",
                                      (lambda dst=dst, srcap=srcap:
                                       lambda e: e.dma_start(dst, srcap))(),
                                      waits=([(s_el, evc + 1)]
                                             if j == 0 else []),
                                      inc=s_scw[evc % 4])
                            st["scw_hist"].append(
                                (s_scw[evc % 4], p.cur(s_scw[evc % 4])))
                            st["el_hist"].append([(s_el, p.cur(s_el)),
                                                  (s_sq, p.cur(s_sq))])
                        else:
                            _, tch, oc = c
                            wt_vh = [(s_pj, chc + 1)]
                            if tch == 0 and oc == 0 and st["vst_dma"] > 0:
                                wt_vh.append((s_vscw, st["vst_dma"]))

                            def mk_vh(tch=tch, oc=oc, chc=chc):
                                def f(e):
                                    last = None
                                    for i in range(8):
                                        cl = (oc * 8 + i) * 260 + tch * 65
                                        last = e.copy(
                                            vst_h[:, cl:cl + 64],
                                            pj_ps[chc % 2][:, i * 64:i * 64 + 64])
                                    return last
                                return f
                            p.add("scalar", mk_vh(), waits=wt_vh, inc=s_vh)

                            def mk_vl(tch=tch, oc=oc, chc=chc):
                                def f(e):
                                    last = None
                                    for i in range(8):
                                        cl = (oc * 8 + i) * 260 + tch * 65
                                        last = e.tensor_sub(
                                            vst_l[:, cl:cl + 64],
                                            pj_ps[chc % 2][:, i * 64:i * 64 + 64],
                                            vst_h[:, cl:cl + 64])
                                    return last
                                return f
                            p.add("vector", mk_vl(),
                                  waits=[(s_vh, p.cur(s_vh))], inc=s_vl)
                            st["el_hist"].append([(s_vl, p.cur(s_vl))])

                    st.setdefault("pj_tile", {})[gt] = p.cur(s_pj)
                    while pend_nrm:
                        (fnrm, wnrm) = pend_nrm.pop(0)
                        p.add("tensor", fnrm, waits=wnrm, inc=s_nm)
                    p.add("vector",
                          (lambda tt=tt, nrm_dst=nrm_dst:
                           lambda e: e.tensor_copy(
                               nrm_dst[:, tt * 512:(tt + 1) * 512],
                               nr_ps[:, :]))(),
                          waits=[(s_nm, p.cur(s_nm))], inc=s_ne)
                    if not q_pass:
                        for (hl, srct) in ((0, vst_h), (1, vst_l)):
                            p.dma("gpsimd",
                                  (lambda hl=hl, srct=srct, tt=tt:
                                   lambda e: e.dma_start(
                                       vscr[hl, :, :,
                                            tt * 260:(tt + 1) * 260],
                                       srct[:, :].rearrange(
                                           "p (h c) -> p h c", h=H, c=260)))(),
                                  waits=([(s_vl, p.cur(s_vl))]
                                         if hl == 0 else []),
                                  inc=s_vscw)
                        st["vst_dma"] = p.cur(s_vscw)

            p = Sched()
            p1_phase(p, False)
            p1_phase(p, True)
            fin = ([(s_scw[i], p.cur(s_scw[i])) for i in range(4)]
                   + [(s_vscw, p.cur(s_vscw))])
            p.add("sync", lambda e: None, waits=fin)
            if trunc is not None:
                p.ops = p.ops[:trunc]
            with nc.Block() as blk:
                p.emit(blk)
            if trunc is not None:
                return nc

            sp1 = gsem("sp1", p1s)
            sv1 = gsem("sv1", p1s)
            with nc.Block() as blk:
                @blk.scalar
                def _(e):
                    e.activation(nrm2k[:, :], nrm2k[:, :],
                                 AF.Sqrt).then_inc(sp1, 1)
                    e.activation(nrm2q[:, :], nrm2q[:, :],
                                 AF.Sqrt).then_inc(sp1, 1)

                @blk.vector
                def _(e):
                    e.wait_ge(sp1, 1)
                    e.reduce_max(mkt[:, :], nrm2k[:, :],
                                 axis=AX.X).then_inc(sv1, 1)
                    e.wait_ge(sp1, 2)
                    e.wait_ge(sv1, 1)
                    e.tensor_scalar(nrm2q[:, :], nrm2q[:, :], mkt[:, :], None,
                                    op0=ALU.mult).then_inc(sv1, 1)
                    e.wait_ge(sv1, 2)
                    e.tensor_copy(chib[:, :], nrm2q[:, :]).then_inc(sv1, 1)
                    e.memset(rowb[:, :], -1.0).then_inc(sv1, 1)

                @blk.sync
                def _(e):
                    e.wait_ge(sv1, 4)
                    e.dma_start(qscr[:, 0, 96, :], chib[:, :]).then_inc(sd, 16)
                    e.dma_start(kscr[:, 0, 96, :], rowb[:, :]).then_inc(sd, 16)
                    e.wait_ge(sd, 48)
            with nc.Block() as blk:
                @blk.vector
                def _(e):
                    e.memset(rowb[:, :], 0.0).then_inc(sv1, 1)

                @blk.sync
                def _(e):
                    e.wait_ge(sv1, 5)
                    e.dma_start(kscr[:, 1, 96, :], rowb[:, :]).then_inc(sd, 16)
                    e.dma_start(qscr[:, 1, 96, :],
                                rowb[:, 0:SQ]).then_inc(sd, 16)
                    e.wait_ge(sd, 80)

        if dev == 1:
            return nc

        with ExitStack() as p23:
            at_h = gsb("at_h", [128, 8 * SQ], BF16, p23)
            at_l = gsb("at_l", [128, 8 * SQ], BF16, p23)
            with ExitStack() as p2s:
                k2 = [gsb(f"k2_{i}", [128, 2 * S], BF16, p2s)
                      for i in range(2)]
                q2 = [gsb(f"q2_{i}", [128, 2 * SQ], BF16, p2s)
                      for i in range(2)]
                v2 = [gsb(f"v2_{i}", [128, 2 * NKC * 65], BF16, p2s)
                      for i in range(2)]
                p32 = [gsb(f"p32_{i}", [128, 2048], F32, p2s)
                       for i in range(2)]
                p_h = [gsb(f"p2h{i}", [128, 2048], BF16, p2s)
                       for i in range(2)]
                p_l = [gsb(f"p2l{i}", [128, 2048], BF16, p2s)
                       for i in range(2)]
                linv = gsb("linv", [128, 2048], F32, p2s)
                bc_sb = gsb("bc_sb", [128, 512], F32, p2s)
                t32 = gsb("t32", [128, 512], F32, p2s)
                hi0 = gsb("hi0", [128, 512], BF16, p2s)
                lo0 = gsb("lo0", [128, 512], BF16, p2s)

                sc_ps = [gps(f"scp{i}", [128, 1024], p2s) for i in range(2)]
                av_ps = [gps(f"avp{i}", [128, 512], p2s) for i in range(4)]

                s2d = [gsem("s2d0", p2s), gsem("s2d1", p2s)]
                s2sa = gsem("s2sa", p2s)
                s2sb = gsem("s2sb", p2s)
                s2ea = gsem("s2ea", p2s)
                s2eb = gsem("s2eb", p2s)
                s2pA = gsem("s2pA", p2s)
                s2pB = gsem("s2pB", p2s)
                s2av = gsem("s2av", p2s)
                s2li = gsem("s2li", p2s)
                s2bc = gsem("s2bc", p2s)
                s2ev = gsem("s2ev", p2s)
                s2ch = gsem("s2ch", p2s)
                s2bcc = gsem("s2bcc", p2s)
                s2hc = gsem("s2hc", p2s)

                p = Sched()

                def head_load(hh):
                    pb = hh % 2
                    wt = [(s2av, NKC * (hh - 1))] if hh >= 2 else []
                    ksrc = bass.AP(kscr, hh * 2 * 97 * S,
                                   [[S, 97], [97 * S, 2], [1, S]])
                    qsrc = bass.AP(qscr, hh * 2 * 97 * SQ,
                                   [[SQ, 97], [97 * SQ, 2], [1, SQ]])
                    vsrc = bass.AP(vscr, hh * NKC * 65,
                                   [[H * NKC * 65, 128],
                                    [128 * H * NKC * 65, 2], [1, NKC * 65]])
                    loads = [(k2[pb][0:97, :], ksrc),
                             (q2[pb][0:97, :], qsrc),
                             (v2[pb][:, :], vsrc)]
                    for (j, (dst, srcap)) in enumerate(loads):
                        p.dma("sync",
                              (lambda dst=dst, srcap=srcap:
                               lambda e: e.dma_start(dst, srcap))(),
                              waits=(wt if j == 0 else []), inc=s2d[pb])

                def sched_evac_recips(hh):
                    for qt in range(4):
                        p.add("vector",
                              (lambda qt=qt: lambda e: e.reciprocal(
                                  linv[64:65, qt * 512:qt * 512 + 512],
                                  av_ps[qt][64:65, :]))(),
                              waits=[(s2av, p.cur(s2av))], inc=s2li)

                def sched_evac(hh):
                    rows0 = (hh % 2) * 64
                    chj = hh // 2
                    li_base = p.cur(s2li)
                    for qt in range(4):
                        wt_b = [(s2li, li_base - 3 + qt)]
                        p.add("tensor",
                              (lambda qt=qt:
                               lambda e: e.matmul(
                                   av_ps[qt][64:128, :],
                                   onesf[64:65, 0:64],
                                   linv[64:65, qt * 512:qt * 512 + 512],
                                   start=True, stop=True))(),
                              waits=wt_b, inc=s2bc)
                        rows0 = (hh % 2) * 64
                        chj = hh // 2
                        cc = chj * SQ + qt * 512
                        p.add("vector",
                              (lambda qt=qt:
                               lambda e: e.tensor_copy(
                                   bc_sb[0:64, :],
                                   av_ps[qt][64:128, :]))(),
                              waits=[(s2bc, p.cur(s2bc)),
                                     (s2ch, p.cur(s2ch))], inc=s2bcc)
                        p.add("vector",
                              (lambda qt=qt: lambda e: e.tensor_mul(
                                  t32[0:64, :], av_ps[qt][0:64, :],
                                  bc_sb[0:64, :]))(),
                              waits=[(s2bcc, p.cur(s2bcc))], inc=s2ch)
                        p.add("vector",
                              (lambda: lambda e: e.tensor_copy(
                                  hi0[0:64, :], t32[0:64, :]))(),
                              waits=[(s2ch, p.cur(s2ch))], inc=s2ch)
                        p.add("vector",
                              (lambda: lambda e: e.tensor_sub(
                                  lo0[0:64, :], t32[0:64, :],
                                  hi0[0:64, :]))(),
                              waits=[(s2ch, p.cur(s2ch)),
                                     (s2ev, p.cur(s2ev))], inc=s2ch)
                        p.add("vector",
                              (lambda cc=cc, rows0=rows0:
                               lambda e: e.tensor_copy(
                                   at_h[rows0:rows0 + 64, cc:cc + 512],
                                   hi0[0:64, :]))(),
                              waits=[(s2ch, p.cur(s2ch))], inc=s2ch)
                        p.add("vector",
                              (lambda cc=cc, rows0=rows0:
                               lambda e: e.tensor_copy(
                                   at_l[rows0:rows0 + 64, cc:cc + 512],
                                   lo0[0:64, :]))(),
                              waits=[(s2ch, p.cur(s2ch))], inc=s2ev)


                head_load(0)
                for hh in range(H):
                    pb = hh % 2
                    d2val = p.cur(s2d[pb])
                    if hh > 0:
                        sched_evac_recips(hh - 1)
                    if hh + 1 < H:
                        head_load(hh + 1)

                    def mk_av(kc, pp, qts, pb=pb):
                        def f(e):
                            last = None
                            VL = NKC * 65
                            for qt in qts:
                                ops = [(0, p_h[pp]), (0, p_l[pp]),
                                       (1, p_h[pp])]
                                for (pi, (va, b)) in enumerate(ops):
                                    last = e.matmul(
                                        av_ps[qt][0:65, :],
                                        v2[pb][:, va * VL + kc * 65:
                                               va * VL + kc * 65 + 65],
                                        b[:, qt * 512:qt * 512 + 512],
                                        start=(kc == 0 and pi == 0),
                                        stop=(kc == NKC - 1 and pi == 2))
                            return last
                        return f

                    p_marks = {}
                    for kc in range(NKC):
                        pp = kc % 2
                        for (half, sps, ssem, esem) in (
                                (0, sc_ps[0], s2sa, s2ea),
                                (1, sc_ps[1], s2sb, s2eb)):
                            wt = [(s2d[pb], d2val), (esem, p.cur(esem))]

                            def mk_sc(kc=kc, half=half, sps=sps, pb=pb):
                                def f(e):
                                    last = None
                                    for qi in range(2):
                                        qt = half * 2 + qi
                                        ops = [(0, 0), (0, 1), (1, 0)]
                                        for (pi, (ka, qa)) in enumerate(ops):
                                            last = e.matmul(
                                                sps[:, qi * 512:qi * 512 + 512],
                                                k2[pb][0:97,
                                                       ka * S + kc * 128:
                                                       ka * S + kc * 128 + 128],
                                                q2[pb][0:97,
                                                       qa * SQ + qt * 512:
                                                       qa * SQ + qt * 512 + 512],
                                                start=(pi == 0), stop=(pi == 2))
                                    return last
                                return f
                            p.add("tensor", mk_sc(), waits=wt, inc=ssem)
                            wt_e = [(ssem, p.cur(ssem))]
                            phs = s2pA if half == 0 else s2pB
                            if p.cur(phs) >= 2:
                                wt_e.append((phs, p.cur(phs) - 1))
                            p.add("scalar",
                                  (lambda pp=pp, half=half, sps=sps:
                                   lambda e: e.activation(
                                       p32[pp][:, half * 1024:
                                               half * 1024 + 1024],
                                       sps[:, :], AF.Exp, scale=SCALE))(),
                                  waits=wt_e, inc=esem)
                        wt_v = [(s2ea, p.cur(s2ea))]
                        if p.cur(s2av) >= 1:
                            wt_v.append((s2av, p.cur(s2av)))
                        p.add("vector",
                              (lambda pp=pp: lambda e: e.tensor_copy(
                                  p_h[pp][:, 0:1024], p32[pp][:, 0:1024]))(),
                              waits=wt_v, inc=s2ch)
                        p.add("vector",
                              (lambda pp=pp: lambda e: e.tensor_sub(
                                  p_l[pp][:, 0:1024], p32[pp][:, 0:1024],
                                  p_h[pp][:, 0:1024]))(),
                              waits=[(s2ch, p.cur(s2ch))], inc=s2pA)
                        p.add("vector",
                              (lambda pp=pp: lambda e: e.tensor_copy(
                                  p_h[pp][:, 1024:2048],
                                  p32[pp][:, 1024:2048]))(),
                              waits=[(s2eb, p.cur(s2eb))], inc=s2ch)
                        p.add("vector",
                              (lambda pp=pp: lambda e: e.tensor_sub(
                                  p_l[pp][:, 1024:2048],
                                  p32[pp][:, 1024:2048],
                                  p_h[pp][:, 1024:2048]))(),
                              waits=[(s2ch, p.cur(s2ch))], inc=s2pB)
                        p_marks[kc] = (p.cur(s2pA), p.cur(s2pB))
                        if kc >= 1:
                            if kc == 1 and hh > 0:
                                sched_evac(hh - 1)
                            wt_av = [(s2pA, p_marks[kc - 1][0])]
                            if kc == 1 and hh > 0:
                                wt_av.append((s2ev, 4 * hh))
                            p.add("tensor",
                                  mk_av(kc - 1, (kc - 1) % 2, (0, 1)),
                                  waits=wt_av)
                            p.add("tensor",
                                  mk_av(kc - 1, (kc - 1) % 2, (2, 3)),
                                  waits=[(s2pB, p_marks[kc - 1][1])],
                                  inc=s2av)
                    p.add("tensor", mk_av(NKC - 1, (NKC - 1) % 2, (0, 1)),
                          waits=[(s2pA, p_marks[NKC - 1][0])])
                    p.add("tensor", mk_av(NKC - 1, (NKC - 1) % 2, (2, 3)),
                          waits=[(s2pB, p_marks[NKC - 1][1])], inc=s2av)
                    if hh == H - 1:
                        sched_evac_recips(hh)
                        sched_evac(hh)

                with nc.Block() as blk:
                    p.emit(blk)

                if dev == 2:
                    with nc.Block() as blk:
                        @blk.sync
                        def _(e):
                            e.dma_start(attn_dbg[0, :, :],
                                        at_h[:, :]).then_inc(sd, 16)
                            e.dma_start(attn_dbg[1, :, :],
                                        at_l[:, :]).then_inc(sd, 16)
                            e.wait_ge(sd, 80 + 32)
                    return nc

            h1f_h, h1f_l = at_h, at_l
            if True:
                with ExitStack() as p3s:
                    wo_h = gsb("wo_h", [128, 8 * D], BF16, p3s)
                    wo_l = gsb("wo_l", [128, 8 * D], BF16, p3s)
                    wst3 = [gsb(f"wst3{i}", [128, D], F32, p3s)
                            for i in range(2)]
                    xq_sb = [gsb(f"xq{i}", [128, D], F32, p3s)
                             for i in range(2)]
                    s1b = [gsb(f"s1b{i}", [128, D], F32, p3s)
                           for i in range(2)]
                    sq3 = [gsb(f"sq3{i}", [128, D], F32, p3s)
                           for i in range(2)]
                    h1b = [gsb(f"h1b{i}", [128, D], F32, p3s)
                           for i in range(2)]
                    st3 = gsb("st3", [128, 8], F32, p3s)

                    y_ps = [gps(f"yp{i}", [128, 1024], p3s) for i in range(2)]
                    t3_ps = [gps(f"t3p{i}", [128, 128], p3s) for i in range(4)]

                    s3w = [gsem("s3w0", p3s), gsem("s3w1", p3s)]
                    s3wh = gsem("s3wh", p3s)
                    s3wl = gsem("s3wl", p3s)
                    s3x = [gsem("s3x0", p3s), gsem("s3x1", p3s)]
                    s3t = gsem("s3t", p3s)
                    s3s1 = gsem("s3s1", p3s)
                    s3sq = gsem("s3sq", p3s)
                    s3st = gsem("s3st", p3s)
                    s3ch = gsem("s3ch", p3s)
                    s3ri = gsem("s3ri", p3s)
                    s3h1 = gsem("s3h1", p3s)
                    s3d = [gsem("s3d0", p3s), gsem("s3d1", p3s)]
                    s3tp = gsem("s3tp", p3s)
                    s3fh = gsem("s3fh", p3s)
                    s3fl = gsem("s3fl", p3s)

                    p = Sched()
                    pend_tp3 = []

                    def flush_tp3():
                        while pend_tp3:
                            (f3, w3, kind3, inc3) = pend_tp3.pop(0)
                            p.add(kind3, f3, waits=w3, inc=inc3)
                    for n in range(8):
                        wt = [(s3wl, n - 1)] if n >= 2 else []
                        p.dma("gpsimd",
                              (lambda n=n: lambda e: e.dma_start(
                                  wst3[n % 2][:, :],
                                  wo_d[n * 128:(n + 1) * 128, :]))(),
                              waits=wt, inc=s3w[n % 2])
                        p.add("scalar",
                              (lambda n=n: lambda e: e.copy(
                                  wo_h[:, n * D:(n + 1) * D],
                                  wst3[n % 2][:, :]))(),
                              waits=[(s3w[n % 2], p.cur(s3w[n % 2]))],
                              inc=s3wh)
                        p.add("vector",
                              (lambda n=n: lambda e: e.tensor_sub(
                                  wo_l[:, n * D:(n + 1) * D],
                                  wst3[n % 2][:, :],
                                  wo_h[:, n * D:(n + 1) * D]))(),
                              waits=[(s3wh, p.cur(s3wh))], inc=s3wl)
                    wosplit = p.cur(s3wl)

                    for tc in range(NTC):
                        pb = tc % 2
                        wt_x = [(s3s1, p.cur(s3s1) - 1)] if tc >= 2 else []
                        p.dma("sync",
                              (lambda tc=tc, pb=pb: lambda e: e.dma_start(
                                  xq_sb[pb][:, :],
                                  x_q[tc * 128:(tc + 1) * 128, :]))(),
                              waits=wt_x, inc=s3x[pb])

                        wt_t = [(s3wl, wosplit)]
                        if tc >= 2:
                            wt_t.append((s3s1, p.cur(s3s1) - 1))

                        def mk_y(tc=tc, pb=pb):
                            def f(e):
                                last = None
                                for ot in range(2):
                                    n = 0
                                    for j in range(8):
                                        ah = at_h[:, j * SQ + tc * 128:
                                                  j * SQ + tc * 128 + 128]
                                        al = at_l[:, j * SQ + tc * 128:
                                                  j * SQ + tc * 128 + 128]
                                        bh = wo_h[:, j * D + ot * 512:
                                                  j * D + ot * 512 + 512]
                                        bl = wo_l[:, j * D + ot * 512:
                                                  j * D + ot * 512 + 512]
                                        for (a, b) in ((ah, bh), (ah, bl),
                                                       (al, bh)):
                                            last = e.matmul(
                                                y_ps[pb][:, ot * 512:
                                                         ot * 512 + 512],
                                                a, b, start=(n == 0),
                                                stop=(n == 23))
                                            n += 1
                                return last
                            return f
                        p.add("tensor", mk_y(), waits=wt_t, inc=s3t)
                        flush_tp3()

                        p.add("vector",
                              (lambda pb=pb: lambda e: e.tensor_add(
                                  s1b[pb][:, :], y_ps[pb][:, :],
                                  xq_sb[pb][:, :]))(),
                              waits=[(s3t, p.cur(s3t)),
                                     (s3x[pb], p.cur(s3x[pb]))],
                              inc=s3s1)
                        p.add("scalar",
                              (lambda pb=pb: lambda e: e.activation(
                                  sq3[pb][:, :], s1b[pb][:, :], AF.Square))(),
                              waits=[(s3s1, p.cur(s3s1))], inc=s3sq)

                        def chain3(fn, waits=()):
                            p.add("vector", fn,
                                  waits=list(waits) + [(s3ch, p.cur(s3ch))],
                                  inc=s3ch)
                        chain3((lambda pb=pb: lambda e: e.reduce_sum(
                            st3[:, 0:1], s1b[pb][:, :], axis=AX.X))(),
                            waits=[(s3ri, p.cur(s3ri)),
                                   (s3s1, p.cur(s3s1))])
                        chain3((lambda pb=pb: lambda e: e.reduce_sum(
                            st3[:, 1:2], sq3[pb][:, :], axis=AX.X))(),
                            waits=[(s3sq, p.cur(s3sq))])
                        chain3(lambda e: e.tensor_scalar(
                            st3[:, 2:3], st3[:, 0:1], c1024[:, :], None,
                            op0=ALU.mult))
                        chain3(lambda e: e.tensor_scalar(
                            st3[:, 3:4], st3[:, 1:2], c1024[:, :], None,
                            op0=ALU.mult))
                        chain3(lambda e: e.tensor_mul(
                            st3[:, 4:5], st3[:, 2:3], st3[:, 2:3]))
                        chain3(lambda e: e.tensor_sub(
                            st3[:, 5:6], st3[:, 3:4], st3[:, 4:5]))
                        chain3(lambda e: e.tensor_add(
                            st3[:, 6:7], st3[:, 5:6], ceps[:, :]))
                        p.add("vector",
                              lambda e: e.reciprocal(st3[:, 7:8],
                                                     st3[:, 6:7]),
                              waits=[(s3ch, p.cur(s3ch))], inc=s3st)
                        p.add("scalar",
                              (lambda: lambda e: e.activation(
                                  st3[:, 6:7], st3[:, 7:8], AF.Sqrt))(),
                              waits=[(s3st, p.cur(s3st))], inc=s3ri)
                        wt_h1 = [(s3ri, p.cur(s3ri))]
                        if tc >= 2:
                            wt_h1 += [(s3d[pb], p.cur(s3d[pb])),
                                      (s3tp, 8 * (tc - 1))]
                        p.add("vector",
                              (lambda pb=pb: lambda e: e.tensor_scalar(
                                  h1b[pb][:, :], s1b[pb][:, :], st3[:, 2:3],
                                  st3[:, 6:7], op0=ALU.subtract,
                                  op1=ALU.mult))(),
                              waits=wt_h1, inc=s3h1)
                        p.dma("sync",
                              (lambda tc=tc, pb=pb: lambda e: e.dma_start(
                                  h1scr[tc * 128:(tc + 1) * 128, :],
                                  h1b[pb][:, :]))(),
                              waits=[(s3h1, p.cur(s3h1))], inc=s3d[pb])
                        h1_mark = p.cur(s3h1)
                        for dc in range(8):
                            gi = st.setdefault("gi3", 0)
                            st["gi3"] = gi + 1
                            wt_tp = [(s3h1, h1_mark)]
                            if gi >= 4:
                                wt_tp.append((s3fl, gi - 3))
                            pend_tp3.append((
                                (lambda gi=gi, pb=pb, dc=dc:
                                 lambda e: e.transpose(
                                     t3_ps[gi % 4][:, :],
                                     h1b[pb][:, dc * 128:dc * 128 + 128],
                                     id_sb[:, :]))(),
                                wt_tp, "tensor", s3tp))
                            col = dc * SQ + tc * 128
                            pend_tp3.append((
                                (lambda gi=gi, col=col: lambda e: e.copy(
                                    h1f_h[:, col:col + 128],
                                    t3_ps[gi % 4][:, :]))(),
                                [(s3tp, gi + 1)], "scalar", s3fh))
                            pend_tp3.append((
                                (lambda gi=gi, col=col:
                                 lambda e: e.tensor_sub(
                                     h1f_l[:, col:col + 128],
                                     t3_ps[gi % 4][:, :],
                                     h1f_h[:, col:col + 128]))(),
                                [(s3fh, gi + 1)], "vector", s3fl))

                    flush_tp3()
                    h1d_tot = (p.cur(s3d[0]), p.cur(s3d[1]))
                    p.add("sync", lambda e: None,
                          waits=[(s3d[0], h1d_tot[0]), (s3d[1], h1d_tot[1])])
                    with nc.Block() as blk:
                        p.emit(blk)

                if dev == 3:
                    with nc.Block() as blk:
                        @blk.sync
                        def _(e):
                            e.wait_ge(s3d[0], h1d_tot[0])
                            e.wait_ge(s3d[1], h1d_tot[1])
                    return nc

                with ExitStack() as p4s:
                    ew_h = [gsb(f"ewh{i}", [128, 8 * D], BF16, p4s)
                            for i in range(2)]
                    ew_l = [gsb(f"ewl{i}", [128, 8 * D], BF16, p4s)
                            for i in range(2)]
                    wst4 = [gsb(f"wst4{i}", [128, D], F32, p4s)
                            for i in range(2)]
                    acc = gsb("acc", [128, 8 * D], F32, p4s)
                    gw_sb = wst4[0][0:8, :]
                    eb_sb = wst4[1][0:8, :]
                    gbr_t = gsb("gbr_t", [1, 8], F32, p4s)
                    gbr = gbr_t[0:1, 0:8]
                    gwt_h = gsb("gwt_h", [128, 64], BF16, p4s)
                    gwt_l = gsb("gwt_l", [128, 64], BF16, p4s)
                    gbr_h = gsb("gbr_h", [1, 8], BF16, p4s)
                    gbr_l = gsb("gbr_l", [1, 8], BF16, p4s)
                    eb_h = gsb("eb_h", [8, D], BF16, p4s)
                    eb_l = gsb("eb_l", [8, D], BF16, p4s)
                    lg_sb = gsb("lg_sb", [128, 8], F32, p4s)
                    gt1 = gsb("gt1", [128, 8], F32, p4s)
                    gt2 = gsb("gt2", [128, 8], F32, p4s)
                    g_all = gsb("g_all", [128, NTC * 8], F32, p4s)
                    gT_h = gsb("gT_h", [8, SQ], BF16, p4s)
                    gT_l = gsb("gT_l", [8, SQ], BF16, p4s)
                    h1r = [gsb(f"h1r{i}", [128, D], F32, p4s)
                           for i in range(2)]
                    s2b = [gsb(f"s2b{i}", [128, D], F32, p4s)
                           for i in range(2)]
                    ob = [gsb(f"ob{i}", [128, D], F32, p4s) for i in range(2)]
                    st4 = gsb("st4", [128, 8], F32, p4s)

                    lg_ps = gps("lgp", [128, 512], p4s)
                    ep_ps = [gps(f"epp{i}", [128, 1024], p4s)
                             for i in range(3)]
                    t4_ps = gps("t4p", [128, 128], p4s)

                    s4gw = gsem("s4gw", p4s)
                    s4w = [gsem("s4w0", p4s), gsem("s4w1", p4s)]
                    s4wh = gsem("s4wh", p4s)
                    s4wl = gsem("s4wl", p4s)
                    s4g = gsem("s4g", p4s)
                    s4gt = gsem("s4gt", p4s)
                    s4ge = gsem("s4ge", p4s)
                    s4t = gsem("s4t", p4s)
                    s4c = gsem("s4c", p4s)
                    s4x = [gsem("s4x0", p4s), gsem("s4x1", p4s)]
                    s4s2 = gsem("s4s2", p4s)
                    s4sq = gsem("s4sq", p4s)
                    s4st = gsem("s4st", p4s)
                    s4ch = gsem("s4ch", p4s)
                    s4ri = gsem("s4ri", p4s)
                    s4o = gsem("s4o", p4s)
                    s4od = [gsem("s4od0", p4s), gsem("s4od1", p4s)]

                    p = Sched()
                    p.dma("sync", lambda e: e.dma_start(gw_sb, gw_d[:, :]),
                          inc=s4gw)
                    p.dma("sync", lambda e: e.dma_start(gbr, gb_d[None, :]),
                          inc=s4gw)
                    p.dma("sync", lambda e: e.dma_start(eb_sb, eb_d[:, :]),
                          inc=s4gw)
                    p.add("scalar", lambda e: e.copy(gbr_h[:, :], gbr),
                          waits=[(s4gw, 48)], inc=s4wh)
                    p.add("vector", lambda e: e.tensor_sub(
                        gbr_l[:, :], gbr, gbr_h[:, :]),
                        waits=[(s4wh, p.cur(s4wh))], inc=s4wl)
                    p.add("scalar", lambda e: e.copy(eb_h[:, :], eb_sb),
                          waits=[(s4gw, 48)], inc=s4wh)
                    p.add("vector", lambda e: e.tensor_sub(
                        eb_l[:, :], eb_sb, eb_h[:, :]),
                        waits=[(s4wh, p.cur(s4wh))], inc=s4wl)
                    for ci in range(8):
                        p.add("tensor",
                              (lambda ci=ci: lambda e: e.transpose(
                                  t4_ps[:, 0:8],
                                  wst4[0][0:8, ci * 128:(ci + 1) * 128],
                                  id_sb[0:8, 0:8]))(),
                              waits=[(s4gw, 48), (s4wl, p.cur(s4wl))],
                              inc=s4gt)
                        p.add("scalar",
                              (lambda ci=ci: lambda e: e.copy(
                                  gwt_h[:, ci * 8:(ci + 1) * 8],
                                  t4_ps[:, 0:8]))(),
                              waits=[(s4gt, p.cur(s4gt))], inc=s4wh)
                        p.add("vector",
                              (lambda ci=ci: lambda e: e.tensor_sub(
                                  gwt_l[:, ci * 8:(ci + 1) * 8],
                                  t4_ps[:, 0:8],
                                  gwt_h[:, ci * 8:(ci + 1) * 8]))(),
                              waits=[(s4wh, p.cur(s4wh))], inc=s4wl)
                    gwt_done = p.cur(s4wl)

                    gate_mark = {}

                    def sched_gate(tc):
                        def mk_lg(tc=tc):
                            def f(e):
                                n = 0
                                for j in range(8):
                                    ah = h1f_h[:, j * SQ + tc * 128:
                                               j * SQ + tc * 128 + 128]
                                    al = h1f_l[:, j * SQ + tc * 128:
                                               j * SQ + tc * 128 + 128]
                                    bh = gwt_h[:, j * 8:(j + 1) * 8]
                                    bl = gwt_l[:, j * 8:(j + 1) * 8]
                                    for (a, b) in ((ah, bh), (ah, bl),
                                                   (al, bh)):
                                        e.matmul(lg_ps[:, 0:8], a, b,
                                                 start=(n == 0), stop=False)
                                        n += 1
                                e.matmul(lg_ps[:, 0:8], onesb[0:1, 0:128],
                                         gbr_h[0:1, :], start=False,
                                         stop=False)
                                return e.matmul(lg_ps[:, 0:8],
                                                onesb[0:1, 0:128],
                                                gbr_l[0:1, :], start=False,
                                                stop=True)
                            return f
                        wt = [(s4wl, gwt_done)]
                        if st.get("copy_mark", 0) > 0:
                            wt.append((s4ch, st["copy_mark"]))
                        p.add("tensor", mk_lg(), waits=wt, inc=s4gt)

                        def chaing(fn, waits=()):
                            p.add("vector", fn,
                                  waits=list(waits)
                                  + [(s4ch, p.cur(s4ch))], inc=s4ch)
                        chaing(lambda e: e.tensor_copy(lg_sb[:, :],
                                                       lg_ps[:, 0:8]),
                               waits=[(s4gt, p.cur(s4gt)),
                                      (s4ge, p.cur(s4ge))])
                        copy_mark_next[0] = p.cur(s4ch)
                        chaing(lambda e: e.reduce_max(
                            st4[:, 0:1], lg_sb[:, :], axis=AX.X, negate=True))
                        chaing(lambda e: e.tensor_scalar(
                            gt1[:, :], lg_sb[:, :], st4[:, 0:1], czero[:, :],
                            op0=ALU.add, op1=ALU.is_ge))
                        chaing(lambda e: e.scalar_tensor_tensor(
                            gt2[:, :], gt1[:, :], cn30[:, :], lg_sb[:, :],
                            op0=ALU.mult, op1=ALU.add))
                        chaing(lambda e: e.reduce_max(
                            st4[:, 1:2], gt2[:, :], axis=AX.X, negate=True))
                        p.add("vector",
                              lambda e: e.tensor_scalar(
                                  gt1[:, :], lg_sb[:, :], st4[:, 1:2],
                                  czero[:, :], op0=ALU.add, op1=ALU.is_ge),
                              waits=[(s4ch, p.cur(s4ch))], inc=s4g)
                        p.add("scalar",
                              (lambda tc=tc: lambda e: e.activation(
                                  gt2[:, :], lg_sb[:, :], AF.Exp,
                                  bias=st4[:, 0:1]))(),
                              waits=[(s4g, p.cur(s4g))], inc=s4ge)
                        chaing(lambda e: e.tensor_mul(
                            gt2[:, :], gt2[:, :], gt1[:, :]),
                            waits=[(s4ge, p.cur(s4ge))])
                        chaing(lambda e: e.reduce_sum(
                            st4[:, 2:3], gt2[:, :], axis=AX.X))
                        chaing(lambda e: e.reciprocal(st4[:, 3:4],
                                                      st4[:, 2:3]))
                        p.add("vector",
                              (lambda tc=tc: lambda e: e.tensor_scalar(
                                  g_all[:, tc * 8:(tc + 1) * 8], gt2[:, :],
                                  st4[:, 3:4], None, op0=ALU.mult))(),
                              waits=[(s4ch, p.cur(s4ch))], inc=s4g)
                        gate_mark[tc] = p.cur(s4g)
                        st["copy_mark"] = copy_mark_next[0]
                        pend_gT.append((tc, p.cur(s4g), p.cur(s4wl)))

                    order = [(g, e) for g in range(2) for e in range(E)]
                    marks = {}
                    smarks = {}
                    slot_free = {0: 0, 1: 0, 2: 0}
                    gates_todo = list(range(NTC))
                    pend_gT = []
                    copy_mark_next = [0]

                    def flush_gT():
                        (tc_, gmark, wlmark) = pend_gT.pop(0)
                        p.add("tensor",
                              (lambda tc=tc_: lambda e: e.transpose(
                                  t4_ps[0:8, 0:128],
                                  g_all[:, tc * 8:(tc + 1) * 8],
                                  id_sb[:, :]))(),
                              waits=[(s4g, gmark),
                                     (s4wl, p.cur(s4wl))], inc=s4gt)
                        p.add("scalar",
                              (lambda tc=tc_: lambda e: e.copy(
                                  gT_h[0:8, tc * 128:(tc + 1) * 128],
                                  t4_ps[0:8, 0:128]))(),
                              waits=[(s4gt, p.cur(s4gt))], inc=s4wh)
                        p.add("vector",
                              (lambda tc=tc_: lambda e: e.tensor_sub(
                                  gT_l[0:8, tc * 128:(tc + 1) * 128],
                                  t4_ps[0:8, 0:128],
                                  gT_h[0:8, tc * 128:(tc + 1) * 128]))(),
                              waits=[(s4wh, p.cur(s4wh))], inc=s4wl)

                    def sched_loads(eli):
                        e_i = order[eli][1]
                        for ci in range(8):
                            n = st.setdefault("n4", 0)
                            st["n4"] = n + 1
                            if eli > 0 or ci >= 2:
                                wt = [(s4wl, p.cur(s4wl) - 1)]
                            else:
                                wt = [(s4wl, gwt_done), (s4gt, 8)]
                            p.dma("gpsimd",
                                  (lambda e_i=e_i, ci=ci, n=n:
                                   lambda e: e.dma_start(
                                       wst4[n % 2][:, :],
                                       ew_d[e_i, ci * 128:(ci + 1) * 128,
                                            :]))(),
                                  waits=wt, inc=s4w[n % 2])
                            wt_h = [(s4w[n % 2], p.cur(s4w[n % 2]))]
                            if ci == 0 and eli >= 2:
                                wt_h.append((s4t, marks[eli - 2]))
                            p.add("scalar",
                                  (lambda eli=eli, ci=ci, n=n:
                                   lambda e: e.copy(
                                       ew_h[eli % 2][:, ci * D:(ci + 1) * D],
                                       wst4[n % 2][:, :]))(),
                                  waits=wt_h, inc=s4wh)
                            p.add("vector",
                                  (lambda eli=eli, ci=ci, n=n:
                                   lambda e: e.tensor_sub(
                                       ew_l[eli % 2][:, ci * D:(ci + 1) * D],
                                       wst4[n % 2][:, :],
                                       ew_h[eli % 2][:, ci * D:(ci + 1) * D]
                                   ))(),
                                  waits=[(s4wh, p.cur(s4wh))], inc=s4wl)
                        smarks[eli] = p.cur(s4wl)

                    for grp in range(2):
                        tcs = list(range(grp * 8, grp * 8 + 8))
                        for e_i in range(E):
                            eli = grp * E + e_i
                            if eli == 0:
                                sched_loads(0)
                            split_mark = smarks[eli]

                            for (ti, tc) in enumerate(tcs):
                                if gates_todo and eli < 4:
                                    sched_gate(gates_todo.pop(0))
                                elif pend_gT:
                                    flush_gT()
                                pb = tc % 3
                                sl = tc % 8

                                def mk_exp(eli=eli, tc=tc, pb=pb):
                                    def f(e):
                                        last = None
                                        for ot in range(2):
                                            n = 0
                                            for j in range(8):
                                                ah = h1f_h[
                                                    :, j * SQ + tc * 128:
                                                    j * SQ + tc * 128 + 128]
                                                al = h1f_l[
                                                    :, j * SQ + tc * 128:
                                                    j * SQ + tc * 128 + 128]
                                                bh = ew_h[eli % 2][
                                                    :, j * D + ot * 512:
                                                    j * D + ot * 512 + 512]
                                                bl = ew_l[eli % 2][
                                                    :, j * D + ot * 512:
                                                    j * D + ot * 512 + 512]
                                                for (a, b) in ((ah, bh),
                                                               (ah, bl),
                                                               (al, bh)):
                                                    last = e.matmul(
                                                        ep_ps[pb][
                                                            :, ot * 512:
                                                            ot * 512 + 512],
                                                        a, b,
                                                        start=(n == 0),
                                                        stop=(n == 23))
                                                    n += 1
                                        return last
                                    return f
                                p.add("tensor", mk_exp(),
                                      waits=[(s4wl, split_mark),
                                             (s4c, slot_free[pb])],
                                      inc=s4t)
                                wt_c = [(s4t, p.cur(s4t))]
                                if e_i == 0:
                                    wt_c.append((s4g, gate_mark[tc]))
                                    if grp > 0:
                                        wt_c.append(
                                            (s4s2, 8 * (grp - 1) + sl + 1))
                                    p.add("vector",
                                          (lambda tc=tc, pb=pb, sl=sl,
                                                  e_i=e_i:
                                           lambda e: e.tensor_scalar(
                                               acc[:, sl * D:(sl + 1) * D],
                                               ep_ps[pb][:, :],
                                               g_all[:, tc * 8 + e_i:
                                                     tc * 8 + e_i + 1],
                                               None, op0=ALU.mult))(),
                                          waits=wt_c, inc=s4c)
                                else:
                                    p.add("vector",
                                          (lambda tc=tc, pb=pb, sl=sl,
                                                  e_i=e_i:
                                           lambda e: e.scalar_tensor_tensor(
                                               acc[:, sl * D:(sl + 1) * D],
                                               ep_ps[pb][:, :],
                                               g_all[:, tc * 8 + e_i:
                                                     tc * 8 + e_i + 1],
                                               acc[:, sl * D:(sl + 1) * D],
                                               op0=ALU.mult,
                                               op1=ALU.add))(),
                                          waits=wt_c, inc=s4c)
                                slot_free[pb] = p.cur(s4c)
                                if ti == 0:
                                    marks[eli] = None
                                    if eli + 1 < len(order):
                                        sched_loads(eli + 1)
                            marks[eli] = p.cur(s4t)

                        while pend_gT:
                            flush_gT()
                        for tc in tcs:
                            sl = tc % 8
                            pb = tc % 3

                            def mk_bias(tc=tc, pb=pb):
                                def f(e):
                                    last = None
                                    for ot in range(2):
                                        ops = [(gT_h, eb_h), (gT_h, eb_l),
                                               (gT_l, eb_h)]
                                        for (pi, (a, b)) in enumerate(ops):
                                            last = e.matmul(
                                                ep_ps[pb][:, ot * 512:
                                                          ot * 512 + 512],
                                                a[0:8, tc * 128:
                                                  (tc + 1) * 128],
                                                b[0:8, ot * 512:
                                                  ot * 512 + 512],
                                                start=(pi == 0),
                                                stop=(pi == 2))
                                    return last
                                return f
                            p.add("tensor", mk_bias(),
                                  waits=[(s4wl, p.cur(s4wl)),
                                         (s4c, slot_free[pb])], inc=s4t)
                            p.add("vector",
                                  (lambda tc=tc, pb=pb, sl=sl:
                                   lambda e: e.tensor_add(
                                       acc[:, sl * D:(sl + 1) * D],
                                       ep_ps[pb][:, :],
                                       acc[:, sl * D:(sl + 1) * D]))(),
                                  waits=[(s4t, p.cur(s4t))], inc=s4c)
                            slot_free[pb] = p.cur(s4c)
                        experts_done = p.cur(s4c)

                        for tc in tcs:
                            pb = tc % 2
                            sl = tc % 8
                            wt_x = [(s4s2, p.cur(s4s2) - 1)] if tc >= 2 else []
                            p.dma("sync",
                                  (lambda tc=tc, pb=pb:
                                   lambda e: e.dma_start(
                                       h1r[pb][:, :],
                                       h1scr[tc * 128:(tc + 1) * 128, :]))(),
                                  waits=wt_x, inc=s4x[pb])
                            p.add("gpsimd",
                                  (lambda tc=tc, pb=pb, sl=sl:
                                   lambda e: e.tensor_add(
                                       s2b[pb][:, :],
                                       acc[:, sl * D:(sl + 1) * D],
                                       h1r[pb][:, :]))(),
                                  waits=[(s4x[pb], p.cur(s4x[pb])),
                                         (s4c, experts_done)], inc=s4s2)
                            wt_sq = [(s4s2, p.cur(s4s2))]
                            if p.cur(s4od[pb]) > 0:
                                wt_sq.append((s4od[pb], p.cur(s4od[pb])))
                            p.add("scalar",
                                  (lambda pb=pb: lambda e: e.activation(
                                      ob[pb][:, :], s2b[pb][:, :],
                                      AF.Square,
                                      accum_out=st4[:, 1:2]))(),
                                  waits=wt_sq, inc=s4sq)

                            def chain4(fn, waits=()):
                                p.add("vector", fn,
                                      waits=list(waits)
                                      + [(s4ch, p.cur(s4ch))], inc=s4ch)
                            chain4((lambda pb=pb: lambda e: e.reduce_sum(
                                st4[:, 0:1], s2b[pb][:, :], axis=AX.X))(),
                                waits=[(s4ri, p.cur(s4ri)),
                                       (s4sq, p.cur(s4sq)),
                                       (s4s2, p.cur(s4s2))])
                            chain4(lambda e: e.tensor_scalar(
                                st4[:, 2:3], st4[:, 0:1], c1024[:, :], None,
                                op0=ALU.mult))
                            chain4(lambda e: e.tensor_scalar(
                                st4[:, 3:4], st4[:, 1:2], c1024[:, :], None,
                                op0=ALU.mult))
                            chain4(lambda e: e.tensor_mul(
                                st4[:, 4:5], st4[:, 2:3], st4[:, 2:3]))
                            chain4(lambda e: e.tensor_sub(
                                st4[:, 5:6], st4[:, 3:4], st4[:, 4:5]))
                            chain4(lambda e: e.tensor_add(
                                st4[:, 6:7], st4[:, 5:6], ceps[:, :]))
                            p.add("vector",
                                  lambda e: e.reciprocal(st4[:, 7:8],
                                                         st4[:, 6:7]),
                                  waits=[(s4ch, p.cur(s4ch))], inc=s4st)
                            p.add("scalar",
                                  (lambda: lambda e: e.activation(
                                      st4[:, 6:7], st4[:, 7:8], AF.Sqrt))(),
                                  waits=[(s4st, p.cur(s4st))], inc=s4ri)
                            p.add("vector",
                                  (lambda pb=pb: lambda e: e.tensor_scalar(
                                      ob[pb][:, :], s2b[pb][:, :],
                                      st4[:, 2:3], st4[:, 6:7],
                                      op0=ALU.subtract, op1=ALU.mult))(),
                                  waits=[(s4ri, p.cur(s4ri))], inc=s4o)
                            p.dma("sync",
                                  (lambda tc=tc, pb=pb:
                                   lambda e: e.dma_start(
                                       out_d[tc * 128:(tc + 1) * 128, :],
                                       ob[pb][:, :]))(),
                                  waits=[(s4o, p.cur(s4o))], inc=s4od[pb])

                    final_od = (p.cur(s4od[0]), p.cur(s4od[1]))
                    with nc.Block() as blk:
                        p.emit(blk)
                    with nc.Block() as blk:
                        @blk.sync
                        def _(e):
                            e.wait_ge(s4od[0], final_od[0])
                            e.wait_ge(s4od[1], final_od[1])

    return nc


def kernel(**inputs):
    xs = np.ascontiguousarray(np.asarray(inputs["L0_attn_x"],
                                         dtype=np.float32))
    ws = {k: np.ascontiguousarray(np.asarray(v, dtype=np.float32))
          for (k, v) in inputs.items() if k != "L0_attn_x"}
    ident = np.eye(128, dtype=np.float32)
    in_maps = []
    for core in range(8):
        b, half = core // 2, core % 2
        in_maps.append(dict(
            x_kv=xs[b],
            x_q=np.ascontiguousarray(xs[b, half * SQ:(half + 1) * SQ]),
            wq=ws["L0_attn_Wq"], wk=ws["L0_attn_Wk"], wv=ws["L0_attn_Wv"],
            wr=ws["L0_attn_Wr"], wo=ws["L0_attn_Wo"],
            gw=ws["L0_ffn_gate_w"], gb=ws["L0_ffn_gate_b"],
            ew=ws["L0_ffn_expert_w"], eb=ws["L0_ffn_expert_b"],
            ident=ident,
        ))
    nc = build(dev=0)
    res = run_bass_kernel_spmd(nc, in_maps, core_ids=list(range(8)))
    out = np.empty((B, S, D), dtype=np.float32)
    for core in range(8):
        b, half = core // 2, core % 2
        out[b, half * SQ:(half + 1) * SQ] = res.results[core]["out"]
    return out


# version 12 (final)
# speedup vs baseline: 1.0645x; 1.0010x over previous
import math
import numpy as np
from contextlib import ExitStack

import concourse.bass as bass
import concourse.mybir as mybir
from concourse.bass_utils import run_bass_kernel_spmd

F32 = mybir.dt.float32
BF16 = mybir.dt.bfloat16
AF = mybir.ActivationFunctionType
ALU = mybir.AluOpType
AX = mybir.AxisListType

B, S, D = 4, 4096, 1024
H, DH, DR = 16, 64, 32
DC = DH + DR
SCALE = 1.0 / math.sqrt(DC)
E = 8
SQ = S // 2
NKV = S // 512
NQ = SQ // 512
NKC = S // 128
NTC = SQ // 128

ENGS = ("tensor", "scalar", "vector", "sync", "gpsimd")


class Sched:

    def __init__(self):
        self.ops = []
        self.cnt = {}

    def cur(self, sem):
        return self.cnt.get(id(sem), 0)

    def add(self, eng, fn, waits=(), inc=None, inc_k=1):
        ws = [(s, v) for (s, v) in waits if v > 0]
        self.ops.append((eng, ws, (inc, inc_k) if inc is not None else None, fn))
        if inc is not None:
            self.cnt[id(inc)] = self.cur(inc) + inc_k

    def dma(self, eng, fn, waits=(), inc=None):
        self.add(eng, fn, waits, inc, 16)

    def emit(self, block):
        def mk(eng_name):
            ops = [(w, i, f) for (e, w, i, f) in self.ops if e == eng_name]

            def run(eng):
                for (waits, inc, fn) in ops:
                    for (sem, val) in waits:
                        eng.wait_ge(sem, val)
                    instr = fn(eng)
                    if inc is not None:
                        assert instr is not None, "op with inc returned None"
                        instr.then_inc(inc[0], inc[1])
            return run

        for name in ENGS:
            if any(e == name for (e, _, _, _) in self.ops):
                getattr(block, name)(mk(name))


def build(dev=0, trunc=None):
    nc = bass.Bass("TRN2", target_bir_lowering=False)

    x_kv = nc.dram_tensor("x_kv", [S, D], F32, kind="ExternalInput")
    x_q = nc.dram_tensor("x_q", [SQ, D], F32, kind="ExternalInput")
    wq_d = nc.dram_tensor("wq", [D, D], F32, kind="ExternalInput")
    wk_d = nc.dram_tensor("wk", [D, D], F32, kind="ExternalInput")
    wv_d = nc.dram_tensor("wv", [D, D], F32, kind="ExternalInput")
    wr_d = nc.dram_tensor("wr", [D, H * DR], F32, kind="ExternalInput")
    wo_d = nc.dram_tensor("wo", [D, D], F32, kind="ExternalInput")
    gw_d = nc.dram_tensor("gw", [E, D], F32, kind="ExternalInput")
    gb_d = nc.dram_tensor("gb", [E], F32, kind="ExternalInput")
    ew_d = nc.dram_tensor("ew", [E, D, D], F32, kind="ExternalInput")
    eb_d = nc.dram_tensor("eb", [E, D], F32, kind="ExternalInput")
    id_d = nc.dram_tensor("ident", [128, 128], F32, kind="ExternalInput")
    out_d = nc.dram_tensor("out", [SQ, D], F32, kind="ExternalOutput")

    skw = dict(kind="ExternalOutput") if dev == 1 else {}
    kscr = nc.dram_tensor("kscr", [H, 2, 97, S], BF16, **skw)
    qscr = nc.dram_tensor("qscr", [H, 2, 97, SQ], BF16, **skw)
    vscr = nc.dram_tensor("vscr", [2, 128, H, NKC * 65], BF16, **skw)
    h1kw = dict(kind="ExternalOutput") if dev == 3 else {}
    h1scr = nc.dram_tensor("h1scr", [SQ, D], F32, **h1kw)
    if dev == 2:
        attn_dbg = nc.dram_tensor("attn_dbg", [2, 128, 8 * SQ], BF16,
                                  kind="ExternalOutput")

    with ExitStack() as g:
        def gsb(name, shape, dt=F32, st=None):
            return (st or g).enter_context(nc.sbuf_tensor(name, shape, dt))

        def gps(name, shape, st=None):
            return (st or g).enter_context(nc.psum_tensor(name, shape, F32))

        def gsem(name, st=None):
            return (st or g).enter_context(nc.semaphore(name))

        id_sb = gsb("id_sb", [128, 128])
        onesb = gsb("onesb", [128, 128], BF16)
        onesf = gsb("onesf", [128, 64])
        c1024 = gsb("c1024", [128, 1])
        ceps = gsb("ceps", [128, 1])
        czero = gsb("czero", [128, 1])
        cn30 = gsb("cn30", [128, 1])

        sd = gsem("sd")
        sv0 = gsem("sv0")
        actw = gsb("actw", [128, 4])
        with nc.Block() as blk:
            @blk.sync
            def _(sy):
                sy.dma_start(id_sb[:, :], id_d[:, :]).then_inc(sd, 16)
                sy.wait_ge(sd, 16)

            @blk.vector
            def _(v):
                v.memset(onesb[:, :], 1.0)
                v.memset(onesf[:, :], 1.0)
                v.memset(c1024[:, :], 1.0 / 1024.0)
                v.memset(ceps[:, :], 1e-5)
                v.memset(czero[:, :], 0.0)
                v.memset(cn30[:, :], -1e30).then_inc(sv0, 1)

            @blk.scalar
            def _(sc):
                sc.wait_ge(sv0, 1)
                sc.activation(actw[:, 0:1], c1024[:, :], AF.Square)
                sc.activation(actw[:, 1:2], c1024[:, :], AF.Exp)
                sc.activation(actw[:, 2:3], c1024[:, :], AF.Sqrt)

        with ExitStack() as p1s:
            wa_h = gsb("wa_h", [128, 8 * D], BF16, p1s)
            wa_l = gsb("wa_l", [128, 8 * D], BF16, p1s)
            wv_h = gsb("wv_h", [128, 8 * D], BF16, p1s)
            wv_l = gsb("wv_l", [128, 8 * D], BF16, p1s)
            wr_h = gsb("wr_h", [128, 8 * 512], BF16, p1s)
            wr_l = gsb("wr_l", [128, 8 * 512], BF16, p1s)
            wst = [gsb(f"wst{i}", [128, D], F32, p1s) for i in range(2)]
            x_tok = [gsb(f"xt{i}", [128, D], F32, p1s) for i in range(4)]
            xfm_h = [gsb(f"xfm_h{i}", [128, 8 * 512], BF16, p1s)
                     for i in range(2)]
            xfm_l = [gsb(f"xfm_l{i}", [128, 8 * 512], BF16, p1s)
                     for i in range(2)]
            ev = [gsb(f"ev{i}", [128, 1024], BF16, p1s) for i in range(4)]
            sqb = [gsb(f"sqb{i}", [128, 512], BF16, p1s) for i in range(4)]
            vst_h = gsb("vst_h", [128, H * 260], BF16, p1s)
            vst_l = gsb("vst_l", [128, H * 260], BF16, p1s)
            emat = gsb("emat", [128, 12 * 16], BF16, p1s)
            nrm2k = gsb("nrm2k", [16, S], F32, p1s)
            nrm2q = gsb("nrm2q", [16, SQ], F32, p1s)
            rowb = gsb("rowb", [16, S], BF16, p1s)
            chib = gsb("chib", [16, SQ], BF16, p1s)
            mkt = gsb("mkt", [16, 1], F32, p1s)

            tp_ps = [gps(f"tpp{i}", [128, 128], p1s) for i in range(4)]
            pj_ps = [gps(f"pjp{i}", [128, 512], p1s) for i in range(2)]
            nr_ps = gps("nrp", [16, 512], p1s)

            s_wld = [gsem("s_wld0", p1s), gsem("s_wld1", p1s)]
            s_wh = gsem("s_wh", p1s)
            s_wl = gsem("s_wl", p1s)
            s_x = [gsem(f"s_x{i}", p1s) for i in range(4)]
            s_tp = gsem("s_tp", p1s)
            s_xh = gsem("s_xh", p1s)
            s_xl = gsem("s_xl", p1s)
            s_pj = gsem("s_pj", p1s)
            s_eh = gsem("s_eh", p1s)
            s_el = gsem("s_el", p1s)
            s_sq = gsem("s_sq", p1s)
            s_nm = gsem("s_nm", p1s)
            s_ne = gsem("s_ne", p1s)
            s_scw = [gsem(f"s_scw{i}", p1s) for i in range(4)]
            s_vscw = gsem("s_vscw", p1s)
            s_vh = gsem("s_vh", p1s)
            s_vl = gsem("s_vl", p1s)
            s_em = gsem("s_em", p1s)

            st = {"el_hist": [], "sq_hist": [], "nm_hist": [],
                  "scw_hist": [], "vst_dma": 0, "pj_tile": {}}
            pend_nrm = []

            def p1_phase(p, q_pass):
                ntile = NQ if q_pass else NKV
                xsrc = x_q if q_pass else x_kv
                scr_main = qscr if q_pass else kscr
                nrm_dst = nrm2q if q_pass else nrm2k

                wl = ([(wq_d, D, wa_h, wa_l)] if q_pass else
                      [(wk_d, D, wa_h, wa_l), (wv_d, D, wv_h, wv_l),
                       (wr_d, 512, wr_h, wr_l)])
                kv_chunks_total = p.cur(s_pj)
                wl_marks = st.setdefault("wl_marks", {})
                for (wd, ncol, whi, wlo) in wl:
                    for ci in range(8):
                        n = st.setdefault("n_w", 0)
                        st["n_w"] = n + 1
                        wt = [(s_wl, p.cur(s_wl) - 1)] if n >= 2 else []
                        p.dma("gpsimd",
                              (lambda wd=wd, ncol=ncol, ci=ci, n=n:
                               lambda e: e.dma_start(
                                   wst[n % 2][:, 0:ncol],
                                   wd[ci * 128:(ci + 1) * 128, 0:ncol]))(),
                              waits=wt, inc=s_wld[n % 2])
                        wt_h = [(s_wld[n % 2], p.cur(s_wld[n % 2]))]
                        if q_pass and ci == 0:
                            wt_h.append((s_pj, kv_chunks_total))
                        p.add("scalar",
                              (lambda whi=whi, ncol=ncol, ci=ci, n=n:
                               lambda e: e.copy(
                                   whi[:, ci * ncol:(ci + 1) * ncol],
                                   wst[n % 2][:, 0:ncol]))(),
                              waits=wt_h, inc=s_wh)
                        p.add("vector",
                              (lambda whi=whi, wlo=wlo, ncol=ncol, ci=ci, n=n:
                               lambda e: e.tensor_sub(
                                   wlo[:, ci * ncol:(ci + 1) * ncol],
                                   wst[n % 2][:, 0:ncol],
                                   whi[:, ci * ncol:(ci + 1) * ncol]))(),
                              waits=[(s_wh, p.cur(s_wh))], inc=s_wl)
                    wl_marks[id(whi)] = p.cur(s_wl)
                wl_total = p.cur(s_wl)

                if not q_pass:
                    p.add("vector", lambda e: e.memset(emat[:, :], 0.0),
                          inc=s_em)

                    def setup(e):
                        for j in range(8):
                            e.memset(emat[0:64, j * 16 + 2 * j:
                                          j * 16 + 2 * j + 1], 1.0)
                            e.memset(emat[64:128, j * 16 + 2 * j + 1:
                                          j * 16 + 2 * j + 2], 1.0)
                        for j in range(4):
                            for rb in range(4):
                                hh = 4 * j + rb
                                e.memset(emat[32 * rb:32 * rb + 32,
                                              (8 + j) * 16 + hh:
                                              (8 + j) * 16 + hh + 1], 1.0)
                        vh3 = vst_h[:, :].rearrange("p (h c) -> p h c",
                                                    h=H * 4, c=65)
                        vl3 = vst_l[:, :].rearrange("p (h c) -> p h c",
                                                    h=H * 4, c=65)
                        e.memset(vh3[:, :, 64:65], 1.0)
                        return e.memset(vl3[:, :, 64:65], 0.0)
                    p.add("vector", setup, waits=[(s_em, 1)], inc=s_vl)

                if q_pass:
                    chunks = [("q", co) for co in range(8)] + \
                             [("r", co) for co in range(4)]
                else:
                    chunks = [("k", co) for co in range(8)] + \
                             [("r", co) for co in range(4)] + \
                             [("v", tch, oc) for tch in range(4)
                              for oc in range(2)]

                for tt in range(ntile):
                    gt = st.setdefault("gtile", 0)
                    st["gtile"] = gt + 1
                    xp = gt % 2
                    for tch in range(4):
                        xc = st.setdefault("xc", 0)
                        st["xc"] = xc + 1
                        wt = ([(s_tp, p.cur(s_tp) - 24)]
                              if p.cur(s_tp) >= 32 else [])
                        p.dma("sync",
                              (lambda xc=xc, tt=tt, tch=tch, xsrc=xsrc:
                               lambda e: e.dma_start(
                                   x_tok[xc % 4][:, :],
                                   xsrc[(tt * 4 + tch) * 128:
                                        (tt * 4 + tch + 1) * 128, :]))(),
                              waits=wt, inc=s_x[xc % 4])
                        for dc in range(8):
                            gi = p.cur(s_tp)
                            wt_t = [(s_x[xc % 4], p.cur(s_x[xc % 4]))]
                            if gi >= 4:
                                wt_t.append((s_xl, gi - 3))
                            p.add("tensor",
                                  (lambda gi=gi, xc=xc, dc=dc:
                                   lambda e: e.transpose(
                                       tp_ps[gi % 4][:, :],
                                       x_tok[xc % 4][:, dc * 128:dc * 128 + 128],
                                       id_sb[:, :]))(),
                                  waits=wt_t, inc=s_tp)
                            wt_s = [(s_tp, gi + 1)]
                            if tch == 0 and dc == 0:
                                if gt >= 2:
                                    wt_s.append(
                                        (s_pj, st["pj_tile"][gt - 2]))
                            col = dc * 512 + tch * 128
                            p.add("scalar",
                                  (lambda gi=gi, col=col, xp=xp:
                                   lambda e: e.copy(
                                       xfm_h[xp][:, col:col + 128],
                                       tp_ps[gi % 4][:, :]))(),
                                  waits=wt_s, inc=s_xh)
                            p.add("vector",
                                  (lambda gi=gi, col=col, xp=xp:
                                   lambda e: e.tensor_sub(
                                       xfm_l[xp][:, col:col + 128],
                                       tp_ps[gi % 4][:, :],
                                       xfm_h[xp][:, col:col + 128]))(),
                                  waits=[(s_xh, gi + 1)], inc=s_xl)

                    for (ic, c) in enumerate(chunks):
                        kind = c[0]
                        chc = p.cur(s_pj)
                        if kind in ("k", "q"):
                            wmark = wl_marks[id(wa_h)]
                        elif kind == "r":
                            wmark = wl_marks[id(wr_h)]
                        else:
                            wmark = wl_marks[id(wv_h)]
                        wt_m = [(s_xl, p.cur(s_xl)), (s_wl, wmark)]
                        if chc >= 2:
                            wt_m += st["el_hist"][chc - 2]

                        def mk_proj(kind=kind, c=c, chc=chc, xp=xp):
                            def f(e):
                                last = None
                                for ci in range(8):
                                    xh_ = xfm_h[xp][:, ci * 512:(ci + 1) * 512]
                                    xl_ = xfm_l[xp][:, ci * 512:(ci + 1) * 512]
                                    if kind in ("k", "q"):
                                        co = c[1]
                                        lh = wa_h[:, ci * D + co * 128:
                                                  ci * D + co * 128 + 128]
                                        ll = wa_l[:, ci * D + co * 128:
                                                  ci * D + co * 128 + 128]
                                        rh, rl = xh_, xl_
                                    elif kind == "r":
                                        co = c[1]
                                        lh = wr_h[:, ci * 512 + co * 128:
                                                  ci * 512 + co * 128 + 128]
                                        ll = wr_l[:, ci * 512 + co * 128:
                                                  ci * 512 + co * 128 + 128]
                                        rh, rl = xh_, xl_
                                    else:
                                        _, tch, oc = c
                                        lh = xfm_h[xp][
                                            :, ci * 512 + tch * 128:
                                            ci * 512 + tch * 128 + 128]
                                        ll = xfm_l[xp][
                                            :, ci * 512 + tch * 128:
                                            ci * 512 + tch * 128 + 128]
                                        rh = wv_h[:, ci * D + oc * 512:
                                                  ci * D + oc * 512 + 512]
                                        rl = wv_l[:, ci * D + oc * 512:
                                                  ci * D + oc * 512 + 512]
                                    for (pi, (a, b)) in enumerate(
                                            [(lh, rh), (lh, rl), (ll, rh)]):
                                        last = e.matmul(
                                            pj_ps[chc % 2][:, :], a, b,
                                            start=(ci == 0 and pi == 0),
                                            stop=(ci == 7 and pi == 2))
                                return last
                            return f
                        p.add("tensor", mk_proj(), waits=wt_m, inc=s_pj)
                        while len(pend_nrm) > 1:
                            (fnrm, wnrm) = pend_nrm.pop(0)
                            p.add("tensor", fnrm, waits=wnrm, inc=s_nm)

                        if kind != "v":
                            evc = p.cur(s_eh)
                            wt_h = [(s_pj, chc + 1)]
                            if evc >= 4:
                                wt_h.append(st["scw_hist"][evc - 4])
                            p.add("scalar",
                                  (lambda evc=evc, chc=chc:
                                   lambda e: e.copy(ev[evc % 4][:, 0:512],
                                                    pj_ps[chc % 2][:, :]))(),
                                  waits=wt_h, inc=s_eh)
                            p.add("vector",
                                  (lambda evc=evc, chc=chc:
                                   lambda e: e.tensor_sub(
                                       ev[evc % 4][:, 512:1024],
                                       pj_ps[chc % 2][:, :],
                                       ev[evc % 4][:, 0:512]))(),
                                  waits=[(s_eh, p.cur(s_eh))], inc=s_el)
                            sqc = p.cur(s_sq)
                            wt_q = [(s_pj, chc + 1), (s_el, p.cur(s_el))]
                            if sqc >= 4:
                                wt_q.append(st["nm_hist"][sqc - 4])
                            p.add("scalar",
                                  (lambda sqc=sqc, chc=chc:
                                   lambda e: e.activation(
                                       sqb[sqc % 4][:, :],
                                       pj_ps[chc % 2][:, :], AF.Square))(),
                                  waits=wt_q, inc=s_sq)
                            co = c[1]
                            ei = co if kind in ("k", "q") else 8 + co
                            wt_n = [(s_sq, sqc + 1), (s_vl, 1)]
                            if ic == 0:
                                wt_n.append((s_ne, p.cur(s_ne)))
                            pend_nrm.append(
                                ((lambda sqc=sqc, ei=ei, nf=(ic == 0),
                                          nl=(ic == 11):
                                  lambda e: e.matmul(
                                      nr_ps[:, :],
                                      emat[:, ei * 16:ei * 16 + 16],
                                      sqb[sqc % 4][:, :],
                                      start=nf, stop=nl))(), wt_n))
                            st["nm_hist"].append((s_nm, p.cur(s_nm) + 1))
                            SS = SQ if q_pass else S
                            dmas = []
                            if kind in ("k", "q"):
                                for hh in (2 * co, 2 * co + 1):
                                    r0 = 0 if hh % 2 == 0 else 64
                                    dst = bass.AP(
                                        scr_main, hh * 2 * 97 * SS + tt * 512,
                                        [[SS, 64], [97 * SS, 2], [1, 512]])
                                    dmas.append(
                                        (dst, ev[evc % 4][r0:r0 + 64, :]))
                            else:
                                for rb in range(4):
                                    hh = 4 * co + rb
                                    dst = bass.AP(
                                        scr_main,
                                        hh * 2 * 97 * SS + 64 * SS + tt * 512,
                                        [[SS, 32], [97 * SS, 2], [1, 512]])
                                    dmas.append(
                                        (dst,
                                         ev[evc % 4][32 * rb:32 * rb + 32, :]))
                            for (j, (dst, srcap)) in enumerate(dmas):
                                p.dma("gpsimd",
                                      (lambda dst=dst, srcap=srcap:
                                       lambda e: e.dma_start(dst, srcap))(),
                                      waits=([(s_el, evc + 1)]
                                             if j == 0 else []),
                                      inc=s_scw[evc % 4])
                            st["scw_hist"].append(
                                (s_scw[evc % 4], p.cur(s_scw[evc % 4])))
                            st["el_hist"].append([(s_el, p.cur(s_el)),
                                                  (s_sq, p.cur(s_sq))])
                        else:
                            _, tch, oc = c
                            wt_vh = [(s_pj, chc + 1)]
                            if tch == 0 and oc == 0 and st["vst_dma"] > 0:
                                wt_vh.append((s_vscw, st["vst_dma"]))

                            def mk_vh(tch=tch, oc=oc, chc=chc):
                                def f(e):
                                    last = None
                                    for i in range(8):
                                        cl = (oc * 8 + i) * 260 + tch * 65
                                        last = e.copy(
                                            vst_h[:, cl:cl + 64],
                                            pj_ps[chc % 2][:, i * 64:i * 64 + 64])
                                    return last
                                return f
                            p.add("scalar", mk_vh(), waits=wt_vh, inc=s_vh)

                            def mk_vl(tch=tch, oc=oc, chc=chc):
                                def f(e):
                                    last = None
                                    for i in range(8):
                                        cl = (oc * 8 + i) * 260 + tch * 65
                                        last = e.tensor_sub(
                                            vst_l[:, cl:cl + 64],
                                            pj_ps[chc % 2][:, i * 64:i * 64 + 64],
                                            vst_h[:, cl:cl + 64])
                                    return last
                                return f
                            p.add("vector", mk_vl(),
                                  waits=[(s_vh, p.cur(s_vh))], inc=s_vl)
                            st["el_hist"].append([(s_vl, p.cur(s_vl))])

                    st.setdefault("pj_tile", {})[gt] = p.cur(s_pj)
                    while pend_nrm:
                        (fnrm, wnrm) = pend_nrm.pop(0)
                        p.add("tensor", fnrm, waits=wnrm, inc=s_nm)
                    p.add("vector",
                          (lambda tt=tt, nrm_dst=nrm_dst:
                           lambda e: e.tensor_copy(
                               nrm_dst[:, tt * 512:(tt + 1) * 512],
                               nr_ps[:, :]))(),
                          waits=[(s_nm, p.cur(s_nm))], inc=s_ne)
                    if not q_pass:
                        for (hl, srct) in ((0, vst_h), (1, vst_l)):
                            p.dma("gpsimd",
                                  (lambda hl=hl, srct=srct, tt=tt:
                                   lambda e: e.dma_start(
                                       vscr[hl, :, :,
                                            tt * 260:(tt + 1) * 260],
                                       srct[:, :].rearrange(
                                           "p (h c) -> p h c", h=H, c=260)))(),
                                  waits=([(s_vl, p.cur(s_vl))]
                                         if hl == 0 else []),
                                  inc=s_vscw)
                        st["vst_dma"] = p.cur(s_vscw)

            p = Sched()
            p1_phase(p, False)
            p1_phase(p, True)
            fin = ([(s_scw[i], p.cur(s_scw[i])) for i in range(4)]
                   + [(s_vscw, p.cur(s_vscw))])
            p.add("sync", lambda e: None, waits=fin)
            if trunc is not None:
                p.ops = p.ops[:trunc]
            with nc.Block() as blk:
                p.emit(blk)
            if trunc is not None:
                return nc

            sp1 = gsem("sp1", p1s)
            sv1 = gsem("sv1", p1s)
            with nc.Block() as blk:
                @blk.scalar
                def _(e):
                    e.activation(nrm2k[:, :], nrm2k[:, :],
                                 AF.Sqrt).then_inc(sp1, 1)
                    e.activation(nrm2q[:, :], nrm2q[:, :],
                                 AF.Sqrt).then_inc(sp1, 1)

                @blk.vector
                def _(e):
                    e.wait_ge(sp1, 1)
                    e.reduce_max(mkt[:, :], nrm2k[:, :],
                                 axis=AX.X).then_inc(sv1, 1)
                    e.wait_ge(sp1, 2)
                    e.wait_ge(sv1, 1)
                    e.tensor_scalar(nrm2q[:, :], nrm2q[:, :], mkt[:, :], None,
                                    op0=ALU.mult).then_inc(sv1, 1)
                    e.wait_ge(sv1, 2)
                    e.tensor_copy(chib[:, :], nrm2q[:, :]).then_inc(sv1, 1)
                    e.memset(rowb[:, :], -1.0).then_inc(sv1, 1)

                @blk.sync
                def _(e):
                    e.wait_ge(sv1, 4)
                    e.dma_start(qscr[:, 0, 96, :], chib[:, :]).then_inc(sd, 16)
                    e.dma_start(kscr[:, 0, 96, :], rowb[:, :]).then_inc(sd, 16)
                    e.wait_ge(sd, 48)
            with nc.Block() as blk:
                @blk.vector
                def _(e):
                    e.memset(rowb[:, :], 0.0).then_inc(sv1, 1)

                @blk.sync
                def _(e):
                    e.wait_ge(sv1, 5)
                    e.dma_start(kscr[:, 1, 96, :], rowb[:, :]).then_inc(sd, 16)
                    e.dma_start(qscr[:, 1, 96, :],
                                rowb[:, 0:SQ]).then_inc(sd, 16)
                    e.wait_ge(sd, 80)

        if dev == 1:
            return nc

        with ExitStack() as p23:
            at_h = gsb("at_h", [128, 8 * SQ], BF16, p23)
            at_l = gsb("at_l", [128, 8 * SQ], BF16, p23)
            with ExitStack() as p2s:
                k2 = [gsb(f"k2_{i}", [128, 2 * S], BF16, p2s)
                      for i in range(2)]
                q2 = [gsb(f"q2_{i}", [128, 2 * SQ], BF16, p2s)
                      for i in range(2)]
                v2 = [gsb(f"v2_{i}", [128, 2 * NKC * 65], BF16, p2s)
                      for i in range(2)]
                p32 = [gsb(f"p32_{i}", [128, 2048], F32, p2s)
                       for i in range(2)]
                p_h = [gsb(f"p2h{i}", [128, 2048], BF16, p2s)
                       for i in range(2)]
                p_l = [gsb(f"p2l{i}", [128, 2048], BF16, p2s)
                       for i in range(2)]
                linv = gsb("linv", [128, 2048], F32, p2s)
                bc_sb = gsb("bc_sb", [128, 512], F32, p2s)
                t32 = gsb("t32", [128, 512], F32, p2s)
                hi0 = gsb("hi0", [128, 512], BF16, p2s)
                lo0 = gsb("lo0", [128, 512], BF16, p2s)

                sc_ps = [gps(f"scp{i}", [128, 1024], p2s) for i in range(2)]
                av_ps = [gps(f"avp{i}", [128, 512], p2s) for i in range(4)]

                s2d = [gsem("s2d0", p2s), gsem("s2d1", p2s)]
                s2sa = gsem("s2sa", p2s)
                s2sb = gsem("s2sb", p2s)
                s2ea = gsem("s2ea", p2s)
                s2eb = gsem("s2eb", p2s)
                s2pA = gsem("s2pA", p2s)
                s2pB = gsem("s2pB", p2s)
                s2av = gsem("s2av", p2s)
                s2li = gsem("s2li", p2s)
                s2bc = gsem("s2bc", p2s)
                s2ev = gsem("s2ev", p2s)
                s2ch = gsem("s2ch", p2s)
                s2bcc = gsem("s2bcc", p2s)
                s2hc = gsem("s2hc", p2s)

                p = Sched()

                def head_load(hh):
                    pb = hh % 2
                    wt = [(s2av, NKC * (hh - 1))] if hh >= 2 else []
                    ksrc = bass.AP(kscr, hh * 2 * 97 * S,
                                   [[S, 97], [97 * S, 2], [1, S]])
                    qsrc = bass.AP(qscr, hh * 2 * 97 * SQ,
                                   [[SQ, 97], [97 * SQ, 2], [1, SQ]])
                    vsrc = bass.AP(vscr, hh * NKC * 65,
                                   [[H * NKC * 65, 128],
                                    [128 * H * NKC * 65, 2], [1, NKC * 65]])
                    loads = [(k2[pb][0:97, :], ksrc),
                             (q2[pb][0:97, :], qsrc),
                             (v2[pb][:, :], vsrc)]
                    for (j, (dst, srcap)) in enumerate(loads):
                        p.dma("sync",
                              (lambda dst=dst, srcap=srcap:
                               lambda e: e.dma_start(dst, srcap))(),
                              waits=(wt if j == 0 else []), inc=s2d[pb])

                def sched_evac_recips(hh):
                    for qt in range(4):
                        p.add("vector",
                              (lambda qt=qt: lambda e: e.reciprocal(
                                  linv[64:65, qt * 512:qt * 512 + 512],
                                  av_ps[qt][64:65, :]))(),
                              waits=[(s2av, p.cur(s2av))], inc=s2li)

                def sched_evac(hh):
                    rows0 = (hh % 2) * 64
                    chj = hh // 2
                    li_base = p.cur(s2li)
                    for qt in range(4):
                        wt_b = [(s2li, li_base - 3 + qt)]
                        p.add("tensor",
                              (lambda qt=qt:
                               lambda e: e.matmul(
                                   av_ps[qt][64:128, :],
                                   onesf[64:65, 0:64],
                                   linv[64:65, qt * 512:qt * 512 + 512],
                                   start=True, stop=True))(),
                              waits=wt_b, inc=s2bc)
                        rows0 = (hh % 2) * 64
                        chj = hh // 2
                        cc = chj * SQ + qt * 512
                        p.add("vector",
                              (lambda qt=qt:
                               lambda e: e.tensor_copy(
                                   bc_sb[0:64, :],
                                   av_ps[qt][64:128, :]))(),
                              waits=[(s2bc, p.cur(s2bc)),
                                     (s2ch, p.cur(s2ch))], inc=s2bcc)
                        p.add("vector",
                              (lambda qt=qt: lambda e: e.tensor_mul(
                                  t32[0:64, :], av_ps[qt][0:64, :],
                                  bc_sb[0:64, :]))(),
                              waits=[(s2bcc, p.cur(s2bcc))], inc=s2ch)
                        p.add("vector",
                              (lambda: lambda e: e.tensor_copy(
                                  hi0[0:64, :], t32[0:64, :]))(),
                              waits=[(s2ch, p.cur(s2ch))], inc=s2ch)
                        p.add("vector",
                              (lambda: lambda e: e.tensor_sub(
                                  lo0[0:64, :], t32[0:64, :],
                                  hi0[0:64, :]))(),
                              waits=[(s2ch, p.cur(s2ch)),
                                     (s2ev, p.cur(s2ev))], inc=s2ch)
                        p.add("vector",
                              (lambda cc=cc, rows0=rows0:
                               lambda e: e.tensor_copy(
                                   at_h[rows0:rows0 + 64, cc:cc + 512],
                                   hi0[0:64, :]))(),
                              waits=[(s2ch, p.cur(s2ch))], inc=s2ch)
                        p.add("vector",
                              (lambda cc=cc, rows0=rows0:
                               lambda e: e.tensor_copy(
                                   at_l[rows0:rows0 + 64, cc:cc + 512],
                                   lo0[0:64, :]))(),
                              waits=[(s2ch, p.cur(s2ch))], inc=s2ev)


                head_load(0)
                for hh in range(H):
                    pb = hh % 2
                    d2val = p.cur(s2d[pb])
                    if hh > 0:
                        sched_evac_recips(hh - 1)
                    if hh + 1 < H:
                        head_load(hh + 1)

                    def mk_av(kc, pp, qts, pb=pb):
                        def f(e):
                            last = None
                            VL = NKC * 65
                            for qt in qts:
                                ops = [(0, p_h[pp]), (0, p_l[pp]),
                                       (1, p_h[pp])]
                                for (pi, (va, b)) in enumerate(ops):
                                    last = e.matmul(
                                        av_ps[qt][0:65, :],
                                        v2[pb][:, va * VL + kc * 65:
                                               va * VL + kc * 65 + 65],
                                        b[:, qt * 512:qt * 512 + 512],
                                        start=(kc == 0 and pi == 0),
                                        stop=(kc == NKC - 1 and pi == 2))
                            return last
                        return f

                    p_marks = {}
                    for kc in range(NKC):
                        pp = kc % 2
                        for (half, sps, ssem, esem) in (
                                (0, sc_ps[0], s2sa, s2ea),
                                (1, sc_ps[1], s2sb, s2eb)):
                            wt = [(s2d[pb], d2val), (esem, p.cur(esem))]

                            def mk_sc(kc=kc, half=half, sps=sps, pb=pb):
                                def f(e):
                                    last = None
                                    for qi in range(2):
                                        qt = half * 2 + qi
                                        ops = [(0, 0), (0, 1), (1, 0)]
                                        for (pi, (ka, qa)) in enumerate(ops):
                                            last = e.matmul(
                                                sps[:, qi * 512:qi * 512 + 512],
                                                k2[pb][0:97,
                                                       ka * S + kc * 128:
                                                       ka * S + kc * 128 + 128],
                                                q2[pb][0:97,
                                                       qa * SQ + qt * 512:
                                                       qa * SQ + qt * 512 + 512],
                                                start=(pi == 0), stop=(pi == 2))
                                    return last
                                return f
                            p.add("tensor", mk_sc(), waits=wt, inc=ssem)
                            wt_e = [(ssem, p.cur(ssem))]
                            phs = s2pA if half == 0 else s2pB
                            if p.cur(phs) >= 2:
                                wt_e.append((phs, p.cur(phs) - 1))
                            p.add("scalar",
                                  (lambda pp=pp, half=half, sps=sps:
                                   lambda e: e.activation(
                                       p32[pp][:, half * 1024:
                                               half * 1024 + 1024],
                                       sps[:, :], AF.Exp, scale=SCALE))(),
                                  waits=wt_e, inc=esem)
                        wt_v = [(s2ea, p.cur(s2ea))]
                        if p.cur(s2av) >= 1:
                            wt_v.append((s2av, p.cur(s2av)))
                        p.add("vector",
                              (lambda pp=pp: lambda e: e.tensor_copy(
                                  p_h[pp][:, 0:1024], p32[pp][:, 0:1024]))(),
                              waits=wt_v, inc=s2ch)
                        p.add("vector",
                              (lambda pp=pp: lambda e: e.tensor_sub(
                                  p_l[pp][:, 0:1024], p32[pp][:, 0:1024],
                                  p_h[pp][:, 0:1024]))(),
                              waits=[(s2ch, p.cur(s2ch))], inc=s2pA)
                        p.add("vector",
                              (lambda pp=pp: lambda e: e.tensor_copy(
                                  p_h[pp][:, 1024:2048],
                                  p32[pp][:, 1024:2048]))(),
                              waits=[(s2eb, p.cur(s2eb))], inc=s2ch)
                        p.add("vector",
                              (lambda pp=pp: lambda e: e.tensor_sub(
                                  p_l[pp][:, 1024:2048],
                                  p32[pp][:, 1024:2048],
                                  p_h[pp][:, 1024:2048]))(),
                              waits=[(s2ch, p.cur(s2ch))], inc=s2pB)
                        p_marks[kc] = (p.cur(s2pA), p.cur(s2pB))
                        if kc >= 1:
                            if kc == 1 and hh > 0:
                                sched_evac(hh - 1)
                            wt_av = [(s2pA, p_marks[kc - 1][0])]
                            if kc == 1 and hh > 0:
                                wt_av.append((s2ev, 4 * hh))
                            p.add("tensor",
                                  mk_av(kc - 1, (kc - 1) % 2, (0, 1)),
                                  waits=wt_av)
                            p.add("tensor",
                                  mk_av(kc - 1, (kc - 1) % 2, (2, 3)),
                                  waits=[(s2pB, p_marks[kc - 1][1])],
                                  inc=s2av)
                    p.add("tensor", mk_av(NKC - 1, (NKC - 1) % 2, (0, 1)),
                          waits=[(s2pA, p_marks[NKC - 1][0])])
                    p.add("tensor", mk_av(NKC - 1, (NKC - 1) % 2, (2, 3)),
                          waits=[(s2pB, p_marks[NKC - 1][1])], inc=s2av)
                    if hh == H - 1:
                        sched_evac_recips(hh)
                        sched_evac(hh)

                with nc.Block() as blk:
                    p.emit(blk)

                if dev == 2:
                    with nc.Block() as blk:
                        @blk.sync
                        def _(e):
                            e.dma_start(attn_dbg[0, :, :],
                                        at_h[:, :]).then_inc(sd, 16)
                            e.dma_start(attn_dbg[1, :, :],
                                        at_l[:, :]).then_inc(sd, 16)
                            e.wait_ge(sd, 80 + 32)
                    return nc

            h1f_h, h1f_l = at_h, at_l
            if True:
                with ExitStack() as p3s:
                    wo_h = gsb("wo_h", [128, 8 * D], BF16, p3s)
                    wo_l = gsb("wo_l", [128, 8 * D], BF16, p3s)
                    wst3 = [gsb(f"wst3{i}", [128, D], F32, p3s)
                            for i in range(2)]
                    xq_sb = [gsb(f"xq{i}", [128, D], F32, p3s)
                             for i in range(2)]
                    s1b = [gsb(f"s1b{i}", [128, D], F32, p3s)
                           for i in range(2)]
                    sq3 = [gsb(f"sq3{i}", [128, D], F32, p3s)
                           for i in range(2)]
                    h1b = [gsb(f"h1b{i}", [128, D], F32, p3s)
                           for i in range(2)]
                    st3 = gsb("st3", [128, 8], F32, p3s)

                    y_ps = [gps(f"yp{i}", [128, 1024], p3s) for i in range(2)]
                    t3_ps = [gps(f"t3p{i}", [128, 128], p3s) for i in range(4)]

                    s3w = [gsem("s3w0", p3s), gsem("s3w1", p3s)]
                    s3wh = gsem("s3wh", p3s)
                    s3wl = gsem("s3wl", p3s)
                    s3x = [gsem("s3x0", p3s), gsem("s3x1", p3s)]
                    s3t = gsem("s3t", p3s)
                    s3s1 = gsem("s3s1", p3s)
                    s3sq = gsem("s3sq", p3s)
                    s3st = gsem("s3st", p3s)
                    s3ch = gsem("s3ch", p3s)
                    s3ri = gsem("s3ri", p3s)
                    s3h1 = gsem("s3h1", p3s)
                    s3d = [gsem("s3d0", p3s), gsem("s3d1", p3s)]
                    s3tp = gsem("s3tp", p3s)
                    s3fh = gsem("s3fh", p3s)
                    s3fl = gsem("s3fl", p3s)

                    p = Sched()
                    pend_tp3 = []

                    def flush_tp3():
                        while pend_tp3:
                            (f3, w3, kind3, inc3) = pend_tp3.pop(0)
                            p.add(kind3, f3, waits=w3, inc=inc3)
                    for n in range(8):
                        wt = [(s3wl, n - 1)] if n >= 2 else []
                        p.dma("gpsimd",
                              (lambda n=n: lambda e: e.dma_start(
                                  wst3[n % 2][:, :],
                                  wo_d[n * 128:(n + 1) * 128, :]))(),
                              waits=wt, inc=s3w[n % 2])
                        p.add("scalar",
                              (lambda n=n: lambda e: e.copy(
                                  wo_h[:, n * D:(n + 1) * D],
                                  wst3[n % 2][:, :]))(),
                              waits=[(s3w[n % 2], p.cur(s3w[n % 2]))],
                              inc=s3wh)
                        p.add("vector",
                              (lambda n=n: lambda e: e.tensor_sub(
                                  wo_l[:, n * D:(n + 1) * D],
                                  wst3[n % 2][:, :],
                                  wo_h[:, n * D:(n + 1) * D]))(),
                              waits=[(s3wh, p.cur(s3wh))], inc=s3wl)
                    wosplit = p.cur(s3wl)

                    for tc in range(NTC):
                        pb = tc % 2
                        wt_x = [(s3s1, p.cur(s3s1) - 1)] if tc >= 2 else []
                        p.dma("sync",
                              (lambda tc=tc, pb=pb: lambda e: e.dma_start(
                                  xq_sb[pb][:, :],
                                  x_q[tc * 128:(tc + 1) * 128, :]))(),
                              waits=wt_x, inc=s3x[pb])

                        wt_t = [(s3wl, wosplit)]
                        if tc >= 2:
                            wt_t.append((s3s1, p.cur(s3s1) - 1))

                        def mk_y(tc=tc, pb=pb):
                            def f(e):
                                last = None
                                for ot in range(2):
                                    n = 0
                                    for j in range(8):
                                        ah = at_h[:, j * SQ + tc * 128:
                                                  j * SQ + tc * 128 + 128]
                                        al = at_l[:, j * SQ + tc * 128:
                                                  j * SQ + tc * 128 + 128]
                                        bh = wo_h[:, j * D + ot * 512:
                                                  j * D + ot * 512 + 512]
                                        bl = wo_l[:, j * D + ot * 512:
                                                  j * D + ot * 512 + 512]
                                        for (a, b) in ((ah, bh), (ah, bl),
                                                       (al, bh)):
                                            last = e.matmul(
                                                y_ps[pb][:, ot * 512:
                                                         ot * 512 + 512],
                                                a, b, start=(n == 0),
                                                stop=(n == 23))
                                            n += 1
                                return last
                            return f
                        p.add("tensor", mk_y(), waits=wt_t, inc=s3t)
                        flush_tp3()

                        p.add("vector",
                              (lambda pb=pb: lambda e: e.tensor_add(
                                  s1b[pb][:, :], y_ps[pb][:, :],
                                  xq_sb[pb][:, :]))(),
                              waits=[(s3t, p.cur(s3t)),
                                     (s3x[pb], p.cur(s3x[pb]))],
                              inc=s3s1)
                        p.add("scalar",
                              (lambda pb=pb: lambda e: e.activation(
                                  sq3[pb][:, :], s1b[pb][:, :], AF.Square))(),
                              waits=[(s3s1, p.cur(s3s1))], inc=s3sq)

                        def chain3(fn, waits=()):
                            p.add("vector", fn,
                                  waits=list(waits) + [(s3ch, p.cur(s3ch))],
                                  inc=s3ch)
                        chain3((lambda pb=pb: lambda e: e.reduce_sum(
                            st3[:, 0:1], s1b[pb][:, :], axis=AX.X))(),
                            waits=[(s3ri, p.cur(s3ri)),
                                   (s3s1, p.cur(s3s1))])
                        chain3((lambda pb=pb: lambda e: e.reduce_sum(
                            st3[:, 1:2], sq3[pb][:, :], axis=AX.X))(),
                            waits=[(s3sq, p.cur(s3sq))])
                        chain3(lambda e: e.tensor_scalar(
                            st3[:, 2:3], st3[:, 0:1], c1024[:, :], None,
                            op0=ALU.mult))
                        chain3(lambda e: e.tensor_scalar(
                            st3[:, 3:4], st3[:, 1:2], c1024[:, :], None,
                            op0=ALU.mult))
                        chain3(lambda e: e.tensor_mul(
                            st3[:, 4:5], st3[:, 2:3], st3[:, 2:3]))
                        chain3(lambda e: e.tensor_sub(
                            st3[:, 5:6], st3[:, 3:4], st3[:, 4:5]))
                        chain3(lambda e: e.tensor_add(
                            st3[:, 6:7], st3[:, 5:6], ceps[:, :]))
                        p.add("vector",
                              lambda e: e.reciprocal(st3[:, 7:8],
                                                     st3[:, 6:7]),
                              waits=[(s3ch, p.cur(s3ch))], inc=s3st)
                        p.add("scalar",
                              (lambda: lambda e: e.activation(
                                  st3[:, 6:7], st3[:, 7:8], AF.Sqrt))(),
                              waits=[(s3st, p.cur(s3st))], inc=s3ri)
                        wt_h1 = [(s3ri, p.cur(s3ri))]
                        if tc >= 2:
                            wt_h1 += [(s3d[pb], p.cur(s3d[pb])),
                                      (s3tp, 8 * (tc - 1))]
                        p.add("vector",
                              (lambda pb=pb: lambda e: e.tensor_scalar(
                                  h1b[pb][:, :], s1b[pb][:, :], st3[:, 2:3],
                                  st3[:, 6:7], op0=ALU.subtract,
                                  op1=ALU.mult))(),
                              waits=wt_h1, inc=s3h1)
                        p.dma("sync",
                              (lambda tc=tc, pb=pb: lambda e: e.dma_start(
                                  h1scr[tc * 128:(tc + 1) * 128, :],
                                  h1b[pb][:, :]))(),
                              waits=[(s3h1, p.cur(s3h1))], inc=s3d[pb])
                        h1_mark = p.cur(s3h1)
                        for dc in range(8):
                            gi = st.setdefault("gi3", 0)
                            st["gi3"] = gi + 1
                            wt_tp = [(s3h1, h1_mark)]
                            if gi >= 4:
                                wt_tp.append((s3fl, gi - 3))
                            pend_tp3.append((
                                (lambda gi=gi, pb=pb, dc=dc:
                                 lambda e: e.transpose(
                                     t3_ps[gi % 4][:, :],
                                     h1b[pb][:, dc * 128:dc * 128 + 128],
                                     id_sb[:, :]))(),
                                wt_tp, "tensor", s3tp))
                            col = dc * SQ + tc * 128
                            pend_tp3.append((
                                (lambda gi=gi, col=col: lambda e: e.copy(
                                    h1f_h[:, col:col + 128],
                                    t3_ps[gi % 4][:, :]))(),
                                [(s3tp, gi + 1)], "scalar", s3fh))
                            pend_tp3.append((
                                (lambda gi=gi, col=col:
                                 lambda e: e.tensor_sub(
                                     h1f_l[:, col:col + 128],
                                     t3_ps[gi % 4][:, :],
                                     h1f_h[:, col:col + 128]))(),
                                [(s3fh, gi + 1)], "vector", s3fl))

                    flush_tp3()
                    h1d_tot = (p.cur(s3d[0]), p.cur(s3d[1]))
                    p.add("sync", lambda e: None,
                          waits=[(s3d[0], h1d_tot[0]), (s3d[1], h1d_tot[1])])
                    with nc.Block() as blk:
                        p.emit(blk)

                if dev == 3:
                    with nc.Block() as blk:
                        @blk.sync
                        def _(e):
                            e.wait_ge(s3d[0], h1d_tot[0])
                            e.wait_ge(s3d[1], h1d_tot[1])
                    return nc

                with ExitStack() as p4s:
                    ew_h = [gsb(f"ewh{i}", [128, 8 * D], BF16, p4s)
                            for i in range(2)]
                    ew_l = [gsb(f"ewl{i}", [128, 8 * D], BF16, p4s)
                            for i in range(2)]
                    wst4 = [gsb(f"wst4{i}", [128, D], F32, p4s)
                            for i in range(2)]
                    acc = gsb("acc", [128, 8 * D], F32, p4s)
                    gw_sb = wst4[0][0:8, :]
                    eb_sb = wst4[1][0:8, :]
                    gbr_t = gsb("gbr_t", [1, 8], F32, p4s)
                    gbr = gbr_t[0:1, 0:8]
                    gwt_h = gsb("gwt_h", [128, 64], BF16, p4s)
                    gwt_l = gsb("gwt_l", [128, 64], BF16, p4s)
                    gbr_h = gsb("gbr_h", [1, 8], BF16, p4s)
                    gbr_l = gsb("gbr_l", [1, 8], BF16, p4s)
                    eb_h = gsb("eb_h", [8, D], BF16, p4s)
                    eb_l = gsb("eb_l", [8, D], BF16, p4s)
                    lg_sb = gsb("lg_sb", [128, 8], F32, p4s)
                    gt1 = gsb("gt1", [128, 8], F32, p4s)
                    gt2 = gsb("gt2", [128, 8], F32, p4s)
                    g_all = gsb("g_all", [128, NTC * 8], F32, p4s)
                    gT_h = gsb("gT_h", [8, SQ], BF16, p4s)
                    gT_l = gsb("gT_l", [8, SQ], BF16, p4s)
                    h1r = [gsb(f"h1r{i}", [128, D], F32, p4s)
                           for i in range(2)]
                    s2b = [gsb(f"s2b{i}", [128, D], F32, p4s)
                           for i in range(2)]
                    ob = [gsb(f"ob{i}", [128, D], F32, p4s) for i in range(2)]
                    st4 = gsb("st4", [128, 8], F32, p4s)

                    lg_ps = gps("lgp", [128, 512], p4s)
                    ep_ps = [gps(f"epp{i}", [128, 1024], p4s)
                             for i in range(3)]
                    t4_ps = gps("t4p", [128, 128], p4s)

                    s4gw = gsem("s4gw", p4s)
                    s4w = [gsem("s4w0", p4s), gsem("s4w1", p4s)]
                    s4wh = gsem("s4wh", p4s)
                    s4wl = gsem("s4wl", p4s)
                    s4g = gsem("s4g", p4s)
                    s4gt = gsem("s4gt", p4s)
                    s4ge = gsem("s4ge", p4s)
                    s4t = gsem("s4t", p4s)
                    s4c = gsem("s4c", p4s)
                    s4x = [gsem("s4x0", p4s), gsem("s4x1", p4s)]
                    s4s2 = gsem("s4s2", p4s)
                    s4sq = gsem("s4sq", p4s)
                    s4st = gsem("s4st", p4s)
                    s4ch = gsem("s4ch", p4s)
                    s4ri = gsem("s4ri", p4s)
                    s4o = gsem("s4o", p4s)
                    s4od = [gsem("s4od0", p4s), gsem("s4od1", p4s)]

                    p = Sched()
                    p.dma("sync", lambda e: e.dma_start(gw_sb, gw_d[:, :]),
                          inc=s4gw)
                    p.dma("sync", lambda e: e.dma_start(gbr, gb_d[None, :]),
                          inc=s4gw)
                    p.dma("sync", lambda e: e.dma_start(eb_sb, eb_d[:, :]),
                          inc=s4gw)
                    p.add("scalar", lambda e: e.copy(gbr_h[:, :], gbr),
                          waits=[(s4gw, 48)], inc=s4wh)
                    p.add("vector", lambda e: e.tensor_sub(
                        gbr_l[:, :], gbr, gbr_h[:, :]),
                        waits=[(s4wh, p.cur(s4wh))], inc=s4wl)
                    p.add("scalar", lambda e: e.copy(eb_h[:, :], eb_sb),
                          waits=[(s4gw, 48)], inc=s4wh)
                    p.add("vector", lambda e: e.tensor_sub(
                        eb_l[:, :], eb_sb, eb_h[:, :]),
                        waits=[(s4wh, p.cur(s4wh))], inc=s4wl)
                    for ci in range(8):
                        p.add("tensor",
                              (lambda ci=ci: lambda e: e.transpose(
                                  t4_ps[:, 0:8],
                                  wst4[0][0:8, ci * 128:(ci + 1) * 128],
                                  id_sb[0:8, 0:8]))(),
                              waits=[(s4gw, 48), (s4wl, p.cur(s4wl))],
                              inc=s4gt)
                        p.add("scalar",
                              (lambda ci=ci: lambda e: e.copy(
                                  gwt_h[:, ci * 8:(ci + 1) * 8],
                                  t4_ps[:, 0:8]))(),
                              waits=[(s4gt, p.cur(s4gt))], inc=s4wh)
                        p.add("vector",
                              (lambda ci=ci: lambda e: e.tensor_sub(
                                  gwt_l[:, ci * 8:(ci + 1) * 8],
                                  t4_ps[:, 0:8],
                                  gwt_h[:, ci * 8:(ci + 1) * 8]))(),
                              waits=[(s4wh, p.cur(s4wh))], inc=s4wl)
                    gwt_done = p.cur(s4wl)

                    gate_mark = {}

                    def sched_gate(tc):
                        def mk_lg(tc=tc):
                            def f(e):
                                n = 0
                                for j in range(8):
                                    ah = h1f_h[:, j * SQ + tc * 128:
                                               j * SQ + tc * 128 + 128]
                                    al = h1f_l[:, j * SQ + tc * 128:
                                               j * SQ + tc * 128 + 128]
                                    bh = gwt_h[:, j * 8:(j + 1) * 8]
                                    bl = gwt_l[:, j * 8:(j + 1) * 8]
                                    for (a, b) in ((ah, bh), (ah, bl),
                                                   (al, bh)):
                                        e.matmul(lg_ps[:, 0:8], a, b,
                                                 start=(n == 0), stop=False)
                                        n += 1
                                e.matmul(lg_ps[:, 0:8], onesb[0:1, 0:128],
                                         gbr_h[0:1, :], start=False,
                                         stop=False)
                                return e.matmul(lg_ps[:, 0:8],
                                                onesb[0:1, 0:128],
                                                gbr_l[0:1, :], start=False,
                                                stop=True)
                            return f
                        wt = [(s4wl, gwt_done)]
                        if st.get("copy_mark", 0) > 0:
                            wt.append((s4ch, st["copy_mark"]))
                        p.add("tensor", mk_lg(), waits=wt, inc=s4gt)

                        def chaing(fn, waits=()):
                            p.add("vector", fn,
                                  waits=list(waits)
                                  + [(s4ch, p.cur(s4ch))], inc=s4ch)
                        chaing(lambda e: e.tensor_copy(lg_sb[:, :],
                                                       lg_ps[:, 0:8]),
                               waits=[(s4gt, p.cur(s4gt)),
                                      (s4ge, p.cur(s4ge))])
                        copy_mark_next[0] = p.cur(s4ch)
                        chaing(lambda e: e.reduce_max(
                            st4[:, 0:1], lg_sb[:, :], axis=AX.X, negate=True))
                        chaing(lambda e: e.tensor_scalar(
                            gt1[:, :], lg_sb[:, :], st4[:, 0:1], czero[:, :],
                            op0=ALU.add, op1=ALU.is_ge))
                        chaing(lambda e: e.scalar_tensor_tensor(
                            gt2[:, :], gt1[:, :], cn30[:, :], lg_sb[:, :],
                            op0=ALU.mult, op1=ALU.add))
                        chaing(lambda e: e.reduce_max(
                            st4[:, 1:2], gt2[:, :], axis=AX.X, negate=True))
                        p.add("vector",
                              lambda e: e.tensor_scalar(
                                  gt1[:, :], lg_sb[:, :], st4[:, 1:2],
                                  czero[:, :], op0=ALU.add, op1=ALU.is_ge),
                              waits=[(s4ch, p.cur(s4ch))], inc=s4g)
                        p.add("scalar",
                              (lambda tc=tc: lambda e: e.activation(
                                  gt2[:, :], lg_sb[:, :], AF.Exp,
                                  bias=st4[:, 0:1]))(),
                              waits=[(s4g, p.cur(s4g))], inc=s4ge)
                        chaing(lambda e: e.tensor_mul(
                            gt2[:, :], gt2[:, :], gt1[:, :]),
                            waits=[(s4ge, p.cur(s4ge))])
                        chaing(lambda e: e.reduce_sum(
                            st4[:, 2:3], gt2[:, :], axis=AX.X))
                        chaing(lambda e: e.reciprocal(st4[:, 3:4],
                                                      st4[:, 2:3]))
                        p.add("vector",
                              (lambda tc=tc: lambda e: e.tensor_scalar(
                                  g_all[:, tc * 8:(tc + 1) * 8], gt2[:, :],
                                  st4[:, 3:4], None, op0=ALU.mult))(),
                              waits=[(s4ch, p.cur(s4ch))], inc=s4g)
                        gate_mark[tc] = p.cur(s4g)
                        st["copy_mark"] = copy_mark_next[0]
                        pend_gT.append((tc, p.cur(s4g), p.cur(s4wl)))

                    order = [(g, e) for g in range(2) for e in range(E)]
                    marks = {}
                    smarks = {}
                    slot_free = {0: 0, 1: 0, 2: 0}
                    gates_todo = list(range(NTC))
                    pend_gT = []
                    copy_mark_next = [0]

                    def flush_gT():
                        (tc_, gmark, wlmark) = pend_gT.pop(0)
                        p.add("tensor",
                              (lambda tc=tc_: lambda e: e.transpose(
                                  t4_ps[0:8, 0:128],
                                  g_all[:, tc * 8:(tc + 1) * 8],
                                  id_sb[:, :]))(),
                              waits=[(s4g, gmark),
                                     (s4wl, p.cur(s4wl))], inc=s4gt)
                        p.add("scalar",
                              (lambda tc=tc_: lambda e: e.copy(
                                  gT_h[0:8, tc * 128:(tc + 1) * 128],
                                  t4_ps[0:8, 0:128]))(),
                              waits=[(s4gt, p.cur(s4gt))], inc=s4wh)
                        p.add("vector",
                              (lambda tc=tc_: lambda e: e.tensor_sub(
                                  gT_l[0:8, tc * 128:(tc + 1) * 128],
                                  t4_ps[0:8, 0:128],
                                  gT_h[0:8, tc * 128:(tc + 1) * 128]))(),
                              waits=[(s4wh, p.cur(s4wh))], inc=s4wl)

                    def sched_loads(eli):
                        e_i = order[eli][1]
                        for ci in range(8):
                            n = st.setdefault("n4", 0)
                            st["n4"] = n + 1
                            if eli > 0 or ci >= 2:
                                wt = [(s4wl, p.cur(s4wl) - 1)]
                            else:
                                wt = [(s4wl, gwt_done), (s4gt, 8)]
                            p.dma("gpsimd",
                                  (lambda e_i=e_i, ci=ci, n=n:
                                   lambda e: e.dma_start(
                                       wst4[n % 2][:, :],
                                       ew_d[e_i, ci * 128:(ci + 1) * 128,
                                            :]))(),
                                  waits=wt, inc=s4w[n % 2])
                            wt_h = [(s4w[n % 2], p.cur(s4w[n % 2]))]
                            if ci == 0 and eli >= 2:
                                wt_h.append((s4t, marks[eli - 2]))
                            p.add("scalar",
                                  (lambda eli=eli, ci=ci, n=n:
                                   lambda e: e.copy(
                                       ew_h[eli % 2][:, ci * D:(ci + 1) * D],
                                       wst4[n % 2][:, :]))(),
                                  waits=wt_h, inc=s4wh)
                            p.add("vector",
                                  (lambda eli=eli, ci=ci, n=n:
                                   lambda e: e.tensor_sub(
                                       ew_l[eli % 2][:, ci * D:(ci + 1) * D],
                                       wst4[n % 2][:, :],
                                       ew_h[eli % 2][:, ci * D:(ci + 1) * D]
                                   ))(),
                                  waits=[(s4wh, p.cur(s4wh))], inc=s4wl)
                        smarks[eli] = p.cur(s4wl)

                    for grp in range(2):
                        tcs = list(range(grp * 8, grp * 8 + 8))
                        for e_i in range(E):
                            eli = grp * E + e_i
                            if eli == 0:
                                sched_loads(0)
                            split_mark = smarks[eli]

                            for (ti, tc) in enumerate(tcs):
                                if gates_todo and eli < 4:
                                    sched_gate(gates_todo.pop(0))
                                elif pend_gT:
                                    flush_gT()
                                pb = tc % 3
                                sl = tc % 8

                                def mk_exp(eli=eli, tc=tc, pb=pb):
                                    def f(e):
                                        last = None
                                        for ot in range(2):
                                            n = 0
                                            for j in range(8):
                                                ah = h1f_h[
                                                    :, j * SQ + tc * 128:
                                                    j * SQ + tc * 128 + 128]
                                                al = h1f_l[
                                                    :, j * SQ + tc * 128:
                                                    j * SQ + tc * 128 + 128]
                                                bh = ew_h[eli % 2][
                                                    :, j * D + ot * 512:
                                                    j * D + ot * 512 + 512]
                                                bl = ew_l[eli % 2][
                                                    :, j * D + ot * 512:
                                                    j * D + ot * 512 + 512]
                                                for (a, b) in ((ah, bh),
                                                               (ah, bl),
                                                               (al, bh)):
                                                    last = e.matmul(
                                                        ep_ps[pb][
                                                            :, ot * 512:
                                                            ot * 512 + 512],
                                                        a, b,
                                                        start=(n == 0),
                                                        stop=(n == 23))
                                                    n += 1
                                        return last
                                    return f
                                p.add("tensor", mk_exp(),
                                      waits=[(s4wl, split_mark),
                                             (s4c, slot_free[pb])],
                                      inc=s4t)
                                wt_c = [(s4t, p.cur(s4t))]
                                if e_i == 0:
                                    wt_c.append((s4g, gate_mark[tc]))
                                    if grp > 0:
                                        wt_c.append(
                                            (s4s2, 8 * (grp - 1) + sl + 1))
                                    p.add("vector",
                                          (lambda tc=tc, pb=pb, sl=sl,
                                                  e_i=e_i:
                                           lambda e: e.tensor_scalar(
                                               acc[:, sl * D:(sl + 1) * D],
                                               ep_ps[pb][:, :],
                                               g_all[:, tc * 8 + e_i:
                                                     tc * 8 + e_i + 1],
                                               None, op0=ALU.mult))(),
                                          waits=wt_c, inc=s4c)
                                else:
                                    p.add("vector",
                                          (lambda tc=tc, pb=pb, sl=sl,
                                                  e_i=e_i:
                                           lambda e: e.scalar_tensor_tensor(
                                               acc[:, sl * D:(sl + 1) * D],
                                               ep_ps[pb][:, :],
                                               g_all[:, tc * 8 + e_i:
                                                     tc * 8 + e_i + 1],
                                               acc[:, sl * D:(sl + 1) * D],
                                               op0=ALU.mult,
                                               op1=ALU.add))(),
                                          waits=wt_c, inc=s4c)
                                slot_free[pb] = p.cur(s4c)
                                if ti == 0:
                                    marks[eli] = None
                                    if eli + 1 < len(order):
                                        sched_loads(eli + 1)
                            marks[eli] = p.cur(s4t)

                        while pend_gT:
                            flush_gT()
                        for tc in tcs:
                            sl = tc % 8
                            pb = tc % 3

                            def mk_bias(tc=tc, pb=pb):
                                def f(e):
                                    last = None
                                    for ot in range(2):
                                        ops = [(gT_h, eb_h), (gT_h, eb_l),
                                               (gT_l, eb_h)]
                                        for (pi, (a, b)) in enumerate(ops):
                                            last = e.matmul(
                                                ep_ps[pb][:, ot * 512:
                                                          ot * 512 + 512],
                                                a[0:8, tc * 128:
                                                  (tc + 1) * 128],
                                                b[0:8, ot * 512:
                                                  ot * 512 + 512],
                                                start=(pi == 0),
                                                stop=(pi == 2))
                                    return last
                                return f
                            p.add("tensor", mk_bias(),
                                  waits=[(s4wl, p.cur(s4wl)),
                                         (s4c, slot_free[pb])], inc=s4t)
                            p.add("vector",
                                  (lambda tc=tc, pb=pb, sl=sl:
                                   lambda e: e.tensor_add(
                                       acc[:, sl * D:(sl + 1) * D],
                                       ep_ps[pb][:, :],
                                       acc[:, sl * D:(sl + 1) * D]))(),
                                  waits=[(s4t, p.cur(s4t))], inc=s4c)
                            slot_free[pb] = p.cur(s4c)
                        experts_done = p.cur(s4c)

                        for tc in tcs:
                            pb = tc % 2
                            sl = tc % 8
                            wt_x = [(s4s2, p.cur(s4s2) - 1)] if tc >= 2 else []
                            p.dma("sync",
                                  (lambda tc=tc, pb=pb:
                                   lambda e: e.dma_start(
                                       h1r[pb][:, :],
                                       h1scr[tc * 128:(tc + 1) * 128, :]))(),
                                  waits=wt_x, inc=s4x[pb])
                            p.add("gpsimd",
                                  (lambda tc=tc, pb=pb, sl=sl:
                                   lambda e: e.tensor_add(
                                       s2b[pb][:, :],
                                       acc[:, sl * D:(sl + 1) * D],
                                       h1r[pb][:, :]))(),
                                  waits=[(s4x[pb], p.cur(s4x[pb])),
                                         (s4c, experts_done)], inc=s4s2)
                            wt_sq = [(s4s2, p.cur(s4s2))]
                            if p.cur(s4od[pb]) > 0:
                                wt_sq.append((s4od[pb], p.cur(s4od[pb])))
                            p.add("scalar",
                                  (lambda pb=pb: lambda e: e.activation(
                                      ob[pb][:, :], s2b[pb][:, :],
                                      AF.Square,
                                      accum_out=st4[:, 1:2]))(),
                                  waits=wt_sq, inc=s4sq)

                            def chain4(fn, waits=()):
                                p.add("vector", fn,
                                      waits=list(waits)
                                      + [(s4ch, p.cur(s4ch))], inc=s4ch)
                            chain4((lambda pb=pb: lambda e: e.reduce_sum(
                                st4[:, 0:1], s2b[pb][:, :], axis=AX.X))(),
                                waits=[(s4ri, p.cur(s4ri)),
                                       (s4sq, p.cur(s4sq)),
                                       (s4s2, p.cur(s4s2))])
                            chain4(lambda e: e.tensor_scalar(
                                st4[:, 2:3], st4[:, 0:1], c1024[:, :], None,
                                op0=ALU.mult))
                            chain4(lambda e: e.tensor_scalar(
                                st4[:, 3:4], st4[:, 1:2], c1024[:, :], None,
                                op0=ALU.mult))
                            chain4(lambda e: e.tensor_mul(
                                st4[:, 4:5], st4[:, 2:3], st4[:, 2:3]))
                            chain4(lambda e: e.tensor_sub(
                                st4[:, 5:6], st4[:, 3:4], st4[:, 4:5]))
                            chain4(lambda e: e.tensor_add(
                                st4[:, 6:7], st4[:, 5:6], ceps[:, :]))
                            p.add("vector",
                                  lambda e: e.reciprocal(st4[:, 7:8],
                                                         st4[:, 6:7]),
                                  waits=[(s4ch, p.cur(s4ch))], inc=s4st)
                            p.add("scalar",
                                  (lambda: lambda e: e.activation(
                                      st4[:, 6:7], st4[:, 7:8], AF.Sqrt))(),
                                  waits=[(s4st, p.cur(s4st))], inc=s4ri)
                            p.add("vector",
                                  (lambda pb=pb: lambda e: e.tensor_scalar(
                                      ob[pb][:, :], s2b[pb][:, :],
                                      st4[:, 2:3], st4[:, 6:7],
                                      op0=ALU.subtract, op1=ALU.mult))(),
                                  waits=[(s4ri, p.cur(s4ri))], inc=s4o)
                            p.dma("sync",
                                  (lambda tc=tc, pb=pb:
                                   lambda e: e.dma_start(
                                       out_d[tc * 128:(tc + 1) * 128, :],
                                       ob[pb][:, :]))(),
                                  waits=[(s4o, p.cur(s4o))], inc=s4od[pb])

                    final_od = (p.cur(s4od[0]), p.cur(s4od[1]))
                    with nc.Block() as blk:
                        p.emit(blk)
                    with nc.Block() as blk:
                        @blk.sync
                        def _(e):
                            e.wait_ge(s4od[0], final_od[0])
                            e.wait_ge(s4od[1], final_od[1])

    return nc


def kernel(**inputs):
    xs = np.ascontiguousarray(np.asarray(inputs["L0_attn_x"],
                                         dtype=np.float32))
    ws = {k: np.ascontiguousarray(np.asarray(v, dtype=np.float32))
          for (k, v) in inputs.items() if k != "L0_attn_x"}
    ident = np.eye(128, dtype=np.float32)
    in_maps = []
    for core in range(8):
        b, half = core // 2, core % 2
        in_maps.append(dict(
            x_kv=xs[b],
            x_q=np.ascontiguousarray(xs[b, half * SQ:(half + 1) * SQ]),
            wq=ws["L0_attn_Wq"], wk=ws["L0_attn_Wk"], wv=ws["L0_attn_Wv"],
            wr=ws["L0_attn_Wr"], wo=ws["L0_attn_Wo"],
            gw=ws["L0_ffn_gate_w"], gb=ws["L0_ffn_gate_b"],
            ew=ws["L0_ffn_expert_w"], eb=ws["L0_ffn_expert_b"],
            ident=ident,
        ))
    nc = build(dev=0)
    res = run_bass_kernel_spmd(nc, in_maps, core_ids=list(range(8)))
    out = np.empty((B, S, D), dtype=np.float32)
    for core in range(8):
        b, half = core // 2, core % 2
        out[b, half * SQ:(half + 1) * SQ] = res.results[core]["out"]
    return out
